# version 10
# baseline (speedup 1.0000x reference)
import math

import numpy as np
import ml_dtypes

import concourse.bass as bass
import concourse.mybir as mybir
from concourse.bass_utils import run_bass_kernel_spmd
from concourse.alu_op_type import AluOpType as ALU

F32 = mybir.dt.float32
BF16 = mybir.dt.bfloat16
AF = mybir.ActivationFunctionType
AX = mybir.AxisListType.X

C = 2048
S = 8192
NH = 16
HD = 128
ROT = 32
T_OWN = 2048
NT = 16
H_FF = 1024
NE = 8
HS = 2048
SCALE = 1.0 / math.sqrt(HD)
NCORE = 8




class _FakeIns:
    __slots__ = ("incs",)

    def __init__(self):
        self.incs = []

    def then_inc(self, sem, val=None):
        self.incs.append((sem, val))
        return self


class _Rec:

    def __init__(self, ops):
        self._ops = ops

    def __getattr__(self, m):
        ops = self._ops

        def call(*a, **kw):
            fi = _FakeIns()
            ops.append((m, a, kw, fi))
            return fi
        return call


class DeferredBlock:
    ENGINES = ("sync", "tensor", "vector", "scalar", "gpsimd")

    def __init__(self, nc):
        self.nc = nc
        self.ops = {e: [] for e in self.ENGINES}

    def _mk(self, engname):
        rec = _Rec(self.ops[engname])

        def deco(fn):
            fn(rec)
        return deco

    def __getattr__(self, engname):
        if engname in self.ENGINES:
            return self._mk(engname)
        raise AttributeError(engname)

    def flush(self):
        nc = self.nc
        with nc.Block() as blk:
            for engname in self.ENGINES:
                ops = self.ops[engname]
                if not ops:
                    continue

                def emit(eng, ops=ops):
                    for m, a, kw, fi in ops:
                        ins = getattr(eng, m)(*a, **kw)
                        for sem, val in fi.incs:
                            if val is None:
                                ins.then_inc(sem)
                            else:
                                ins.then_inc(sem, val)
                getattr(blk, engname)(emit)
        for e in self.ENGINES:
            self.ops[e] = []


class Sem:
    def __init__(self, nc, name):
        self.h = nc.semaphore(name).__enter__()
        self.val = 0

    def inc(self, ins, by=1):
        ins.then_inc(self.h, by)
        self.val += by
        return self.val


class DmaStream:

    def __init__(self, nc, name, k=2):
        self.sems = [Sem(nc, f"{name}{j}") for j in range(k)]
        self.k = k
        self.n = 0
        self.marks = []

    def issue(self, ins, key=None):
        kj = (self.n if key is None else key) % self.k
        s = self.sems[kj]
        s.inc(ins, 16)
        self.marks.append((kj, s.val))
        self.n += 1
        return self.n - 1

    def wait(self, eng, i):
        j, v = self.marks[i]
        eng.wait_ge(self.sems[j].h, v)

    def wait_upto(self, eng, i):
        last = {}
        for j, v in self.marks[:i + 1]:
            last[j] = v
        for j, v in last.items():
            eng.wait_ge(self.sems[j].h, v)

    def wait_all(self, eng):
        for s in self.sems:
            if s.val:
                eng.wait_ge(s.h, s.val)

    def guard(self, eng, key):
        s = self.sems[key % self.k]
        if s.val:
            eng.wait_ge(s.h, s.val)


def build_nc(debug=False, stage=3):
    nc = bass.Bass("TRN2", target_bir_lowering=False)
    _lm = [None]

    def MM(t, *a, **kw):
        _lm[0] = t.matmul(*a, **kw)
        return _lm[0]

    def TR(t, *a, **kw):
        _lm[0] = t.transpose(*a, **kw)
        return _lm[0]

    def din(name, shape, dt=F32):
        return nc.dram_tensor(name, shape, dt, kind="ExternalInput").ap()

    x_own = din("x_own", [T_OWN, C])
    wqkv = din("wqkv", [C, 3 * C])
    bqkv_col = din("bqkv_col", [128, 48])
    wdense = din("wdense", [C, C])
    bd_row = din("bd_row", [1, C])
    gwT = din("gwT", [C, NE])
    wgT = din("wgT", [NE * C, H_FF])
    wuT = din("wuT", [NE * C, H_FF])
    wdT = din("wdT", [NE * H_FF, C])
    shgT = din("shgT", [C, HS])
    shuT = din("shuT", [C, HS])
    shdT = din("shdT", [HS, C])
    seT_col = din("seT_col", [128, 16])
    cos_own = din("cos_own", [ROT, T_OWN], BF16)
    sin_own = din("sin_own", [ROT, T_OWN], BF16)
    maskp = din("maskp", [8, 128 * 1024], BF16)
    ident128 = din("ident128", [128, 128], BF16)
    identf32 = din("identf32", [128, 128], F32)
    ones1 = din("ones1", [1, 128], BF16)
    ones128 = din("ones128", [128, 128], BF16)
    onehot8 = din("onehot8", [8, 8 * 128], BF16)
    rpermT = din("rpermT", [ROT, ROT], BF16)
    cm8 = din("cm8", [128, 1])

    out_own = nc.dram_tensor("out_own", [T_OWN, C], F32, kind="ExternalOutput").ap()

    def dscr(name, shape, dt):
        return nc.dram_tensor(name, shape, dt).ap()

    wqkv_bf = dscr("wqkv_bf", [C, 3 * C], BF16)
    wdense_bf = dscr("wdense_bf", [C, C], BF16)
    ln1_tm = dscr("ln1_tm", [T_OWN, C], BF16)
    cc_ink = dscr("cc_ink", [C, T_OWN], BF16)
    cc_invT = dscr("cc_invT", [T_OWN, C], BF16)
    q_scr = dscr("q_scr", [C, T_OWN], BF16)
    gk = [dscr(f"gk{g}", [4 * 512, T_OWN], BF16) for g in range(4)]
    gvT = [dscr(f"gvT{t}", [4 * 512, C], BF16) for t in range(4)]
    o_scr = dscr("o_scr", [C, T_OWN], BF16)
    x2_scr = dscr("x2_scr", [T_OWN, C], F32)
    ln2_tm = dscr("ln2_tm", [T_OWN, C], BF16)
    wg_scr = dscr("wg_scr", [NE * C, H_FF], BF16)
    wu_scr = dscr("wu_scr", [NE * C, H_FF], BF16)
    wd_scr = dscr("wd_scr", [NE * H_FF, C], BF16)
    shg_scr = dscr("shg_scr", [C, HS], BF16)
    shu_scr = dscr("shu_scr", [C, HS], BF16)
    shd_scr = dscr("shd_scr", [HS, C], BF16)

    dbg = {}
    if debug:
        for nm, shp in [("d_ln1", [T_OWN, C]), ("d_k", [C, T_OWN]), ("d_v", [C, T_OWN]),
                        ("d_q", [C, T_OWN]), ("d_o", [C, T_OWN]), ("d_x2", [T_OWN, C])]:
            dbg[nm] = nc.dram_tensor(nm, shp, F32, kind="ExternalOutput").ap()

    live = {}
    pslive = {}

    def alloc(name, shape, dt):
        t = nc.sbuf_tensor(name, shape, dt)
        h = t.__enter__()
        live[name] = t
        return h

    def free(*names):
        order = list(live)
        for n in sorted(names, key=order.index, reverse=True):
            live.pop(n).__exit__(None, None, None)

    def psalloc(name, shape):
        t = nc.psum_tensor(name, shape, F32)
        h = t.__enter__()
        pslive[name] = t
        return h

    def psfree(*names):
        order = list(pslive)
        for n in sorted(names, key=order.index, reverse=True):
            pslive.pop(n).__exit__(None, None, None)

    ident_sb = alloc("ident_sb", [128, 128], BF16)
    identf_sb = alloc("identf_sb", [128, 128], F32)
    ones1_sb = alloc("ones1_sb", [1, 128], BF16)
    ones128_sb = alloc("ones128_sb", [128, 128], BF16)
    onehot_sb = alloc("onehot_sb", [8, 8 * 128], BF16)
    rperm_sb = alloc("rperm_sb", [ROT, ROT], BF16)
    bqkv_sb = alloc("bqkv_sb", [128, 48], F32)
    cm8_sb = alloc("cm8_sb", [128, 1], F32)
    gwT_sb = alloc("gwT_sb", [128, 16 * NE], BF16)
    seT_sb = alloc("seT_sb", [128, 16], BF16)
    bd_sb = alloc("bd_sb", [1, C], BF16)

    sd_init = Sem(nc, "init")
    sd_initg = Sem(nc, "initg")
    sd_init2 = Sem(nc, "init2")
    sd_pcq = Sem(nc, "pcq")
    sd_pcd = Sem(nc, "pcd")
    sd_pc = Sem(nc, "pcw")
    sd_x = DmaStream(nc, "sdx", 2)
    sd_ln1w = DmaStream(nc, "ln1w", 2)
    sd_ln1r = DmaStream(nc, "ln1r", 2)
    sd_w = DmaStream(nc, "sdw", 2)
    sd_kv = DmaStream(nc, "kvw", 2)
    sd_vw = DmaStream(nc, "vww", 2)
    cck = [Sem(nc, f"cck{g}") for g in range(4)]
    ccv = [Sem(nc, f"ccv{g}") for g in range(4)]
    sv = Sem(nc, "sv")
    ss = Sem(nc, "ss")
    st = Sem(nc, "st")
    sg = Sem(nc, "sg")
    sd_kh = DmaStream(nc, "kh", 4)
    sd_vh = DmaStream(nc, "vh", 3)
    sd_o = DmaStream(nc, "ow", 2)
    sd_d = DmaStream(nc, "dd", 2)
    sd_x2 = DmaStream(nc, "x2w", 2)
    sd_lnw = DmaStream(nc, "lnw", 2)
    sd_fw = DmaStream(nc, "ffw", 8)
    sd_fl = DmaStream(nc, "ffl", 1)
    sd_fx = DmaStream(nc, "ffx", 1)
    sd_out = Sem(nc, "outw")
    sd_dbg = Sem(nc, "dbgw")

    blk = DeferredBlock(nc)

    cosb = alloc("cosb", [ROT, T_OWN], BF16)
    sinb = alloc("sinb", [ROT, T_OWN], BF16)
    ln1_fm = alloc("ln1_fm", [128, 16 * T_OWN], BF16)
    xt = [alloc(f"xt{j}", [128, C], F32) for j in range(2)]
    lnt = [alloc(f"lnt{j}", [128, C], BF16) for j in range(2)]
    stats = alloc("stats", [128, 48], F32)
    mv = alloc("mv", [128, 32], F32)
    negmean = alloc("negmean", [128, 16], F32)
    lnvar = alloc("lnvar", [128, 16], F32)
    rstd = alloc("rstd", [128, 16], F32)

    @blk.sync
    def _(sync):
        for dst, src in [(ident_sb, ident128), (identf_sb, identf32), (ones1_sb, ones1),
                         (ones128_sb, ones128), (onehot_sb, onehot8), (rperm_sb, rpermT),
                         (bqkv_sb, bqkv_col), (cm8_sb, cm8), (cosb, cos_own), (sinb, sin_own)]:
            sd_init.inc(sync.dma_start(out=dst[:], in_=src[:, :]), 16)
        sd_x.issue(sync.dma_start(out=xt[0][:], in_=x_own[0:128, :]))
        sd_x.issue(sync.dma_start(out=xt[1][:], in_=x_own[128:256, :]))

    @blk.gpsimd
    def _(g):
        sd_initg.inc(g.dma_start(out=gwT_sb[:].rearrange("p (cc e) -> p cc e", e=NE),
                                 in_=gwT[:, :].rearrange("(cc p) e -> p cc e", p=128)), 16)
        sd_initg.inc(g.dma_start(out=seT_sb[:], in_=seT_col[:, :]), 16)
        sd_initg.inc(g.dma_start(out=bd_sb[:], in_=bd_row[:, :]), 16)
        sd_pcq.inc(g.dma_start(out=wqkv_bf[:, :], in_=wqkv[:, :]), 16)
    pc_all = None

    stat_ms = [0] * NT
    for i in range(NT):
        @blk.vector
        def _(v, i=i):
            sd_x.wait(v, i)
            if i >= 2:
                v.wait_ge(sv.h, stat_ms[i - 2])
            sh = (i % 2) * 24
            for q in range(4):
                ins = v.bn_stats(stats[:, sh + q * 6:sh + (q + 1) * 6], xt[i % 2][:, q * 512:(q + 1) * 512])
            sv.inc(ins)
            v.wait_ge(sv.h, sv.val)
            sv.inc(v.bn_aggr(mv[:, 2 * i:2 * i + 2], stats[:, sh:sh + 24]))
        stat_ms[i] = sv.val

        @blk.sync
        def _(sync, i=i):
            if i + 2 < NT:
                sync.wait_ge(sv.h, stat_ms[i])
                sd_x.issue(sync.dma_start(out=xt[i % 2][:], in_=x_own[(i + 2) * 128:(i + 3) * 128, :]))

    @blk.vector
    def _(v):
        v.wait_ge(sv.h, stat_ms[NT - 1])
        mview = mv[:].rearrange("p (i two) -> p i two", two=2)
        v.tensor_scalar(negmean[:], mview[:, :, 0], -1.0, None, ALU.mult)
        sv.inc(v.tensor_scalar(lnvar[:], mview[:, :, 1], 1e-5, None, ALU.add))
    ms_lnstats = sv.val

    @blk.scalar
    def _(s):
        s.wait_ge(sv.h, ms_lnstats)
        ss.inc(s.activation(rstd[:], lnvar[:], AF.Ln))
    ms_ln = ss.val

    @blk.vector
    def _(v):
        v.wait_ge(ss.h, ms_ln)
        sv.inc(v.tensor_scalar(rstd[:], rstd[:], -0.5, None, ALU.mult))
    ms_half = sv.val

    @blk.scalar
    def _(s):
        s.wait_ge(sv.h, ms_half)
        ss.inc(s.activation(rstd[:], rstd[:], AF.Exp))
    ms_rstd = ss.val

    norm_ms = [0] * NT
    x2nd = [0] * NT
    for i in range(NT):
        @blk.sync
        def _(sync, i=i):
            if i < 2:
                sync.wait_ge(sv.h, stat_ms[NT - 1])
            else:
                sync.wait_ge(sv.h, norm_ms[i - 2])
            x2nd[i] = sd_x.issue(sync.dma_start(out=xt[i % 2][:], in_=x_own[i * 128:(i + 1) * 128, :]))

        @blk.vector
        def _(v, i=i):
            sd_x.wait(v, x2nd[i])
            if i == 0:
                v.wait_ge(ss.h, ms_rstd)
            if i >= 2:
                sd_ln1w.wait(v, i - 2)
            sv.inc(v.tensor_scalar(lnt[i % 2][:], xt[i % 2][:], negmean[:, i:i + 1], rstd[:, i:i + 1], ALU.add, ALU.mult))
        norm_ms[i] = sv.val

        @blk.sync
        def _(sync, i=i):
            sync.wait_ge(sv.h, norm_ms[i])
            sd_ln1w.issue(sync.dma_start(out=ln1_tm[i * 128:(i + 1) * 128, :], in_=lnt[i % 2][:]), key=i)

    @blk.sync
    def _(sync):
        sd_ln1w.wait_all(sync)
        for cc in range(16):
            sd_ln1r.issue(sync.dma_start(out=ln1_fm[:, cc * T_OWN:(cc + 1) * T_OWN],
                                         in_=ln1_tm[:, cc * 128:(cc + 1) * 128], transpose=True))

    free("xt0", "xt1", "lnt0", "lnt1", "stats", "mv", "negmean", "lnvar", "rstd")

    wsl = [alloc(f"wsl{j}", [128, 16 * 384], BF16) for j in range(2)]
    k_sb = [alloc(f"ksb{j}", [128, C], BF16) for j in range(2)]
    q_sb = [alloc(f"qsb{j}", [128, C], BF16) for j in range(2)]
    v_sbf = alloc("v_sbf", [128, C], F32)
    v_sbT = [alloc(f"vsbT{j}", [128, C], BF16) for j in range(2)]
    rt1 = alloc("rt1", [ROT, C], BF16)
    rt2 = alloc("rt2", [ROT, C], BF16)

    psA = psalloc("psA", [128, C])
    psB = psalloc("psB", [128, C])

    n_heads_emit = {10: 0, 11: 1, 12: 2, 13: 1, 14: 1, 15: 1, 16: 1}.get(stage, NH)
    head_parts = {13: ("q",), 14: ("q", "k"), 15: ("q", "k", "rope"), 16: ("q",)}.get(stage, ("q", "k", "rope", "v", "dma"))
    q_mm_single = (stage == 16)

    wslv = [alloc(f"wslv{j}", [128, 16 * 128], BF16) for j in range(2)]
    vq_mm = {}
    vq_ev = {}
    vq_tr = {}
    vq_te = {}
    vq_dma = {}
    if "v" in head_parts:
        @blk.sync
        def _(sync):
            sync.wait_ge(sd_pcq.h, 16)
            for h in (0, 1):
                sd_w.issue(sync.dma_start(
                    out=wslv[h % 2][:].rearrange("p (cc d) -> p cc d", d=128),
                    in_=wqkv_bf[:, h * 384 + 256:h * 384 + 384].rearrange("(cc p) d -> p cc d", p=128)), key=h)

        for tq in range(4):
            for h in range(NH):
                jv = h % 2
                pv_, po_ = (psA, psB) if h % 2 == 0 else (psB, psA)

                @blk.tensor
                def _(t, tq=tq, h=h, jv=jv, pv_=pv_):
                    if tq == 0 and h == 0:
                        t.wait_ge(sd_init.h, sd_init.val)
                        t.wait_ge(sd_initg.h, sd_initg.val)
                        sd_ln1r.wait_all(t)
                    sd_w.wait(t, 16 * tq + h)
                    k2 = 16 * tq + h
                    if k2 >= 2:
                        pe = vq_te.get(k2 - 2)
                        if pe:
                            t.wait_ge(ss.h, pe)
                    for cc in range(16):
                        MM(t, pv_[:, 0:512], wslv[jv][:, cc * 128:(cc + 1) * 128],
                           ln1_fm[:, cc * T_OWN + tq * 512:cc * T_OWN + tq * 512 + 512],
                           start=(cc == 0), stop=(cc == 15))
                    vq_mm[16 * tq + h] = st.inc(_lm[0])

                @blk.scalar
                def _(s, tq=tq, h=h, jv=jv, pv_=pv_):
                    k2 = 16 * tq + h
                    s.wait_ge(st.h, vq_mm[k2])
                    if k2 >= 2:
                        pe = vq_tr.get(k2 - 2)
                        if pe:
                            s.wait_ge(st.h, pe)
                    ss.inc(s.activation(v_sbf[:, jv * 512:(jv + 1) * 512], pv_[:, 0:512], AF.Identity,
                                        bias=bqkv_sb[:, 3 * h + 2:3 * h + 3]))
                    vq_ev[k2] = ss.val

                @blk.tensor
                def _(t, tq=tq, h=h, jv=jv, pv_=pv_):
                    k2 = 16 * tq + h
                    t.wait_ge(ss.h, vq_ev[k2])
                    if k2 >= 2:
                        pe = vq_te.get(k2 - 2)
                        if pe:
                            t.wait_ge(ss.h, pe)
                    for tl in range(4):
                        TR(t, pv_[:, 512 + tl * 128:512 + (tl + 1) * 128],
                           v_sbf[:, jv * 512 + tl * 128:jv * 512 + (tl + 1) * 128], identf_sb[:])
                    vq_tr[k2] = st.inc(_lm[0])

                @blk.scalar
                def _(s, tq=tq, h=h, jv=jv, pv_=pv_):
                    k2 = 16 * tq + h
                    s.wait_ge(st.h, vq_tr[k2])
                    if k2 >= 2:
                        pd = vq_dma.get(k2 - 2)
                        if pd is not None:
                            sd_vw.wait(s, pd)
                    ss.inc(s.activation(v_sbT[jv][:, 0:512], pv_[:, 512:1024], AF.Identity))
                    vq_te[k2] = ss.val

                @blk.sync
                def _(sync, tq=tq, h=h, jv=jv):
                    k2 = 16 * tq + h
                    sync.wait_ge(ss.h, vq_te[k2])
                    sd_vw.guard(sync, k2)
                    vq_dma[k2] = sd_vw.issue(sync.dma_start(
                        out=cc_invT[tq * 512:(tq + 1) * 512, h * 128:(h + 1) * 128]
                            .rearrange("(tl p) d -> p tl d", p=128),
                        in_=v_sbT[jv][:, 0:512].rearrange("p (tl d) -> p tl d", d=128)), key=k2)
                    k3 = k2 + 2
                    if k3 < 64:
                        h3 = k3 % 16
                        sync.wait_ge(st.h, vq_mm[k2])
                        sd_w.issue(sync.dma_start(
                            out=wslv[k3 % 2][:].rearrange("p (cc d) -> p cc d", d=128),
                            in_=wqkv_bf[:, h3 * 384 + 256:h3 * 384 + 384].rearrange("(cc p) d -> p cc d", p=128)), key=k3)

            if stage not in (-1, 10, 11, 12):
                @blk.gpsimd
                def _(g, tq=tq):
                    sd_vw.wait_upto(g, vq_dma[16 * tq + 15])
                    g.collective_compute(
                        "AllGather", ALU.bypass,
                        replica_groups=[[0, 1, 2, 3], [4, 5, 6, 7]],
                        ins=[cc_invT[tq * 512:(tq + 1) * 512, :]],
                        outs=[gvT[tq][:, :]],
                    ).then_inc(ccv[tq].h)
    n_w_loads = sd_w.n


    q_ready = [0] * NH
    q_evac = [0] * NH
    q_dma = [0] * NH
    vT_ready = [0] * NH
    vT_evac = [0] * NH
    vT_dma = [0] * NH
    k_ready = [0] * NH
    k_evac = [0] * NH
    rq_ready = [0] * NH
    rk_ready = [0] * NH
    qrot = [0] * NH
    krot = [0] * NH
    v_ready = [0] * NH
    v_evac = [0] * NH
    kv_after = [0] * NH

    @blk.sync
    def _(sync):
        sync.wait_ge(sd_pcq.h, 16)
        for h in (0, 1):
            sd_w.issue(sync.dma_start(
                out=wsl[h % 2][:].rearrange("p (cc d) -> p cc d", d=384),
                in_=wqkv_bf[:, h * 384:(h + 1) * 384].rearrange("(cc p) d -> p cc d", p=128)), key=64 + h)

    def wslice(j, cc, which):
        return wsl[j][:, cc * 384 + which * 128: cc * 384 + (which + 1) * 128]

    for h in range(n_heads_emit):
        j = h % 2
        psQ, psK = (psA, psB) if h % 2 == 0 else (psB, psA)

        @blk.tensor
        def _(t, h=h, j=j, psQ=psQ, psK=psK):
            if h == 0:
                t.wait_ge(sd_init.h, sd_init.val)
                t.wait_ge(sd_initg.h, sd_initg.val)
                sd_ln1r.wait_all(t)
                if "v" in head_parts:
                    t.wait_ge(ss.h, vq_te[63])
            sd_w.wait(t, n_w_loads + h)
            if h >= 1:
                t.wait_ge(sv.h, krot[h - 1])
            for n4 in range(4):
                for cc in range(16):
                    MM(t, psQ[:, n4 * 512:(n4 + 1) * 512], wslice(j, cc, 0),
                             ln1_fm[:, cc * T_OWN + n4 * 512:cc * T_OWN + (n4 + 1) * 512],
                             start=(cc == 0), stop=(cc == 15))
            q_ready[h] = st.inc(_lm[0])
            if "k" not in head_parts:
                return
            if h >= 1:
                t.wait_ge(sv.h, qrot[h - 1])
            for n4 in range(4):
                for cc in range(16):
                    MM(t, psK[:, n4 * 512:(n4 + 1) * 512], wslice(j, cc, 1),
                             ln1_fm[:, cc * T_OWN + n4 * 512:cc * T_OWN + (n4 + 1) * 512],
                             start=(cc == 0), stop=(cc == 15))
            k_ready[h] = st.inc(_lm[0])

        @blk.vector
        def _(v, h=h, j=j, psQ=psQ):
            v.wait_ge(st.h, q_ready[h])
            if h >= 2:
                sd_kv.wait(v, kv_after[h - 2])
            sv.inc(v.tensor_scalar(q_sb[j][:], psQ[:], bqkv_sb[:, 3 * h:3 * h + 1], SCALE, ALU.add, ALU.mult))
            q_evac[h] = sv.val

        @blk.scalar
        def _(s, h=h, j=j, psK=psK):
            if "k" not in head_parts:
                return
            s.wait_ge(st.h, k_ready[h])
            if h >= 2:
                sd_kv.wait(s, kv_after[h - 2])
            ss.inc(s.activation(k_sb[j][:], psK[:], AF.Identity, bias=bqkv_sb[:, 3 * h + 1:3 * h + 2]))
            k_evac[h] = ss.val

        @blk.tensor
        def _(t, h=h, j=j, psQ=psQ, psK=psK):
            if "rope" not in head_parts:
                return
            t.wait_ge(sv.h, q_evac[h])
            for n4 in range(4):
                MM(t, psQ[0:ROT, n4 * 512:(n4 + 1) * 512], rperm_sb[:],
                         q_sb[j][0:ROT, n4 * 512:(n4 + 1) * 512], start=True, stop=True)
            rq_ready[h] = st.inc(_lm[0])
            t.wait_ge(ss.h, k_evac[h])
            for n4 in range(4):
                MM(t, psK[0:ROT, n4 * 512:(n4 + 1) * 512], rperm_sb[:],
                         k_sb[j][0:ROT, n4 * 512:(n4 + 1) * 512], start=True, stop=True)
            rk_ready[h] = st.inc(_lm[0])

        @blk.vector
        def _(v, h=h, j=j, psQ=psQ, psK=psK):
            if "rope" not in head_parts:
                return
            v.wait_ge(st.h, rq_ready[h])
            if h >= 1:
                v.wait_ge(sv.h, krot[h - 1])
            v.tensor_tensor(rt1[:], q_sb[j][0:ROT, :], cosb[:], ALU.mult)
            sv.inc(v.tensor_tensor(rt2[:], psQ[0:ROT, :], sinb[:], ALU.mult))
            v.wait_ge(sv.h, sv.val)
            sv.inc(v.tensor_tensor(q_sb[j][0:ROT, :], rt1[:], rt2[:], ALU.add))
            qrot[h] = sv.val
            v.wait_ge(st.h, rk_ready[h])
            v.wait_ge(sv.h, qrot[h])
            v.tensor_tensor(rt1[:], k_sb[j][0:ROT, :], cosb[:], ALU.mult)
            sv.inc(v.tensor_tensor(rt2[:], psK[0:ROT, :], sinb[:], ALU.mult))
            v.wait_ge(sv.h, sv.val)
            sv.inc(v.tensor_tensor(k_sb[j][0:ROT, :], rt1[:], rt2[:], ALU.add))
            krot[h] = sv.val


        @blk.sync
        def _(sync, h=h, j=j):
            if "dma" not in head_parts:
                return
            sync.wait_ge(sv.h, krot[h])
            sd_kv.guard(sync, h)
            sd_kv.issue(sync.dma_start(out=cc_ink[h * 128:(h + 1) * 128, :], in_=k_sb[j][:]), key=h)
            sync.wait_ge(sv.h, qrot[h])
            q_dma[h] = sd_kv.issue(sync.dma_start(out=q_scr[h * 128:(h + 1) * 128, :], in_=q_sb[j][:]), key=h)
            kv_after[h] = sd_kv.n - 1
            if h + 2 < NH:
                sync.wait_ge(st.h, k_ready[h] if "k" in head_parts else q_ready[h])
                sd_w.issue(sync.dma_start(
                    out=wsl[j][:].rearrange("p (cc d) -> p cc d", d=384),
                    in_=wqkv_bf[:, (h + 2) * 384:(h + 3) * 384].rearrange("(cc p) d -> p cc d", p=128)), key=64 + h)

        if h % 4 == 3 and stage not in (-1, 10, 11, 12):
            g4 = h // 4

            @blk.gpsimd
            def _(g, g4=g4, h=h):
                sd_kv.wait_upto(g, sd_kv.n - 1)
                g.collective_compute(
                    "AllGather", ALU.bypass,
                    replica_groups=[[0, 1, 2, 3], [4, 5, 6, 7]],
                    ins=[cc_ink[g4 * 512:(g4 + 1) * 512, :]],
                    outs=[gk[g4][:, :]],
                ).then_inc(cck[g4].h)

    p0_end_t = st.val
    p0_end_s = ss.val
    p0_end_v = sv.val
    @blk.gpsimd
    def _(g):
        if stage not in (-1, 10, 11, 12):
            g.wait_ge(cck[3].h, 1)
        sd_pcd.inc(g.dma_start(out=wdense_bf[:, :], in_=wdense[:, :]), 16)
        for e in range(NE):
            sd_pc.inc(g.dma_start(out=wg_scr[e * C:(e + 1) * C, :], in_=wgT[e * C:(e + 1) * C, :]), 16)
            sd_pc.inc(g.dma_start(out=wu_scr[e * C:(e + 1) * C, :], in_=wuT[e * C:(e + 1) * C, :]), 16)
            sd_pc.inc(g.dma_start(out=wd_scr[e * H_FF:(e + 1) * H_FF, :], in_=wdT[e * H_FF:(e + 1) * H_FF, :]), 16)
        sd_pc.inc(g.dma_start(out=shg_scr[:, :], in_=shgT[:, :]), 16)
        sd_pc.inc(g.dma_start(out=shu_scr[:, :], in_=shuT[:, :]), 16)
        sd_pc.inc(g.dma_start(out=shd_scr[:, :], in_=shdT[:, :]), 16)
    pc_all = sd_pc.val


    if debug and stage not in (13, 14, 15, 16):
        @blk.gpsimd
        def _(g):
            sd_ln1w.wait_all(g)
            sd_dbg.inc(g.dma_start(out=dbg["d_ln1"][:, :], in_=ln1_tm[:, :]), 16)
            sd_kv.wait_all(g)
            sd_vw.wait_all(g)
            sd_dbg.inc(g.dma_start(out=dbg["d_k"][:, :], in_=cc_ink[:, :]), 16)
            sd_dbg.inc(g.dma_start(out=dbg["d_v"][:, :], in_=cc_invT[:, :]), 16)
            sd_dbg.inc(g.dma_start(out=dbg["d_q"][:, :], in_=q_scr[:, :]), 16)

    free("cosb", "sinb", "ln1_fm", "wsl0", "wsl1", "ksb0", "ksb1", "qsb0", "qsb1",
         "v_sbf", "vsbT0", "vsbT1", "rt1", "rt2", "wslv0", "wslv1")
    psfree("psA", "psB")

    if stage in (0, -1, 10, 11, 12, 13, 14, 15, 16):
        @blk.gpsimd
        def _(g):
            sd_ln1w.wait_all(g)
            sd_kv.wait_all(g)
            if stage in (13, 14, 15, 16):
                g.wait_ge(sv.h, q_evac[0])
            if stage in (14, 15):
                g.wait_ge(ss.h, k_evac[0])
            if stage == 15:
                g.wait_ge(sv.h, krot[0])
            if stage == 10:
                g.wait_ge(sd_w.sems[0].h, sd_w.sems[0].val)
                g.wait_ge(sd_w.sems[1].h, sd_w.sems[1].val)
                for sY in sd_ln1r.sems:
                    g.wait_ge(sY.h, sY.val)
            if stage == 0:
                for g4 in range(4):
                    g.wait_ge(cck[g4].h, 1)
                    g.wait_ge(ccv[g4].h, 1)
            for hh in range(NH):
                g.wait_ge(sv.h, qrot[hh])
            sd_out.inc(g.dma_start(out=out_own[:, :], in_=ln1_tm[:, :]), 16)
            g.wait_ge(sd_out.h, sd_out.val)
            if debug:
                g.wait_ge(sd_dbg.h, sd_dbg.val)
        blk.flush()
        return nc

    masks_sb = alloc("masks_sb", [128, 8 * 1024], BF16)
    qp_buf = [alloc(f"qp{j}", [128, T_OWN], BF16) for j in range(4)]
    k_hbuf = [alloc(f"khb{j}", [128, 4 * T_OWN], BF16) for j in range(4)]
    vt_buf = [alloc(f"vtb{j}", [128, 64 * 128], BF16) for j in range(3)]
    pT = [alloc(f"pT{j}", [128, 1024], BF16) for j in range(4)]
    rD2 = [alloc(f"rD{j}", [128, 512], F32) for j in range(2)]
    o_ev = [alloc(f"oev{j}", [128, 512], BF16) for j in range(2)]

    @blk.sync
    def _(sync):
        sd_init.inc(sync.dma_start(out=masks_sb[:].rearrange("p (m f) -> p m f", f=1024),
                                   in_=maskp[:, :].rearrange("m (p f) -> p m f", f=1024)), 16)
    init_all = sd_init.val

    psS = [psalloc(f"psS{j}", [128, 1024]) for j in range(2)]
    psO2 = [psalloc(f"psO{j}", [128, 512]) for j in range(2)]
    psDa = [psalloc(f"psDa{j}", [128, 512]) for j in range(2)]

    def kcol(kt):
        return (kt % 4) * T_OWN + (kt // 4) * 128

    kq_load = [0] * NH
    vt_load = {}
    av_last = [0] * NH
    s_ready = {}
    exp_ms = {}
    mask_ms = {}
    dadd_ms = {}
    av_done = {}
    o_ev_done = {}
    o_dma_mark = {}
    rd_done = {}
    rd_prev = [0, 0]

    def issue_kq(h):
        j3 = h % 4
        g4 = h // 4

        @blk.sync
        def _(sync, h=h, j3=j3, g4=g4):
            sync.wait_ge(cck[g4].h, 1)
            if h < 4:
                sync.wait_ge(st.h, p0_end_t)
                sync.wait_ge(ss.h, p0_end_s)
                sync.wait_ge(sv.h, p0_end_v)
                sd_kv.wait_all(sync)
                sd_vw.wait_all(sync)
                sd_w.wait_all(sync)
            else:
                sync.wait_ge(st.h, av_last[h - 4])
            sd_kh.guard(sync, h)
            sd_kh.issue(sync.dma_start(out=qp_buf[j3][:], in_=q_scr[h * 128:(h + 1) * 128, :]), key=h)
            hl = h % 4
            for r in range(4):
                sd_kh.issue(sync.dma_start(out=k_hbuf[j3][:, r * T_OWN:(r + 1) * T_OWN],
                                           in_=gk[g4][r * 512 + hl * 128:r * 512 + (hl + 1) * 128, :]), key=h)
            kq_load[h] = sd_kh.n - 1

    def issue_vt(h, w):
        j3 = h % 3

        @blk.sync
        def _(sync, h=h, w=w, j3=j3):
            sync.wait_ge(ccv[w].h, 1)
            if h >= 3:
                sync.wait_ge(st.h, av_last[h - 3])
            sd_vh.guard(sync, h % 3)
            for r in range(4):
                base = (16 * w + r) * 128
                dst = vt_buf[j3][:].rearrange("p (kt d) -> p kt d", d=128)[:, 16 * w + r:16 * w + r + 13:4, :]
                srcv = gvT[w][r * 512:(r + 1) * 512, h * 128:(h + 1) * 128] \
                    .rearrange("(jj p) d -> p jj d", p=128)
                sd_vh.issue(sync.dma_start(out=dst, in_=srcv), key=h % 3)
            vt_load[(h, w)] = sd_vh.n - 1

    issue_kq(0)
    issue_kq(1)
    issue_kq(2)
    issue_kq(3)

    for m in range(NH // 2):
        hA, hB = 2 * m, 2 * m + 1
        jA, jB = hA % 4, hB % 4
        vA, vB = hA % 3, hB % 3

        issue_vt(hA, 0)
        issue_vt(hB, 0)

        @blk.tensor
        def _(t, hA=hA, hB=hB):
            sd_kh.wait(t, kq_load[hA])
            sd_kh.wait(t, kq_load[hB])
            if hA == 0:
                t.wait_ge(sd_init.h, init_all)
                t.wait_ge(ss.h, p0_end_s)
                t.wait_ge(sv.h, p0_end_v)

        for w in range(4):
            npair = 8 * (w + 1)

            def emit_S(t, head, pair, w=w):
                jj = head % 4
                hp = head % 2
                if pair >= 1:
                    pk = exp_ms.get((head, w, pair - 1))
                elif w >= 1:
                    pk = exp_ms.get((head, w - 1, 8 * w - 1))
                else:
                    pk = exp_ms.get((head - 2, 3, 31))
                if pk:
                    t.wait_ge(ss.h, pk)
                qs = w * 512
                for half in range(2):
                    kt = 2 * pair + half
                    MM(t, psS[hp][:, half * 512:(half + 1) * 512],
                       k_hbuf[jj][:, kcol(kt):kcol(kt) + 128],
                       qp_buf[jj][:, qs:qs + 512], start=True, stop=True)
                s_ready[(head, w, pair)] = st.inc(_lm[0])

            @blk.tensor
            def _(t, hA=hA, hB=hB, w=w):
                sd_vh.wait(t, vt_load[(hA, w)])
                sd_vh.wait(t, vt_load[(hB, w)])
                emit_S(t, hA, 0)
                emit_S(t, hB, 0)

            for p in range(npair):
                for head in (hA, hB):
                    hp = head % 2

                    @blk.scalar
                    def _(s, head=head, hp=hp, w=w, p=p):
                        s.wait_ge(st.h, s_ready[(head, w, p)])
                        pprev = []
                        for back in (2,):
                            q_ = p - back
                            if q_ >= 0:
                                pprev.append((head, w, q_))
                            elif w >= 1:
                                pprev.append((head, w - 1, 8 * w + q_))
                            else:
                                pprev.append((head - 2, 3, 32 + q_))
                        for key in pprev:
                            pa = av_done.get(key)
                            if pa:
                                s.wait_ge(st.h, pa)
                        ss.inc(s.activation(pT[2 * hp + p % 2][:], psS[hp][:], AF.Exp))
                        exp_ms[(head, w, p)] = ss.val

                if p + 1 < npair:
                    @blk.tensor
                    def _(t, hA=hA, hB=hB, p=p):
                        emit_S(t, hA, p + 1)
                        emit_S(t, hB, p + 1)

                for head in (hA, hB):
                    hp = head % 2
                    if p >= 8 * w:
                        @blk.vector
                        def _(v, head=head, hp=hp, w=w, p=p):
                            v.wait_ge(ss.h, exp_ms[(head, w, p)])
                            mi = p - 8 * w
                            pb = pT[2 * hp + p % 2]
                            sv.inc(v.tensor_tensor(pb[:], pb[:],
                                                   masks_sb[:, mi * 1024:(mi + 1) * 1024], ALU.mult))
                            mask_ms[(head, w, p)] = sv.val

                for head in (hA, hB):
                    hp = head % 2
                    j3 = head % 3

                    @blk.tensor
                    def _(t, head=head, hp=hp, j3=j3, w=w, p=p, npair=npair):
                        if (head, w, p) in mask_ms:
                            t.wait_ge(sv.h, mask_ms[(head, w, p)])
                        else:
                            t.wait_ge(ss.h, exp_ms[(head, w, p)])
                        if p == 0:
                            pv = o_ev_done.get((head, w - 1), o_ev_done.get((head - 2, 3)))
                            if pv:
                                t.wait_ge(sv.h, pv)
                            if rd_prev[hp]:
                                t.wait_ge(sv.h, rd_prev[hp])
                        for half in range(2):
                            kt = 2 * p + half
                            MM(t, psO2[hp][:], vt_buf[j3][:, kt * 128:(kt + 1) * 128],
                               pT[2 * hp + p % 2][:, half * 512:(half + 1) * 512],
                               start=(p == 0 and half == 0), stop=(p == npair - 1 and half == 1))
                        for half in range(2):
                            MM(t, psDa[hp][:], ones128_sb[:],
                               pT[2 * hp + p % 2][:, half * 512:(half + 1) * 512],
                               start=(p == 0 and half == 0), stop=(p == npair - 1 and half == 1))
                        av_done[(head, w, p)] = st.inc(_lm[0])

            if w + 1 < 4:
                issue_vt(hA, w + 1)
                issue_vt(hB, w + 1)

            for head in (hA, hB):
                hp = head % 2

                @blk.vector
                def _(v, head=head, hp=hp, w=w, npair=npair):
                    v.wait_ge(st.h, av_done[(head, w, npair - 1)])
                    sv.inc(v.reciprocal(rD2[hp][:], psDa[hp][:]))
                    rd_done[(head, w)] = sv.val
                    rd_prev[hp] = sv.val
                    v.wait_ge(sv.h, sv.val)
                    pm = o_dma_mark.get((head, w - 1), o_dma_mark.get((head - 2, 3)))
                    if pm is not None:
                        sd_o.wait(v, pm)
                    sv.inc(v.tensor_tensor(o_ev[hp][:], psO2[hp][:], rD2[hp][:], ALU.mult))
                    o_ev_done[(head, w)] = sv.val

                @blk.sync
                def _(sync, head=head, hp=hp, w=w):
                    sync.wait_ge(sv.h, o_ev_done[(head, w)])
                    sd_o.issue(sync.dma_start(out=o_scr[head * 128:(head + 1) * 128, w * 512:(w + 1) * 512],
                                              in_=o_ev[hp][:]), key=hp)
                    o_dma_mark[(head, w)] = sd_o.n - 1

        av_last[hA] = av_done[(hA, 3, 8 * 4 - 1)]
        av_last[hB] = av_done[(hB, 3, 8 * 4 - 1)]
        if 2 * m + 4 < NH:
            issue_kq(2 * m + 4)
        if 2 * m + 5 < NH:
            issue_kq(2 * m + 5)

    p1_end_t = st.val
    p1_end_v = sv.val
    p1_end_s = ss.val

    if debug:
        @blk.gpsimd
        def _(g):
            sd_o.wait_all(g)
            sd_dbg.inc(g.dma_start(out=dbg["d_o"][:, :], in_=o_scr[:, :]), 16)

    free("masks_sb", "qp0", "qp1", "qp2", "qp3", "khb0", "khb1", "khb2", "khb3",
         "vtb0", "vtb1", "vtb2", "pT0", "pT1", "pT2", "pT3", "rD0", "rD1", "oev0", "oev1")
    psfree("psS0", "psS1", "psO0", "psO1", "psDa0", "psDa1")

    if stage == 1:
        @blk.gpsimd
        def _(g):
            sd_o.wait_all(g)
            g.wait_ge(st.h, p1_end_t)
            sd_out.inc(g.dma_start(out=out_own[:, :], in_=o_scr[:, :]), 16)
            g.wait_ge(sd_out.h, sd_out.val)
            if debug:
                g.wait_ge(sd_dbg.h, sd_dbg.val)
        blk.flush()
        return nc

    o_sb = alloc("o_sb", [128, 16 * T_OWN], BF16)
    wd_sb = alloc("wd_sb", [128, 16 * C], BF16)
    brep = alloc("brep", [128, C], F32)
    xt2 = [alloc(f"xt2{j}", [128, C], F32) for j in range(2)]
    x2t = [alloc(f"x2t{j}", [128, C], F32) for j in range(2)]
    lnt2 = [alloc(f"lnt2{j}", [128, C], BF16) for j in range(2)]
    stats2 = alloc("stats2", [128, 48], F32)
    mv2 = alloc("mv2", [128, 32], F32)
    negmean2 = alloc("negmean2", [128, 16], F32)
    lnvar2 = alloc("lnvar2", [128, 16], F32)
    rstd2 = alloc("rstd2", [128, 16], F32)

    psD2 = [psalloc(f"psD2{j}", [128, C]) for j in range(2)]

    xt2_idx = {}

    @blk.sync
    def _(sync):
        sd_o.wait_all(sync)
        sync.wait_ge(st.h, p1_end_t)
        sync.wait_ge(sv.h, p1_end_v)
        for cc in range(16):
            sd_d.issue(sync.dma_start(out=o_sb[:, cc * T_OWN:(cc + 1) * T_OWN],
                                      in_=o_scr[cc * 128:(cc + 1) * 128, :]))
        sync.wait_ge(sd_pcd.h, 16)
        sd_d.issue(sync.dma_start(out=wd_sb[:].rearrange("p (cc f) -> p cc f", f=C),
                                  in_=wdense_bf[:, :].rearrange("(cc p) f -> p cc f", p=128)))
        xt2_idx[0] = sd_d.issue(sync.dma_start(out=xt2[0][:], in_=x_own[0:128, :]))
        xt2_idx[1] = sd_d.issue(sync.dma_start(out=xt2[1][:], in_=x_own[128:256, :]))
    dense_last_load = sd_d.n - 1

    ms = {}

    @blk.tensor
    def _(t):
        t.wait_ge(sd_init.h, init_all)
        t.wait_ge(sd_initg.h, sd_initg.val)
        t.wait_ge(ss.h, p1_end_s)
        t.wait_ge(sv.h, p1_end_v)
        for n4 in range(4):
            MM(t, psD2[0][:, n4 * 512:(n4 + 1) * 512], ones1_sb[:], bd_sb[:, n4 * 512:(n4 + 1) * 512],
                     start=True, stop=True)
        ms["brep_t"] = st.inc(_lm[0])

    @blk.vector
    def _(v):
        v.wait_ge(st.h, ms["brep_t"])
        sv.inc(v.tensor_copy(brep[:], psD2[0][:]))
        ms["brep_v"] = sv.val

    dmm_ms = [0] * NT
    x2_ms = [0] * NT
    ln2_ms = [0] * NT
    x2_after = [0] * NT
    ln2w = [0] * NT
    for tt in range(NT):
        jj = tt % 2

        @blk.tensor
        def _(t, tt=tt, jj=jj):
            if tt == 0:
                sd_d.wait_upto(t, dense_last_load)
                t.wait_ge(sv.h, ms["brep_v"])
            if tt >= 2:
                t.wait_ge(sv.h, x2_ms[tt - 2])
            for cc in range(16):
                for n4 in range(4):
                    MM(t, psD2[jj][:, n4 * 512:(n4 + 1) * 512],
                             o_sb[:, cc * T_OWN + tt * 128:cc * T_OWN + (tt + 1) * 128],
                             wd_sb[:, cc * C + n4 * 512:cc * C + (n4 + 1) * 512],
                             start=(cc == 0), stop=(cc == 15))
            dmm_ms[tt] = st.inc(_lm[0])

        @blk.vector
        def _(v, tt=tt, jj=jj):
            v.wait_ge(st.h, dmm_ms[tt])
            if tt >= 2:
                sd_x2.wait(v, x2_after[tt - 2])
            sv.inc(v.tensor_tensor(x2t[jj][:], psD2[jj][:], brep[:], ALU.add))
            v.wait_ge(sv.h, sv.val)
            sd_d.wait(v, xt2_idx[tt])
            sv.inc(v.tensor_tensor(x2t[jj][:], x2t[jj][:], xt2[jj][:], ALU.add))
            v.wait_ge(sv.h, sv.val)
            sh = (tt % 2) * 24
            for q in range(4):
                ins = v.bn_stats(stats2[:, sh + q * 6:sh + (q + 1) * 6], x2t[jj][:, q * 512:(q + 1) * 512])
            sv.inc(ins)
            v.wait_ge(sv.h, sv.val)
            sv.inc(v.bn_aggr(mv2[:, 2 * tt:2 * tt + 2], stats2[:, sh:sh + 24]))
            ln2_ms[tt] = sv.val
            x2_ms[tt] = sv.val

        @blk.sync
        def _(sync, tt=tt, jj=jj):
            sync.wait_ge(sv.h, x2_ms[tt])
            x2_after[tt] = sd_x2.issue(sync.dma_start(out=x2_scr[tt * 128:(tt + 1) * 128, :], in_=x2t[jj][:]), key=tt)
            if tt + 2 < NT:
                xt2_idx[tt + 2] = sd_d.issue(sync.dma_start(out=xt2[jj][:], in_=x_own[(tt + 2) * 128:(tt + 3) * 128, :]))

    @blk.vector
    def _(v):
        v.wait_ge(sv.h, x2_ms[NT - 1])
        mview = mv2[:].rearrange("p (i two) -> p i two", two=2)
        v.tensor_scalar(negmean2[:], mview[:, :, 0], -1.0, None, ALU.mult)
        sv.inc(v.tensor_scalar(lnvar2[:], mview[:, :, 1], 1e-5, None, ALU.add))
    ms2_stats = sv.val

    @blk.scalar
    def _(s):
        s.wait_ge(sv.h, ms2_stats)
        ss.inc(s.activation(rstd2[:], lnvar2[:], AF.Ln))
    ms2_ln = ss.val

    @blk.vector
    def _(v):
        v.wait_ge(ss.h, ms2_ln)
        sv.inc(v.tensor_scalar(rstd2[:], rstd2[:], -0.5, None, ALU.mult))
    ms2_half = sv.val

    @blk.scalar
    def _(s):
        s.wait_ge(sv.h, ms2_half)
        ss.inc(s.activation(rstd2[:], rstd2[:], AF.Exp))
    ms2_rstd = ss.val

    ln2n_ms = [0] * NT
    x2rd = [0] * NT
    for tt in range(NT):
        jj = tt % 2

        @blk.sync
        def _(sync, tt=tt, jj=jj):
            if tt < 2:
                sync.wait_ge(sv.h, x2_ms[NT - 1])
            else:
                sync.wait_ge(sv.h, ln2n_ms[tt - 2])
            sd_x2.wait(sync, x2_after[tt])
            x2rd[tt] = sd_d.issue(sync.dma_start(out=xt2[jj][:], in_=x2_scr[tt * 128:(tt + 1) * 128, :]))

        @blk.vector
        def _(v, tt=tt, jj=jj):
            sd_d.wait(v, x2rd[tt])
            if tt == 0:
                v.wait_ge(ss.h, ms2_rstd)
            if tt >= 2:
                sd_lnw.wait(v, ln2w[tt - 2])
            sv.inc(v.tensor_scalar(lnt2[jj][:], xt2[jj][:], negmean2[:, tt:tt + 1], rstd2[:, tt:tt + 1], ALU.add, ALU.mult))
        ln2n_ms[tt] = sv.val

        @blk.sync
        def _(sync, tt=tt, jj=jj):
            sync.wait_ge(sv.h, ln2n_ms[tt])
            ln2w[tt] = sd_lnw.issue(sync.dma_start(out=ln2_tm[tt * 128:(tt + 1) * 128, :], in_=lnt2[jj][:]), key=tt)

    p2_end_t = st.val
    p2_end_v = sv.val
    p2_end_s = ss.val

    if debug:
        @blk.gpsimd
        def _(g):
            sd_x2.wait_all(g)
            sd_lnw.wait_all(g)
            sd_dbg.inc(g.dma_start(out=dbg["d_x2"][:, :], in_=x2_scr[:, :]), 16)

    free("o_sb", "wd_sb", "brep", "xt20", "xt21", "x2t0", "x2t1", "lnt20", "lnt21",
         "stats2", "mv2", "negmean2", "lnvar2", "rstd2")
    psfree("psD20", "psD21")

    if stage == 2:
        @blk.sync
        def _(sync):
            sd_x2.wait_all(sync)
            sd_lnw.wait_all(sync)
            sd_out.inc(sync.dma_start(out=out_own[:, :], in_=x2_scr[:, :]), 16)
            sync.wait_ge(sd_out.h, sd_out.val)
            if debug:
                sync.wait_ge(sd_dbg.h, sd_dbg.val)
        blk.flush()
        return nc

    warena = alloc("warena", [128, 8 * 4096], BF16)
    ln2f = alloc("ln2f", [128, 16 * 512], BF16)
    x13w = alloc("x13w", [128, 4 * 8 * 512], BF16)
    accb = alloc("accb", [128, 4 * C], F32)
    xt3 = alloc("xt3", [128, C], F32)
    s1_sb = [alloc(f"s1sb{j}", [128, 512], BF16) for j in range(2)]
    tmpb = [alloc(f"tmpb{j}", [128, 512], BF16) for j in range(2)]
    wrep = alloc("wrep", [128, 512], BF16)
    sgrep = alloc("sgrep", [128, 512], BF16)
    gsf = alloc("gsf", [8, 512], F32)
    wtm = alloc("wtm", [128, 8], F32)
    wembf = alloc("wembf", [8, 512], BF16)
    m1n = alloc("m1n", [128, 1], F32)
    m1p = alloc("m1p", [128, 1], F32)
    m2p = alloc("m2p", [128, 1], F32)
    eqb = alloc("eqb", [128, 8], F32)
    sm2 = alloc("sm2", [128, 8], F32)
    expsb = alloc("expsb", [128, 8], F32)
    ge2 = alloc("ge2", [128, 8], F32)
    wun = alloc("wun", [128, 8], F32)
    d2b = alloc("d2b", [128, 1], F32)
    rd2b = alloc("rd2b", [128, 1], F32)
    sgf = alloc("sgf", [1, 512], F32)
    sgs = alloc("sgs", [1, 512], F32)
    sgbf = alloc("sgbf", [1, 512], BF16)

    pA = psalloc("pA", [128, C])
    pB = psalloc("pB", [128, C])
    px1 = [pA[:, 0:512], pA[:, 512:1024]]
    px3 = [pA[:, 1024:1536], pA[:, 1536:2048]]
    pso2 = [pA, pB]

    def wslot(q):
        return warena[:, (q % 8) * 4096:(q % 8 + 1) * 4096]

    def gate_src(e, qh):
        if e < NE:
            return wg_scr[e * C:(e + 1) * C, qh * 256:(qh + 1) * 256]
        sp = e - NE
        return shg_scr[:, sp * H_FF + qh * 256:sp * H_FF + (qh + 1) * 256]

    def up_src(e, qh):
        if e < NE:
            return wu_scr[e * C:(e + 1) * C, qh * 256:(qh + 1) * 256]
        sp = e - NE
        return shu_scr[:, sp * H_FF + qh * 256:sp * H_FF + (qh + 1) * 256]

    def down_src(e, half):
        if e < NE:
            return wd_scr[e * H_FF + half * 512:e * H_FF + (half + 1) * 512, :]
        sp = e - NE
        return shd_scr[sp * H_FF + half * 512:sp * H_FF + (half + 1) * 512, :]

    groups = [[0, 1, 2, 3], [4, 5, 6, 7], [8, 9]]

    slot_user = {}
    wrep_user = [0]
    px1_free = {}
    px3_free = {}
    s1_free = {}
    tmp_free = {}
    pso_free = {}
    x13_free_st = [0]
    nq = [0]
    f3 = {}

    for tw in range(4):
        tw0 = tw * 512

        @blk.sync
        def _(sync, tw=tw, tw0=tw0):
            sd_x2.wait_all(sync)
            sd_lnw.wait_all(sync)
            if tw == 0:
                sync.wait_ge(st.h, p2_end_t)
                sync.wait_ge(sd_pc.h, pc_all)
            else:
                sync.wait_ge(st.h, f3[("lnlast", tw - 1)])
            for cc in range(16):
                sd_fl.issue(sync.dma_start(out=ln2f[:, cc * 512:(cc + 1) * 512],
                                           in_=ln2_tm[tw0:tw0 + 512, cc * 128:(cc + 1) * 128], transpose=True))
        f3[("lnload", tw)] = sd_fl.n - 1

        @blk.tensor
        def _(t, tw=tw):
            sd_fl.wait(t, f3[("lnload", tw)])
            if tw == 0:
                t.wait_ge(sv.h, p2_end_v)
                t.wait_ge(ss.h, p2_end_s)
                t.wait_ge(sd_initg.h, sd_initg.val)
            else:
                t.wait_ge(sv.h, f3[("accdone", tw - 1)])
                t.wait_ge(sv.h, f3[("final", tw - 1, 3)])
            for cc in range(16):
                MM(t, px1[0][0:8, :], gwT_sb[:, cc * 8:(cc + 1) * 8], ln2f[:, cc * 512:(cc + 1) * 512],
                         start=(cc == 0), stop=(cc == 15))
            f3[("gs", tw)] = st.inc(_lm[0])

        @blk.vector
        def _(v, tw=tw):
            v.wait_ge(st.h, f3[("gs", tw)])
            sv.inc(v.tensor_copy(gsf[:], px1[0][0:8, :]))
            f3[("gsf", tw)] = sv.val

        @blk.tensor
        def _(t, tw=tw):
            t.wait_ge(sv.h, f3[("gsf", tw)])
            for i in range(4):
                TR(t, px3[0][:, i * 8:(i + 1) * 8], gsf[0:8, i * 128:(i + 1) * 128], identf_sb[0:8, 0:8])
            f3[("gt", tw)] = st.inc(_lm[0])

        for i in range(4):
            @blk.vector
            def _(v, tw=tw, i=i):
                if i == 0:
                    v.wait_ge(st.h, f3[("gt", tw)])
                sp = px3[0][:, i * 8:(i + 1) * 8]
                v.reduce_max(m1n[:], sp, axis=AX, negate=True)
                sv.inc(v.reduce_max(m1p[:], sp, axis=AX))
                v.wait_ge(sv.h, sv.val)
                sv.inc(v.tensor_scalar(eqb[:], sp, m1p[:, 0:1], None, ALU.is_equal))
                v.wait_ge(sv.h, sv.val)
                sv.inc(v.scalar_tensor_tensor(sm2[:], eqb[:], -1e30, sp, ALU.mult, ALU.add))
                v.wait_ge(sv.h, sv.val)
                sv.inc(v.reduce_max(m2p[:], sm2[:], axis=AX))
                f3[("g1", tw, i)] = sv.val

            @blk.scalar
            def _(s, tw=tw, i=i):
                s.wait_ge(sv.h, f3[("g1", tw, i)])
                ss.inc(s.activation(expsb[:], px3[0][:, i * 8:(i + 1) * 8], AF.Exp, bias=m1n[:, 0:1]))
                f3[("gexp", tw, i)] = ss.val

            @blk.vector
            def _(v, tw=tw, i=i):
                v.wait_ge(ss.h, f3[("gexp", tw, i)])
                sv.inc(v.tensor_scalar(ge2[:], px3[0][:, i * 8:(i + 1) * 8], m2p[:, 0:1], None, ALU.is_ge))
                v.wait_ge(sv.h, sv.val)
                sv.inc(v.tensor_tensor(wun[:], expsb[:], ge2[:], ALU.mult))
                v.wait_ge(sv.h, sv.val)
                sv.inc(v.reduce_sum(d2b[:], wun[:], axis=AX))
                v.wait_ge(sv.h, sv.val)
                sv.inc(v.reciprocal(rd2b[:], d2b[:]))
                v.wait_ge(sv.h, sv.val)
                sv.inc(v.tensor_scalar(wtm[:], wun[:], rd2b[:, 0:1], None, ALU.mult))
                f3[("wtm", tw, i)] = sv.val

            @blk.tensor
            def _(t, tw=tw, i=i):
                t.wait_ge(sv.h, f3[("wtm", tw, i)])
                TR(t, px1[1][0:8, i * 128:(i + 1) * 128], wtm[:], identf_sb[:])
                f3[("wt", tw, i)] = st.inc(_lm[0])

            @blk.vector
            def _(v, tw=tw, i=i):
                v.wait_ge(st.h, f3[("wt", tw, i)])
                sv.inc(v.tensor_copy(wembf[0:8, i * 128:(i + 1) * 128], px1[1][0:8, i * 128:(i + 1) * 128]))
                f3[("wem", tw, i)] = sv.val

        @blk.tensor
        def _(t, tw=tw):
            t.wait_ge(sv.h, f3[("gsf", tw)])
            for cc in range(16):
                MM(t, px1[0][0:1, :], seT_sb[:, cc:cc + 1], ln2f[:, cc * 512:(cc + 1) * 512],
                         start=(cc == 0), stop=(cc == 15))
            f3[("sg", tw)] = st.inc(_lm[0])

        @blk.vector
        def _(v, tw=tw):
            v.wait_ge(st.h, f3[("sg", tw)])
            sv.inc(v.tensor_copy(sgf[:], px1[0][0:1, :]))
            f3[("sgf", tw)] = sv.val

        @blk.scalar
        def _(s, tw=tw):
            s.wait_ge(sv.h, f3[("sgf", tw)])
            ss.inc(s.activation(sgs[:], sgf[:], AF.Sigmoid))
            f3[("sgs", tw)] = ss.val

        @blk.vector
        def _(v, tw=tw):
            v.wait_ge(ss.h, f3[("sgs", tw)])
            sv.inc(v.tensor_copy(sgbf[:], sgs[:]))
            f3[("sgbf", tw)] = sv.val

        @blk.tensor
        def _(t, tw=tw):
            t.wait_ge(sv.h, f3[("sgbf", tw)])
            MM(t, px3[1][:, :], ones1_sb[:], sgbf[:], start=True, stop=True)
            f3[("sgrep_t", tw)] = st.inc(_lm[0])

        @blk.vector
        def _(v, tw=tw):
            v.wait_ge(st.h, f3[("sgrep_t", tw)])
            if ("x13", tw - 1, 9, 7) in f3:
                v.wait_ge(sg.h, f3[("x13", tw - 1, 9, 7)])
            sv.inc(v.tensor_copy(sgrep[:], px3[1][:, :]))
            f3[("sgrep", tw)] = sv.val
        px3_free[1] = f3[("sgrep", tw)]
        px1_free[0] = None
        px1_free[1] = None
        px3_free[0] = f3[("wtm", tw, 3)]
        first_of_tw = [True]

        for gi, grp in enumerate(groups):
            for ei, e in enumerate(grp):
                if e < NE:
                    @blk.tensor
                    def _(t, tw=tw, e=e):
                        for i in range(4):
                            t.wait_ge(sv.h, f3[("wem", tw, i)])
                        if px3_free[1]:
                            t.wait_ge(sv.h, px3_free[1])
                        MM(t, px3[1][:, :], onehot_sb[:, e * 128:(e + 1) * 128], wembf[:],
                                 start=True, stop=True)
                        f3[("wrep_t", tw, e)] = st.inc(_lm[0])

                    @blk.vector
                    def _(v, tw=tw, e=e):
                        v.wait_ge(st.h, f3[("wrep_t", tw, e)])
                        if wrep_user[0]:
                            v.wait_ge(sg.h, wrep_user[0])
                        sv.inc(v.tensor_copy(wrep[:], px3[1][:, :]))
                        f3[("wrep_v", tw, e)] = sv.val
                    px3_free[1] = f3[("wrep_v", tw, e)]

                for qh in range(4):
                    qg = nq[0]
                    nq[0] += 2
                    sgs_ = qg % 8
                    sus_ = (qg + 1) % 8

                    @blk.sync
                    def _(sync, tw=tw, e=e, qh=qh, sgs_=sgs_, sus_=sus_):
                        for sl in (sgs_, sus_):
                            pu = slot_user.get(sl)
                            if pu:
                                sync.wait_ge(st.h, pu)
                        sd_fw.issue(sync.dma_start(
                            out=wslot(sgs_).rearrange("p (cc f) -> p cc f", f=256),
                            in_=gate_src(e, qh).rearrange("(cc p) f -> p cc f", p=128)), key=sgs_)
                        sd_fw.issue(sync.dma_start(
                            out=wslot(sus_).rearrange("p (cc f) -> p cc f", f=256),
                            in_=up_src(e, qh).rearrange("(cc p) f -> p cc f", p=128)), key=sus_)
                    ld_i = sd_fw.n - 1

                    for hh in range(2):
                        hp = 2 * qh + hh
                        jp = hp % 2

                        @blk.tensor
                        def _(t, tw=tw, e=e, hp=hp, jp=jp, hh=hh, ld_i=ld_i, sgs_=sgs_):
                            sd_fw.wait(t, ld_i)
                            sd_fw.wait(t, ld_i - 1)
                            if first_of_tw[0]:
                                t.wait_ge(sv.h, f3[("sgrep", tw)])
                                t.wait_ge(ss.h, f3[("sgs", tw)])
                                first_of_tw[0] = False
                            if px1_free.get(jp):
                                t.wait_ge(ss.h, px1_free[jp])
                            if px3_free.get(jp):
                                t.wait_ge(sv.h, px3_free[jp])
                            for cc in range(16):
                                MM(t, px1[jp][:, :],
                                         wslot(sgs_)[:, cc * 256 + hh * 128:cc * 256 + (hh + 1) * 128],
                                         ln2f[:, cc * 512:(cc + 1) * 512], start=(cc == 0), stop=(cc == 15))
                            f3[("x1", tw, e, hp)] = st.inc(_lm[0])
                        slot_user[sgs_] = f3[("x1", tw, e, hp)]

                        @blk.scalar
                        def _(s, tw=tw, e=e, hp=hp, jp=jp):
                            s.wait_ge(st.h, f3[("x1", tw, e, hp)])
                            if s1_free.get(jp):
                                s.wait_ge(sv.h, s1_free[jp])
                            ss.inc(s.activation(s1_sb[jp][:], px1[jp][:], AF.Silu))
                            f3[("silu", tw, e, hp)] = ss.val
                        px1_free[jp] = f3[("silu", tw, e, hp)]

                        @blk.tensor
                        def _(t, tw=tw, e=e, hp=hp, jp=jp, hh=hh, sus_=sus_):
                            if px3_free.get(jp):
                                t.wait_ge(sv.h, px3_free[jp])
                            for cc in range(16):
                                MM(t, px3[jp][:, :],
                                         wslot(sus_)[:, cc * 256 + hh * 128:cc * 256 + (hh + 1) * 128],
                                         ln2f[:, cc * 512:(cc + 1) * 512], start=(cc == 0), stop=(cc == 15))
                            f3[("x3", tw, e, hp)] = st.inc(_lm[0])
                        slot_user[sus_] = f3[("x3", tw, e, hp)]

                        @blk.vector
                        def _(v, tw=tw, e=e, hp=hp, jp=jp):
                            v.wait_ge(st.h, f3[("x3", tw, e, hp)])
                            v.wait_ge(ss.h, f3[("silu", tw, e, hp)])
                            if tmp_free.get(jp):
                                v.wait_ge(sg.h, tmp_free[jp])
                            sv.inc(v.tensor_tensor(tmpb[jp][:], px3[jp][:], s1_sb[jp][:], ALU.mult))
                            f3[("tmp", tw, e, hp)] = sv.val
                        px3_free[jp] = f3[("tmp", tw, e, hp)]
                        s1_free[jp] = f3[("tmp", tw, e, hp)]

                        @blk.gpsimd
                        def _(g, tw=tw, e=e, ei=ei, hp=hp, jp=jp):
                            g.wait_ge(sv.h, f3[("tmp", tw, e, hp)])
                            if x13_free_st[0]:
                                g.wait_ge(st.h, x13_free_st[0])
                            if e < NE:
                                g.wait_ge(sv.h, f3[("wrep_v", tw, e)])
                                mulsrc = wrep
                            else:
                                g.wait_ge(sv.h, f3[("sgrep", tw)])
                                mulsrc = sgrep
                            sg.inc(g.tensor_tensor(x13w[:, (ei * 8 + hp) * 512:(ei * 8 + hp + 1) * 512],
                                                   tmpb[jp][:], mulsrc[:], ALU.mult))
                            f3[("x13", tw, e, hp)] = sg.val
                        tmp_free[jp] = f3[("x13", tw, e, hp)]
                        if hp == 7 and e < NE:
                            wrep_user[0] = f3[("x13", tw, e, 7)]

            def emit_wd_load(ei2, e2):
                half_b2 = (ei2 % 2) * 4

                @blk.sync
                def _(sync, e2=e2, half_b2=half_b2):
                    for sl in range(half_b2, half_b2 + 4):
                        pu = slot_user.get(sl)
                        if pu:
                            sync.wait_ge(st.h, pu)
                    for half in range(2):
                        sd_fw.issue(sync.dma_start(
                            out=warena[:, half_b2 * 4096 + half * 8192: half_b2 * 4096 + (half + 1) * 8192]
                                .rearrange("p (hc f) -> p hc f", f=C),
                            in_=down_src(e2, half).rearrange("(hc p) f -> p hc f", p=128)), key=half_b2)
                f3[("wdld", tw, e2)] = sd_fw.n - 1

            for ei in range(min(2, len(grp))):
                emit_wd_load(ei, grp[ei])

            for ei, e in enumerate(grp):
                half_base = (ei % 2) * 4
                for tt in range(4):
                    jj = tt % 2

                    @blk.tensor
                    def _(t, tw=tw, e=e, ei=ei, gi=gi, tt=tt, jj=jj, half_base=half_base):
                        sd_fw.wait(t, f3[("wdld", tw, e)])
                        t.wait_ge(sg.h, f3[("x13", tw, e, 7)])
                        if pso_free.get(jj):
                            t.wait_ge(sv.h, pso_free[jj])
                        if ei == 0 and tt == 0:
                            for jpx in (0, 1):
                                if px1_free.get(jpx):
                                    t.wait_ge(ss.h, px1_free[jpx])
                                if px3_free.get(jpx):
                                    t.wait_ge(sv.h, px3_free[jpx])
                        for hc in range(8):
                            for n4 in range(4):
                                MM(t, pso2[jj][:, n4 * 512:(n4 + 1) * 512],
                                         x13w[:, (ei * 8 + hc) * 512 + tt * 128:(ei * 8 + hc) * 512 + (tt + 1) * 128],
                                         warena[:, half_base * 4096 + hc * C + n4 * 512:
                                                half_base * 4096 + hc * C + (n4 + 1) * 512],
                                         start=(hc == 0), stop=(hc == 7))
                        f3[("dmm", tw, gi, ei, tt)] = st.inc(_lm[0])
                        if tt == 3:
                            for sl in range(half_base, half_base + 4):
                                slot_user[sl] = f3[("dmm", tw, gi, ei, tt)]

                    @blk.vector
                    def _(v, tw=tw, gi=gi, ei=ei, tt=tt, jj=jj):
                        v.wait_ge(st.h, f3[("dmm", tw, gi, ei, tt)])
                        if gi == 0 and ei == 0:
                            sv.inc(v.tensor_copy(accb[:, tt * C:(tt + 1) * C], pso2[jj][:]))
                        else:
                            sv.inc(v.tensor_tensor(accb[:, tt * C:(tt + 1) * C],
                                                   accb[:, tt * C:(tt + 1) * C], pso2[jj][:], ALU.add))
                        f3[("accadd", tw, gi, ei, tt)] = sv.val
                    pso_free[jj] = f3[("accadd", tw, gi, ei, tt)]

                if ei + 2 < len(grp):
                    emit_wd_load(ei + 2, grp[ei + 2])

            x13_free_st[0] = f3[("dmm", tw, gi, len(grp) - 1, 3)]
            px1_free[0] = None
            px1_free[1] = None
            if ("accadd", tw, gi, len(grp) - 1, 2) in f3:
                pa_guard = f3[("accadd", tw, gi, len(grp) - 1, 2)]
            else:
                pa_guard = f3[("accadd", tw, gi, len(grp) - 1, 3)]
            px3_free[0] = pa_guard
            px3_free[1] = pa_guard
            px1_guard = pa_guard

            f3[("pAfree", tw, gi)] = pa_guard


        for tt in range(4):
            @blk.sync
            def _(sync, tw=tw, tt=tt, tw0=tw0):
                if ("outdma", tw, tt - 1) in f3:
                    sync.wait_ge(sd_out.h, f3[("outdma", tw, tt - 1)])
                elif ("outdma", tw - 1, 3) in f3:
                    sync.wait_ge(sd_out.h, f3[("outdma", tw - 1, 3)])
                f3[("x3ld", tw, tt)] = sd_fx.issue(
                    sync.dma_start(out=xt3[:], in_=x2_scr[tw0 + tt * 128:tw0 + (tt + 1) * 128, :]))

            @blk.vector
            def _(v, tw=tw, tt=tt):
                last_gi = len(groups) - 1
                last_ei = len(groups[last_gi]) - 1
                v.wait_ge(sv.h, f3[("accadd", tw, last_gi, last_ei, tt)])
                sd_fx.wait(v, f3[("x3ld", tw, tt)])
                sv.inc(v.tensor_tensor(xt3[:], xt3[:], accb[:, tt * C:(tt + 1) * C], ALU.add))
                f3[("final", tw, tt)] = sv.val

            @blk.sync
            def _(sync, tw=tw, tt=tt, tw0=tw0):
                sync.wait_ge(sv.h, f3[("final", tw, tt)])
                sd_out.inc(sync.dma_start(out=out_own[tw0 + tt * 128:tw0 + (tt + 1) * 128, :], in_=xt3[:]), 16)
                f3[("outdma", tw, tt)] = sd_out.val

        f3[("accdone", tw)] = f3[("accadd", tw, len(groups) - 1, len(groups[-1]) - 1, 3)]
        f3[("lnlast", tw)] = f3[("x3", tw, 9, 7)]

    @blk.sync
    def _(sync):
        sync.wait_ge(sd_out.h, sd_out.val)
        if debug:
            sync.wait_ge(sd_dbg.h, sd_dbg.val)

    blk.flush()
    return nc



def _host_arrays(inputs, c):
    b, s = c // 4, c % 4
    x = inputs["L0_attn_hidden_states"]
    rows = ((np.arange(NT) * 4 + s)[:, None] * 128 + np.arange(128)[None, :]).ravel()
    d = {}
    d["x_own"] = np.ascontiguousarray(x[b][rows]).astype(np.float32, copy=False)

    inv = (1.0 / (10000.0 ** (np.arange(0, ROT, 2, dtype=np.float64) / ROT))).astype(np.float64)
    emb = rows[:, None].astype(np.float64) * inv[None, :]
    embf = np.concatenate([emb, emb], axis=1)
    d["cos_own"] = np.ascontiguousarray(np.cos(embf).T).astype(ml_dtypes.bfloat16)
    d["sin_own"] = np.ascontiguousarray(np.sin(embf).T).astype(ml_dtypes.bfloat16)

    tri = np.triu(np.ones((128, 128), dtype=np.float32))
    mk = np.zeros((8, 128, 1024), dtype=np.float32)
    for pp in range(8):
        for half in range(2):
            dd = 2 * pp + half
            for j in range(4):
                grel = 4 * j + s
                blkm = mk[pp][:, half * 512 + j * 128: half * 512 + (j + 1) * 128]
                if dd < grel:
                    blkm[:] = 1.0
                elif dd == grel:
                    blkm[:] = tri
    d["maskp"] = mk.reshape(8, 128 * 1024).astype(ml_dtypes.bfloat16)
    return d


def _shared_arrays(inputs):
    f32 = np.float32
    d = {}
    d["wqkv"] = np.ascontiguousarray(inputs["L0_attn_W_qkv"], dtype=f32)
    d["bqkv_col"] = np.ascontiguousarray(np.asarray(inputs["L0_attn_b_qkv"], dtype=f32).reshape(48, 128).T)
    d["wdense"] = np.ascontiguousarray(inputs["L0_attn_W_dense"], dtype=f32)
    d["bd_row"] = np.ascontiguousarray(np.asarray(inputs["L0_attn_b_dense"], dtype=f32)[None, :])
    d["gwT"] = np.ascontiguousarray(np.asarray(inputs["L0_ffn_gate_w"], dtype=f32).T)
    d["wgT"] = np.ascontiguousarray(np.asarray(inputs["L0_ffn_w_gate"], dtype=f32).transpose(0, 2, 1)).reshape(NE * C, H_FF)
    d["wuT"] = np.ascontiguousarray(np.asarray(inputs["L0_ffn_w_up"], dtype=f32).transpose(0, 2, 1)).reshape(NE * C, H_FF)
    d["wdT"] = np.ascontiguousarray(np.asarray(inputs["L0_ffn_w_down"], dtype=f32).transpose(0, 2, 1)).reshape(NE * H_FF, C)
    d["shgT"] = np.ascontiguousarray(np.asarray(inputs["L0_ffn_sh_gate"], dtype=f32).T)
    d["shuT"] = np.ascontiguousarray(np.asarray(inputs["L0_ffn_sh_up"], dtype=f32).T)
    d["shdT"] = np.ascontiguousarray(np.asarray(inputs["L0_ffn_sh_down"], dtype=f32).T)
    d["seT_col"] = np.ascontiguousarray(np.asarray(inputs["L0_ffn_se_gate_w"], dtype=f32).reshape(16, 128).T)
    d["ident128"] = np.eye(128, dtype=ml_dtypes.bfloat16)
    d["identf32"] = np.eye(128, dtype=f32)
    d["ones1"] = np.ones((1, 128), dtype=ml_dtypes.bfloat16)
    d["ones128"] = np.ones((128, 128), dtype=ml_dtypes.bfloat16)
    oh = np.zeros((8, 8 * 128), dtype=np.float32)
    for e in range(8):
        oh[e, e * 128:(e + 1) * 128] = 1.0
    d["onehot8"] = oh.astype(ml_dtypes.bfloat16)
    R = np.zeros((ROT, ROT), dtype=np.float32)
    for m in range(16):
        R[m, m + 16] = -1.0
        R[m + 16, m] = 1.0
    d["rpermT"] = np.ascontiguousarray(R.T).astype(ml_dtypes.bfloat16)
    d["cm8"] = np.full((128, 1), -8.0, dtype=f32)
    return d


_NC_CACHE = {}


def _get_nc(debug=False):
    key = bool(debug)
    if key not in _NC_CACHE:
        _NC_CACHE[key] = build_nc(debug=key)
    return _NC_CACHE[key]


def kernel(**inputs):
    nc = _get_nc(debug=False)
    shared = _shared_arrays(inputs)
    in_maps = []
    for c in range(NCORE):
        m = dict(shared)
        m.update(_host_arrays(inputs, c))
        in_maps.append(m)
    res = run_bass_kernel_spmd(nc, in_maps, core_ids=list(range(NCORE))).results

    x = inputs["L0_attn_hidden_states"]
    out = np.empty((2, S, C), dtype=np.float32)
    for c in range(NCORE):
        b, s = c // 4, c % 4
        rows = ((np.arange(NT) * 4 + s)[:, None] * 128 + np.arange(128)[None, :]).ravel()
        out[b][rows] = res[c]["out_own"]
    return out


# version 11
# speedup vs baseline: 1.0235x; 1.0235x over previous
import math

import numpy as np
import ml_dtypes

import concourse.bass as bass
import concourse.mybir as mybir
from concourse.bass_utils import run_bass_kernel_spmd
from concourse.alu_op_type import AluOpType as ALU

F32 = mybir.dt.float32
BF16 = mybir.dt.bfloat16
AF = mybir.ActivationFunctionType
AX = mybir.AxisListType.X

C = 2048
S = 8192
NH = 16
HD = 128
ROT = 32
T_OWN = 2048
NT = 16
H_FF = 1024
NE = 8
HS = 2048
SCALE = 1.0 / math.sqrt(HD)
NCORE = 8




class _FakeIns:
    __slots__ = ("incs",)

    def __init__(self):
        self.incs = []

    def then_inc(self, sem, val=None):
        self.incs.append((sem, val))
        return self


class _Rec:

    def __init__(self, ops):
        self._ops = ops

    def __getattr__(self, m):
        ops = self._ops

        def call(*a, **kw):
            fi = _FakeIns()
            ops.append((m, a, kw, fi))
            return fi
        return call


class DeferredBlock:
    ENGINES = ("sync", "tensor", "vector", "scalar", "gpsimd")

    def __init__(self, nc):
        self.nc = nc
        self.ops = {e: [] for e in self.ENGINES}

    def _mk(self, engname):
        rec = _Rec(self.ops[engname])

        def deco(fn):
            fn(rec)
        return deco

    def __getattr__(self, engname):
        if engname in self.ENGINES:
            return self._mk(engname)
        raise AttributeError(engname)

    def flush(self):
        nc = self.nc
        with nc.Block() as blk:
            for engname in self.ENGINES:
                ops = self.ops[engname]
                if not ops:
                    continue

                def emit(eng, ops=ops):
                    for m, a, kw, fi in ops:
                        ins = getattr(eng, m)(*a, **kw)
                        for sem, val in fi.incs:
                            if val is None:
                                ins.then_inc(sem)
                            else:
                                ins.then_inc(sem, val)
                getattr(blk, engname)(emit)
        for e in self.ENGINES:
            self.ops[e] = []


class Sem:
    def __init__(self, nc, name):
        self.h = nc.semaphore(name).__enter__()
        self.val = 0

    def inc(self, ins, by=1):
        ins.then_inc(self.h, by)
        self.val += by
        return self.val


class DmaStream:

    def __init__(self, nc, name, k=2):
        self.sems = [Sem(nc, f"{name}{j}") for j in range(k)]
        self.k = k
        self.n = 0
        self.marks = []

    def issue(self, ins, key=None):
        kj = (self.n if key is None else key) % self.k
        s = self.sems[kj]
        s.inc(ins, 16)
        self.marks.append((kj, s.val))
        self.n += 1
        return self.n - 1

    def wait(self, eng, i):
        j, v = self.marks[i]
        eng.wait_ge(self.sems[j].h, v)

    def wait_upto(self, eng, i):
        last = {}
        for j, v in self.marks[:i + 1]:
            last[j] = v
        for j, v in last.items():
            eng.wait_ge(self.sems[j].h, v)

    def wait_all(self, eng):
        for s in self.sems:
            if s.val:
                eng.wait_ge(s.h, s.val)

    def guard(self, eng, key):
        s = self.sems[key % self.k]
        if s.val:
            eng.wait_ge(s.h, s.val)


def build_nc(debug=False, stage=3):
    nc = bass.Bass("TRN2", target_bir_lowering=False)
    _lm = [None]

    def MM(t, *a, **kw):
        _lm[0] = t.matmul(*a, **kw)
        return _lm[0]

    def TR(t, *a, **kw):
        _lm[0] = t.transpose(*a, **kw)
        return _lm[0]

    def din(name, shape, dt=F32):
        return nc.dram_tensor(name, shape, dt, kind="ExternalInput").ap()

    x_own = din("x_own", [T_OWN, C])
    wqkv = din("wqkv", [C, 3 * C])
    bqkv_col = din("bqkv_col", [128, 48])
    wdense = din("wdense", [C, C])
    bd_row = din("bd_row", [1, C])
    gwT = din("gwT", [C, NE])
    wgT = din("wgT", [NE * C, H_FF])
    wuT = din("wuT", [NE * C, H_FF])
    wdT = din("wdT", [NE * H_FF, C])
    shgT = din("shgT", [C, HS])
    shuT = din("shuT", [C, HS])
    shdT = din("shdT", [HS, C])
    seT_col = din("seT_col", [128, 16])
    cos_own = din("cos_own", [ROT, T_OWN], BF16)
    sin_own = din("sin_own", [ROT, T_OWN], BF16)
    maskp = din("maskp", [8, 128 * 1024], BF16)
    ident128 = din("ident128", [128, 128], BF16)
    identf32 = din("identf32", [128, 128], F32)
    ones1 = din("ones1", [1, 128], BF16)
    ones128 = din("ones128", [128, 128], BF16)
    onehot8 = din("onehot8", [8, 8 * 128], BF16)
    rpermT = din("rpermT", [ROT, ROT], BF16)
    cm8 = din("cm8", [128, 1])

    out_own = nc.dram_tensor("out_own", [T_OWN, C], F32, kind="ExternalOutput").ap()

    def dscr(name, shape, dt):
        return nc.dram_tensor(name, shape, dt).ap()

    wqkv_bf = dscr("wqkv_bf", [C, 3 * C], BF16)
    wdense_bf = dscr("wdense_bf", [C, C], BF16)
    ln1_tm = dscr("ln1_tm", [T_OWN, C], BF16)
    cc_ink = dscr("cc_ink", [C, T_OWN], BF16)
    cc_invT = dscr("cc_invT", [T_OWN, C], BF16)
    q_scr = dscr("q_scr", [C, T_OWN], BF16)
    gk = [dscr(f"gk{g}", [4 * 512, T_OWN], BF16) for g in range(4)]
    gvT = [dscr(f"gvT{t}", [4 * 512, C], BF16) for t in range(4)]
    o_scr = dscr("o_scr", [C, T_OWN], BF16)
    x2_scr = dscr("x2_scr", [T_OWN, C], F32)
    ln2_tm = dscr("ln2_tm", [T_OWN, C], BF16)
    wg_scr = dscr("wg_scr", [NE * C, H_FF], BF16)
    wu_scr = dscr("wu_scr", [NE * C, H_FF], BF16)
    wd_scr = dscr("wd_scr", [NE * H_FF, C], BF16)
    shg_scr = dscr("shg_scr", [C, HS], BF16)
    shu_scr = dscr("shu_scr", [C, HS], BF16)
    shd_scr = dscr("shd_scr", [HS, C], BF16)

    dbg = {}
    if debug:
        for nm, shp in [("d_ln1", [T_OWN, C]), ("d_k", [C, T_OWN]), ("d_v", [C, T_OWN]),
                        ("d_q", [C, T_OWN]), ("d_o", [C, T_OWN]), ("d_x2", [T_OWN, C])]:
            dbg[nm] = nc.dram_tensor(nm, shp, F32, kind="ExternalOutput").ap()

    live = {}
    pslive = {}

    def alloc(name, shape, dt):
        t = nc.sbuf_tensor(name, shape, dt)
        h = t.__enter__()
        live[name] = t
        return h

    def free(*names):
        order = list(live)
        for n in sorted(names, key=order.index, reverse=True):
            live.pop(n).__exit__(None, None, None)

    def psalloc(name, shape):
        t = nc.psum_tensor(name, shape, F32)
        h = t.__enter__()
        pslive[name] = t
        return h

    def psfree(*names):
        order = list(pslive)
        for n in sorted(names, key=order.index, reverse=True):
            pslive.pop(n).__exit__(None, None, None)

    ident_sb = alloc("ident_sb", [128, 128], BF16)
    identf_sb = alloc("identf_sb", [128, 128], F32)
    ones1_sb = alloc("ones1_sb", [1, 128], BF16)
    ones128_sb = alloc("ones128_sb", [128, 128], BF16)
    onehot_sb = alloc("onehot_sb", [8, 8 * 128], BF16)
    rperm_sb = alloc("rperm_sb", [ROT, ROT], BF16)
    bqkv_sb = alloc("bqkv_sb", [128, 48], F32)
    cm8_sb = alloc("cm8_sb", [128, 1], F32)
    gwT_sb = alloc("gwT_sb", [128, 16 * NE], BF16)
    seT_sb = alloc("seT_sb", [128, 16], BF16)
    bd_sb = alloc("bd_sb", [1, C], BF16)

    sd_init = Sem(nc, "init")
    sd_initg = Sem(nc, "initg")
    sd_init2 = Sem(nc, "init2")
    sd_pcq = Sem(nc, "pcq")
    sd_pcd = Sem(nc, "pcd")
    sd_pc = Sem(nc, "pcw")
    sd_x = DmaStream(nc, "sdx", 2)
    sd_ln1w = DmaStream(nc, "ln1w", 2)
    sd_ln1r = DmaStream(nc, "ln1r", 2)
    sd_w = DmaStream(nc, "sdw", 2)
    sd_kv = DmaStream(nc, "kvw", 2)
    sd_vw = DmaStream(nc, "vww", 2)
    cck = [Sem(nc, f"cck{g}") for g in range(4)]
    ccv = [Sem(nc, f"ccv{g}") for g in range(4)]
    sv = Sem(nc, "sv")
    ss = Sem(nc, "ss")
    st = Sem(nc, "st")
    sg = Sem(nc, "sg")
    sd_kh = DmaStream(nc, "kh", 4)
    sd_vh = DmaStream(nc, "vh", 3)
    sd_o = DmaStream(nc, "ow", 2)
    sd_d = DmaStream(nc, "dd", 2)
    sd_x2 = DmaStream(nc, "x2w", 2)
    sd_lnw = DmaStream(nc, "lnw", 2)
    sd_fw = DmaStream(nc, "ffw", 8)
    sd_pcs = DmaStream(nc, "pcs", 8)
    sd_fl = DmaStream(nc, "ffl", 1)
    sd_fx = DmaStream(nc, "ffx", 1)
    sd_out = Sem(nc, "outw")
    sd_dbg = Sem(nc, "dbgw")

    blk = DeferredBlock(nc)

    cosb = alloc("cosb", [ROT, T_OWN], BF16)
    sinb = alloc("sinb", [ROT, T_OWN], BF16)
    ln1_fm = alloc("ln1_fm", [128, 16 * T_OWN], BF16)
    xt = [alloc(f"xt{j}", [128, C], F32) for j in range(2)]
    lnt = [alloc(f"lnt{j}", [128, C], BF16) for j in range(2)]
    stats = alloc("stats", [128, 48], F32)
    mv = alloc("mv", [128, 32], F32)
    negmean = alloc("negmean", [128, 16], F32)
    lnvar = alloc("lnvar", [128, 16], F32)
    rstd = alloc("rstd", [128, 16], F32)

    @blk.sync
    def _(sync):
        for dst, src in [(ident_sb, ident128), (identf_sb, identf32), (ones1_sb, ones1),
                         (ones128_sb, ones128), (onehot_sb, onehot8), (rperm_sb, rpermT),
                         (bqkv_sb, bqkv_col), (cm8_sb, cm8), (cosb, cos_own), (sinb, sin_own)]:
            sd_init.inc(sync.dma_start(out=dst[:], in_=src[:, :]), 16)
        sd_x.issue(sync.dma_start(out=xt[0][:], in_=x_own[0:128, :]))
        sd_x.issue(sync.dma_start(out=xt[1][:], in_=x_own[128:256, :]))

    @blk.gpsimd
    def _(g):
        sd_initg.inc(g.dma_start(out=gwT_sb[:].rearrange("p (cc e) -> p cc e", e=NE),
                                 in_=gwT[:, :].rearrange("(cc p) e -> p cc e", p=128)), 16)
        sd_initg.inc(g.dma_start(out=seT_sb[:], in_=seT_col[:, :]), 16)
        sd_initg.inc(g.dma_start(out=bd_sb[:], in_=bd_row[:, :]), 16)
        sd_pcq.inc(g.dma_start(out=wqkv_bf[:, :], in_=wqkv[:, :]), 16)
    pc_all = None

    stat_ms = [0] * NT
    for i in range(NT):
        @blk.vector
        def _(v, i=i):
            sd_x.wait(v, i)
            if i >= 2:
                v.wait_ge(sv.h, stat_ms[i - 2])
            sh = (i % 2) * 24
            for q in range(4):
                ins = v.bn_stats(stats[:, sh + q * 6:sh + (q + 1) * 6], xt[i % 2][:, q * 512:(q + 1) * 512])
            sv.inc(ins)
            v.wait_ge(sv.h, sv.val)
            sv.inc(v.bn_aggr(mv[:, 2 * i:2 * i + 2], stats[:, sh:sh + 24]))
        stat_ms[i] = sv.val

        @blk.sync
        def _(sync, i=i):
            if i + 2 < NT:
                sync.wait_ge(sv.h, stat_ms[i])
                sd_x.issue(sync.dma_start(out=xt[i % 2][:], in_=x_own[(i + 2) * 128:(i + 3) * 128, :]))

    @blk.vector
    def _(v):
        v.wait_ge(sv.h, stat_ms[NT - 1])
        mview = mv[:].rearrange("p (i two) -> p i two", two=2)
        v.tensor_scalar(negmean[:], mview[:, :, 0], -1.0, None, ALU.mult)
        sv.inc(v.tensor_scalar(lnvar[:], mview[:, :, 1], 1e-5, None, ALU.add))
    ms_lnstats = sv.val

    @blk.scalar
    def _(s):
        s.wait_ge(sv.h, ms_lnstats)
        ss.inc(s.activation(rstd[:], lnvar[:], AF.Ln))
    ms_ln = ss.val

    @blk.vector
    def _(v):
        v.wait_ge(ss.h, ms_ln)
        sv.inc(v.tensor_scalar(rstd[:], rstd[:], -0.5, None, ALU.mult))
    ms_half = sv.val

    @blk.scalar
    def _(s):
        s.wait_ge(sv.h, ms_half)
        ss.inc(s.activation(rstd[:], rstd[:], AF.Exp))
    ms_rstd = ss.val

    norm_ms = [0] * NT
    x2nd = [0] * NT
    for i in range(NT):
        @blk.sync
        def _(sync, i=i):
            if i < 2:
                sync.wait_ge(sv.h, stat_ms[NT - 1])
            else:
                sync.wait_ge(sv.h, norm_ms[i - 2])
            x2nd[i] = sd_x.issue(sync.dma_start(out=xt[i % 2][:], in_=x_own[i * 128:(i + 1) * 128, :]))

        @blk.vector
        def _(v, i=i):
            sd_x.wait(v, x2nd[i])
            if i == 0:
                v.wait_ge(ss.h, ms_rstd)
            if i >= 2:
                sd_ln1w.wait(v, i - 2)
            sv.inc(v.tensor_scalar(lnt[i % 2][:], xt[i % 2][:], negmean[:, i:i + 1], rstd[:, i:i + 1], ALU.add, ALU.mult))
        norm_ms[i] = sv.val

        @blk.sync
        def _(sync, i=i):
            sync.wait_ge(sv.h, norm_ms[i])
            sd_ln1w.issue(sync.dma_start(out=ln1_tm[i * 128:(i + 1) * 128, :], in_=lnt[i % 2][:]), key=i)

    @blk.sync
    def _(sync):
        sd_ln1w.wait_all(sync)
        for cc in range(16):
            sd_ln1r.issue(sync.dma_start(out=ln1_fm[:, cc * T_OWN:(cc + 1) * T_OWN],
                                         in_=ln1_tm[:, cc * 128:(cc + 1) * 128], transpose=True))

    free("xt0", "xt1", "lnt0", "lnt1", "stats", "mv", "negmean", "lnvar", "rstd")

    wsl = [alloc(f"wsl{j}", [128, 16 * 384], BF16) for j in range(2)]
    k_sb = [alloc(f"ksb{j}", [128, C], BF16) for j in range(2)]
    q_sb = [alloc(f"qsb{j}", [128, C], BF16) for j in range(2)]
    v_sbf = alloc("v_sbf", [128, C], F32)
    v_sbT = [alloc(f"vsbT{j}", [128, C], BF16) for j in range(2)]
    rt1 = alloc("rt1", [ROT, C], BF16)
    rt2 = alloc("rt2", [ROT, C], BF16)

    psA = psalloc("psA", [128, C])
    psB = psalloc("psB", [128, C])

    n_heads_emit = {10: 0, 11: 1, 12: 2, 13: 1, 14: 1, 15: 1, 16: 1}.get(stage, NH)
    head_parts = {13: ("q",), 14: ("q", "k"), 15: ("q", "k", "rope"), 16: ("q",)}.get(stage, ("q", "k", "rope", "v", "dma"))
    q_mm_single = (stage == 16)

    wslv = [alloc(f"wslv{j}", [128, 16 * 128], BF16) for j in range(2)]
    vq_mm = {}
    vq_ev = {}
    vq_tr = {}
    vq_te = {}
    vq_dma = {}
    if "v" in head_parts:
        @blk.sync
        def _(sync):
            sync.wait_ge(sd_pcq.h, 16)
            for h in (0, 1):
                sd_w.issue(sync.dma_start(
                    out=wslv[h % 2][:].rearrange("p (cc d) -> p cc d", d=128),
                    in_=wqkv_bf[:, h * 384 + 256:h * 384 + 384].rearrange("(cc p) d -> p cc d", p=128)), key=h)

        for tq in range(4):
            for h in range(NH):
                jv = h % 2
                pv_, po_ = (psA, psB) if h % 2 == 0 else (psB, psA)

                @blk.tensor
                def _(t, tq=tq, h=h, jv=jv, pv_=pv_):
                    if tq == 0 and h == 0:
                        t.wait_ge(sd_init.h, sd_init.val)
                        t.wait_ge(sd_initg.h, sd_initg.val)
                        sd_ln1r.wait_all(t)
                    sd_w.wait(t, 16 * tq + h)
                    k2 = 16 * tq + h
                    if k2 >= 2:
                        pe = vq_te.get(k2 - 2)
                        if pe:
                            t.wait_ge(ss.h, pe)
                    for cc in range(16):
                        MM(t, pv_[:, 0:512], wslv[jv][:, cc * 128:(cc + 1) * 128],
                           ln1_fm[:, cc * T_OWN + tq * 512:cc * T_OWN + tq * 512 + 512],
                           start=(cc == 0), stop=(cc == 15))
                    vq_mm[16 * tq + h] = st.inc(_lm[0])

                @blk.scalar
                def _(s, tq=tq, h=h, jv=jv, pv_=pv_):
                    k2 = 16 * tq + h
                    s.wait_ge(st.h, vq_mm[k2])
                    if k2 >= 2:
                        pe = vq_tr.get(k2 - 2)
                        if pe:
                            s.wait_ge(st.h, pe)
                    ss.inc(s.activation(v_sbf[:, jv * 512:(jv + 1) * 512], pv_[:, 0:512], AF.Identity,
                                        bias=bqkv_sb[:, 3 * h + 2:3 * h + 3]))
                    vq_ev[k2] = ss.val

                @blk.tensor
                def _(t, tq=tq, h=h, jv=jv, pv_=pv_):
                    k2 = 16 * tq + h
                    t.wait_ge(ss.h, vq_ev[k2])
                    if k2 >= 2:
                        pe = vq_te.get(k2 - 2)
                        if pe:
                            t.wait_ge(ss.h, pe)
                    for tl in range(4):
                        TR(t, pv_[:, 512 + tl * 128:512 + (tl + 1) * 128],
                           v_sbf[:, jv * 512 + tl * 128:jv * 512 + (tl + 1) * 128], identf_sb[:])
                    vq_tr[k2] = st.inc(_lm[0])

                @blk.scalar
                def _(s, tq=tq, h=h, jv=jv, pv_=pv_):
                    k2 = 16 * tq + h
                    s.wait_ge(st.h, vq_tr[k2])
                    if k2 >= 2:
                        pd = vq_dma.get(k2 - 2)
                        if pd is not None:
                            sd_vw.wait(s, pd)
                    ss.inc(s.activation(v_sbT[jv][:, 0:512], pv_[:, 512:1024], AF.Identity))
                    vq_te[k2] = ss.val

                @blk.sync
                def _(sync, tq=tq, h=h, jv=jv):
                    k2 = 16 * tq + h
                    sync.wait_ge(ss.h, vq_te[k2])
                    sd_vw.guard(sync, k2)
                    vq_dma[k2] = sd_vw.issue(sync.dma_start(
                        out=cc_invT[tq * 512:(tq + 1) * 512, h * 128:(h + 1) * 128]
                            .rearrange("(tl p) d -> p tl d", p=128),
                        in_=v_sbT[jv][:, 0:512].rearrange("p (tl d) -> p tl d", d=128)), key=k2)
                    k3 = k2 + 2
                    if k3 < 64:
                        h3 = k3 % 16
                        sync.wait_ge(st.h, vq_mm[k2])
                        sd_w.issue(sync.dma_start(
                            out=wslv[k3 % 2][:].rearrange("p (cc d) -> p cc d", d=128),
                            in_=wqkv_bf[:, h3 * 384 + 256:h3 * 384 + 384].rearrange("(cc p) d -> p cc d", p=128)), key=k3)

            if stage not in (-1, 10, 11, 12):
                @blk.gpsimd
                def _(g, tq=tq):
                    sd_vw.wait_upto(g, vq_dma[16 * tq + 15])
                    g.collective_compute(
                        "AllGather", ALU.bypass,
                        replica_groups=[[0, 1, 2, 3], [4, 5, 6, 7]],
                        ins=[cc_invT[tq * 512:(tq + 1) * 512, :]],
                        outs=[gvT[tq][:, :]],
                    ).then_inc(ccv[tq].h)
    n_w_loads = sd_w.n


    q_ready = [0] * NH
    q_evac = [0] * NH
    q_dma = [0] * NH
    vT_ready = [0] * NH
    vT_evac = [0] * NH
    vT_dma = [0] * NH
    k_ready = [0] * NH
    k_evac = [0] * NH
    rq_ready = [0] * NH
    rk_ready = [0] * NH
    qrot = [0] * NH
    krot = [0] * NH
    v_ready = [0] * NH
    v_evac = [0] * NH
    kv_after = [0] * NH

    @blk.sync
    def _(sync):
        sync.wait_ge(sd_pcq.h, 16)
        for h in (0, 1):
            sd_w.issue(sync.dma_start(
                out=wsl[h % 2][:].rearrange("p (cc d) -> p cc d", d=384),
                in_=wqkv_bf[:, h * 384:(h + 1) * 384].rearrange("(cc p) d -> p cc d", p=128)), key=64 + h)

    def wslice(j, cc, which):
        return wsl[j][:, cc * 384 + which * 128: cc * 384 + (which + 1) * 128]

    for h in range(n_heads_emit):
        j = h % 2
        psQ, psK = (psA, psB) if h % 2 == 0 else (psB, psA)

        @blk.tensor
        def _(t, h=h, j=j, psQ=psQ, psK=psK):
            if h == 0:
                t.wait_ge(sd_init.h, sd_init.val)
                t.wait_ge(sd_initg.h, sd_initg.val)
                sd_ln1r.wait_all(t)
                if "v" in head_parts:
                    t.wait_ge(ss.h, vq_te[63])
            sd_w.wait(t, n_w_loads + h)
            if h >= 1:
                t.wait_ge(sv.h, krot[h - 1])
            for n4 in range(4):
                for cc in range(16):
                    MM(t, psQ[:, n4 * 512:(n4 + 1) * 512], wslice(j, cc, 0),
                             ln1_fm[:, cc * T_OWN + n4 * 512:cc * T_OWN + (n4 + 1) * 512],
                             start=(cc == 0), stop=(cc == 15))
            q_ready[h] = st.inc(_lm[0])
            if "k" not in head_parts:
                return
            if h >= 1:
                t.wait_ge(sv.h, qrot[h - 1])
            for n4 in range(4):
                for cc in range(16):
                    MM(t, psK[:, n4 * 512:(n4 + 1) * 512], wslice(j, cc, 1),
                             ln1_fm[:, cc * T_OWN + n4 * 512:cc * T_OWN + (n4 + 1) * 512],
                             start=(cc == 0), stop=(cc == 15))
            k_ready[h] = st.inc(_lm[0])

        @blk.vector
        def _(v, h=h, j=j, psQ=psQ):
            v.wait_ge(st.h, q_ready[h])
            if h >= 2:
                sd_kv.wait(v, kv_after[h - 2])
            sv.inc(v.tensor_scalar(q_sb[j][:], psQ[:], bqkv_sb[:, 3 * h:3 * h + 1], SCALE, ALU.add, ALU.mult))
            q_evac[h] = sv.val

        @blk.scalar
        def _(s, h=h, j=j, psK=psK):
            if "k" not in head_parts:
                return
            s.wait_ge(st.h, k_ready[h])
            if h >= 2:
                sd_kv.wait(s, kv_after[h - 2])
            ss.inc(s.activation(k_sb[j][:], psK[:], AF.Identity, bias=bqkv_sb[:, 3 * h + 1:3 * h + 2]))
            k_evac[h] = ss.val

        @blk.tensor
        def _(t, h=h, j=j, psQ=psQ, psK=psK):
            if "rope" not in head_parts:
                return
            t.wait_ge(sv.h, q_evac[h])
            for n4 in range(4):
                MM(t, psQ[0:ROT, n4 * 512:(n4 + 1) * 512], rperm_sb[:],
                         q_sb[j][0:ROT, n4 * 512:(n4 + 1) * 512], start=True, stop=True)
            rq_ready[h] = st.inc(_lm[0])
            t.wait_ge(ss.h, k_evac[h])
            for n4 in range(4):
                MM(t, psK[0:ROT, n4 * 512:(n4 + 1) * 512], rperm_sb[:],
                         k_sb[j][0:ROT, n4 * 512:(n4 + 1) * 512], start=True, stop=True)
            rk_ready[h] = st.inc(_lm[0])

        @blk.vector
        def _(v, h=h, j=j, psQ=psQ, psK=psK):
            if "rope" not in head_parts:
                return
            v.wait_ge(st.h, rq_ready[h])
            if h >= 1:
                v.wait_ge(sv.h, krot[h - 1])
            v.tensor_tensor(rt1[:], q_sb[j][0:ROT, :], cosb[:], ALU.mult)
            sv.inc(v.tensor_tensor(rt2[:], psQ[0:ROT, :], sinb[:], ALU.mult))
            v.wait_ge(sv.h, sv.val)
            sv.inc(v.tensor_tensor(q_sb[j][0:ROT, :], rt1[:], rt2[:], ALU.add))
            qrot[h] = sv.val
            v.wait_ge(st.h, rk_ready[h])
            v.wait_ge(sv.h, qrot[h])
            v.tensor_tensor(rt1[:], k_sb[j][0:ROT, :], cosb[:], ALU.mult)
            sv.inc(v.tensor_tensor(rt2[:], psK[0:ROT, :], sinb[:], ALU.mult))
            v.wait_ge(sv.h, sv.val)
            sv.inc(v.tensor_tensor(k_sb[j][0:ROT, :], rt1[:], rt2[:], ALU.add))
            krot[h] = sv.val


        @blk.sync
        def _(sync, h=h, j=j):
            if "dma" not in head_parts:
                return
            sync.wait_ge(sv.h, krot[h])
            sd_kv.guard(sync, h)
            sd_kv.issue(sync.dma_start(out=cc_ink[h * 128:(h + 1) * 128, :], in_=k_sb[j][:]), key=h)
            sync.wait_ge(sv.h, qrot[h])
            q_dma[h] = sd_kv.issue(sync.dma_start(out=q_scr[h * 128:(h + 1) * 128, :], in_=q_sb[j][:]), key=h)
            kv_after[h] = sd_kv.n - 1
            if h + 2 < NH:
                sync.wait_ge(st.h, k_ready[h] if "k" in head_parts else q_ready[h])
                sd_w.issue(sync.dma_start(
                    out=wsl[j][:].rearrange("p (cc d) -> p cc d", d=384),
                    in_=wqkv_bf[:, (h + 2) * 384:(h + 3) * 384].rearrange("(cc p) d -> p cc d", p=128)), key=64 + h)

        if h % 4 == 3 and stage not in (-1, 10, 11, 12):
            g4 = h // 4

            @blk.gpsimd
            def _(g, g4=g4, h=h):
                sd_kv.wait_upto(g, sd_kv.n - 1)
                g.collective_compute(
                    "AllGather", ALU.bypass,
                    replica_groups=[[0, 1, 2, 3], [4, 5, 6, 7]],
                    ins=[cc_ink[g4 * 512:(g4 + 1) * 512, :]],
                    outs=[gk[g4][:, :]],
                ).then_inc(cck[g4].h)

    p0_end_t = st.val
    p0_end_s = ss.val
    p0_end_v = sv.val
    @blk.gpsimd
    def _(g):
        if stage not in (-1, 10, 11, 12):
            g.wait_ge((ccv if stage == 3 else cck)[3].h, 1)
        sd_pcd.inc(g.dma_start(out=wdense_bf[:, :], in_=wdense[:, :]), 16)
        if stage == 3:
            return
        for e in range(NE):
            sd_pc.inc(g.dma_start(out=wg_scr[e * C:(e + 1) * C, :], in_=wgT[e * C:(e + 1) * C, :]), 16)
            sd_pc.inc(g.dma_start(out=wu_scr[e * C:(e + 1) * C, :], in_=wuT[e * C:(e + 1) * C, :]), 16)
            sd_pc.inc(g.dma_start(out=wd_scr[e * H_FF:(e + 1) * H_FF, :], in_=wdT[e * H_FF:(e + 1) * H_FF, :]), 16)
        sd_pc.inc(g.dma_start(out=shg_scr[:, :], in_=shgT[:, :]), 16)
        sd_pc.inc(g.dma_start(out=shu_scr[:, :], in_=shuT[:, :]), 16)
        sd_pc.inc(g.dma_start(out=shd_scr[:, :], in_=shdT[:, :]), 16)
    pc_all = sd_pc.val


    if debug and stage not in (13, 14, 15, 16):
        @blk.gpsimd
        def _(g):
            sd_ln1w.wait_all(g)
            sd_dbg.inc(g.dma_start(out=dbg["d_ln1"][:, :], in_=ln1_tm[:, :]), 16)
            sd_kv.wait_all(g)
            sd_vw.wait_all(g)
            sd_dbg.inc(g.dma_start(out=dbg["d_k"][:, :], in_=cc_ink[:, :]), 16)
            sd_dbg.inc(g.dma_start(out=dbg["d_v"][:, :], in_=cc_invT[:, :]), 16)
            sd_dbg.inc(g.dma_start(out=dbg["d_q"][:, :], in_=q_scr[:, :]), 16)

    free("cosb", "sinb", "ln1_fm", "wsl0", "wsl1", "ksb0", "ksb1", "qsb0", "qsb1",
         "v_sbf", "vsbT0", "vsbT1", "rt1", "rt2", "wslv0", "wslv1")
    psfree("psA", "psB")

    if stage in (0, -1, 10, 11, 12, 13, 14, 15, 16):
        @blk.gpsimd
        def _(g):
            sd_ln1w.wait_all(g)
            sd_kv.wait_all(g)
            if stage in (13, 14, 15, 16):
                g.wait_ge(sv.h, q_evac[0])
            if stage in (14, 15):
                g.wait_ge(ss.h, k_evac[0])
            if stage == 15:
                g.wait_ge(sv.h, krot[0])
            if stage == 10:
                g.wait_ge(sd_w.sems[0].h, sd_w.sems[0].val)
                g.wait_ge(sd_w.sems[1].h, sd_w.sems[1].val)
                for sY in sd_ln1r.sems:
                    g.wait_ge(sY.h, sY.val)
            if stage == 0:
                for g4 in range(4):
                    g.wait_ge(cck[g4].h, 1)
                    g.wait_ge(ccv[g4].h, 1)
            for hh in range(NH):
                g.wait_ge(sv.h, qrot[hh])
            sd_out.inc(g.dma_start(out=out_own[:, :], in_=ln1_tm[:, :]), 16)
            g.wait_ge(sd_out.h, sd_out.val)
            if debug:
                g.wait_ge(sd_dbg.h, sd_dbg.val)
        blk.flush()
        return nc

    masks_sb = alloc("masks_sb", [128, 8 * 1024], BF16)
    qp_buf = [alloc(f"qp{j}", [128, T_OWN], BF16) for j in range(4)]
    k_hbuf = [alloc(f"khb{j}", [128, 4 * T_OWN], BF16) for j in range(4)]
    vt_buf = [alloc(f"vtb{j}", [128, 64 * 128], BF16) for j in range(3)]
    pT = [alloc(f"pT{j}", [128, 1024], BF16) for j in range(4)]
    rD2 = [alloc(f"rD{j}", [128, 512], F32) for j in range(2)]
    o_ev = [alloc(f"oev{j}", [128, 512], BF16) for j in range(2)]

    @blk.sync
    def _(sync):
        sd_init.inc(sync.dma_start(out=masks_sb[:].rearrange("p (m f) -> p m f", f=1024),
                                   in_=maskp[:, :].rearrange("m (p f) -> p m f", f=1024)), 16)
    init_all = sd_init.val

    psS = [psalloc(f"psS{j}", [128, 1024]) for j in range(2)]
    psO2 = [psalloc(f"psO{j}", [128, 512]) for j in range(2)]
    psDa = [psalloc(f"psDa{j}", [128, 512]) for j in range(2)]

    def kcol(kt):
        return (kt % 4) * T_OWN + (kt // 4) * 128

    kq_load = [0] * NH
    vt_load = {}
    av_last = [0] * NH
    s_ready = {}
    exp_ms = {}
    mask_ms = {}
    dadd_ms = {}
    av_done = {}
    o_ev_done = {}
    o_dma_mark = {}
    rd_done = {}
    rd_prev = [0, 0]

    def issue_kq(h):
        j3 = h % 4
        g4 = h // 4

        @blk.sync
        def _(sync, h=h, j3=j3, g4=g4):
            sync.wait_ge(cck[g4].h, 1)
            if h < 4:
                sync.wait_ge(st.h, p0_end_t)
                sync.wait_ge(ss.h, p0_end_s)
                sync.wait_ge(sv.h, p0_end_v)
                sd_kv.wait_all(sync)
                sd_vw.wait_all(sync)
                sd_w.wait_all(sync)
            else:
                sync.wait_ge(st.h, av_last[h - 4])
            sd_kh.guard(sync, h)
            sd_kh.issue(sync.dma_start(out=qp_buf[j3][:], in_=q_scr[h * 128:(h + 1) * 128, :]), key=h)
            hl = h % 4
            for r in range(4):
                sd_kh.issue(sync.dma_start(out=k_hbuf[j3][:, r * T_OWN:(r + 1) * T_OWN],
                                           in_=gk[g4][r * 512 + hl * 128:r * 512 + (hl + 1) * 128, :]), key=h)
            kq_load[h] = sd_kh.n - 1

    def issue_vt(h, w):
        j3 = h % 3

        @blk.sync
        def _(sync, h=h, w=w, j3=j3):
            sync.wait_ge(ccv[w].h, 1)
            if h >= 3:
                sync.wait_ge(st.h, av_last[h - 3])
            sd_vh.guard(sync, h % 3)
            for r in range(4):
                base = (16 * w + r) * 128
                dst = vt_buf[j3][:].rearrange("p (kt d) -> p kt d", d=128)[:, 16 * w + r:16 * w + r + 13:4, :]
                srcv = gvT[w][r * 512:(r + 1) * 512, h * 128:(h + 1) * 128] \
                    .rearrange("(jj p) d -> p jj d", p=128)
                sd_vh.issue(sync.dma_start(out=dst, in_=srcv), key=h % 3)
            vt_load[(h, w)] = sd_vh.n - 1

    issue_kq(0)
    issue_kq(1)
    issue_kq(2)
    issue_kq(3)

    for m in range(NH // 2):
        hA, hB = 2 * m, 2 * m + 1
        jA, jB = hA % 4, hB % 4
        vA, vB = hA % 3, hB % 3

        issue_vt(hA, 0)
        issue_vt(hB, 0)

        @blk.tensor
        def _(t, hA=hA, hB=hB):
            sd_kh.wait(t, kq_load[hA])
            sd_kh.wait(t, kq_load[hB])
            if hA == 0:
                t.wait_ge(sd_init.h, init_all)
                t.wait_ge(ss.h, p0_end_s)
                t.wait_ge(sv.h, p0_end_v)

        for w in range(4):
            npair = 8 * (w + 1)

            def emit_S(t, head, pair, w=w):
                jj = head % 4
                hp = head % 2
                if pair >= 1:
                    pk = exp_ms.get((head, w, pair - 1))
                elif w >= 1:
                    pk = exp_ms.get((head, w - 1, 8 * w - 1))
                else:
                    pk = exp_ms.get((head - 2, 3, 31))
                if pk:
                    t.wait_ge(ss.h, pk)
                qs = w * 512
                for half in range(2):
                    kt = 2 * pair + half
                    MM(t, psS[hp][:, half * 512:(half + 1) * 512],
                       k_hbuf[jj][:, kcol(kt):kcol(kt) + 128],
                       qp_buf[jj][:, qs:qs + 512], start=True, stop=True)
                s_ready[(head, w, pair)] = st.inc(_lm[0])

            @blk.tensor
            def _(t, hA=hA, hB=hB, w=w):
                sd_vh.wait(t, vt_load[(hA, w)])
                sd_vh.wait(t, vt_load[(hB, w)])
                emit_S(t, hA, 0)
                emit_S(t, hB, 0)

            for p in range(npair):
                for head in (hA, hB):
                    hp = head % 2

                    @blk.scalar
                    def _(s, head=head, hp=hp, w=w, p=p):
                        s.wait_ge(st.h, s_ready[(head, w, p)])
                        pprev = []
                        for back in (2,):
                            q_ = p - back
                            if q_ >= 0:
                                pprev.append((head, w, q_))
                            elif w >= 1:
                                pprev.append((head, w - 1, 8 * w + q_))
                            else:
                                pprev.append((head - 2, 3, 32 + q_))
                        for key in pprev:
                            pa = av_done.get(key)
                            if pa:
                                s.wait_ge(st.h, pa)
                        ss.inc(s.activation(pT[2 * hp + p % 2][:], psS[hp][:], AF.Exp))
                        exp_ms[(head, w, p)] = ss.val

                if p + 1 < npair:
                    @blk.tensor
                    def _(t, hA=hA, hB=hB, p=p):
                        emit_S(t, hA, p + 1)
                        emit_S(t, hB, p + 1)

                for head in (hA, hB):
                    hp = head % 2
                    if p >= 8 * w:
                        @blk.vector
                        def _(v, head=head, hp=hp, w=w, p=p):
                            v.wait_ge(ss.h, exp_ms[(head, w, p)])
                            mi = p - 8 * w
                            pb = pT[2 * hp + p % 2]
                            sv.inc(v.tensor_tensor(pb[:], pb[:],
                                                   masks_sb[:, mi * 1024:(mi + 1) * 1024], ALU.mult))
                            mask_ms[(head, w, p)] = sv.val

                for head in (hA, hB):
                    hp = head % 2
                    j3 = head % 3

                    @blk.tensor
                    def _(t, head=head, hp=hp, j3=j3, w=w, p=p, npair=npair):
                        if (head, w, p) in mask_ms:
                            t.wait_ge(sv.h, mask_ms[(head, w, p)])
                        else:
                            t.wait_ge(ss.h, exp_ms[(head, w, p)])
                        if p == 0:
                            pv = o_ev_done.get((head, w - 1), o_ev_done.get((head - 2, 3)))
                            if pv:
                                t.wait_ge(sv.h, pv)
                            if rd_prev[hp]:
                                t.wait_ge(sv.h, rd_prev[hp])
                        for half in range(2):
                            kt = 2 * p + half
                            MM(t, psO2[hp][:], vt_buf[j3][:, kt * 128:(kt + 1) * 128],
                               pT[2 * hp + p % 2][:, half * 512:(half + 1) * 512],
                               start=(p == 0 and half == 0), stop=(p == npair - 1 and half == 1))
                        for half in range(2):
                            MM(t, psDa[hp][:], ones128_sb[:],
                               pT[2 * hp + p % 2][:, half * 512:(half + 1) * 512],
                               start=(p == 0 and half == 0), stop=(p == npair - 1 and half == 1))
                        av_done[(head, w, p)] = st.inc(_lm[0])

            if w + 1 < 4:
                issue_vt(hA, w + 1)
                issue_vt(hB, w + 1)

            for head in (hA, hB):
                hp = head % 2

                @blk.vector
                def _(v, head=head, hp=hp, w=w, npair=npair):
                    v.wait_ge(st.h, av_done[(head, w, npair - 1)])
                    sv.inc(v.reciprocal(rD2[hp][:], psDa[hp][:]))
                    rd_done[(head, w)] = sv.val
                    rd_prev[hp] = sv.val
                    v.wait_ge(sv.h, sv.val)
                    pm = o_dma_mark.get((head, w - 1), o_dma_mark.get((head - 2, 3)))
                    if pm is not None:
                        sd_o.wait(v, pm)
                    sv.inc(v.tensor_tensor(o_ev[hp][:], psO2[hp][:], rD2[hp][:], ALU.mult))
                    o_ev_done[(head, w)] = sv.val

                @blk.sync
                def _(sync, head=head, hp=hp, w=w):
                    sync.wait_ge(sv.h, o_ev_done[(head, w)])
                    sd_o.issue(sync.dma_start(out=o_scr[head * 128:(head + 1) * 128, w * 512:(w + 1) * 512],
                                              in_=o_ev[hp][:]), key=hp)
                    o_dma_mark[(head, w)] = sd_o.n - 1

        av_last[hA] = av_done[(hA, 3, 8 * 4 - 1)]
        av_last[hB] = av_done[(hB, 3, 8 * 4 - 1)]
        if 2 * m + 4 < NH:
            issue_kq(2 * m + 4)
        if 2 * m + 5 < NH:
            issue_kq(2 * m + 5)

    p1_end_t = st.val
    p1_end_v = sv.val
    p1_end_s = ss.val

    if debug:
        @blk.gpsimd
        def _(g):
            sd_o.wait_all(g)
            sd_dbg.inc(g.dma_start(out=dbg["d_o"][:, :], in_=o_scr[:, :]), 16)

    free("masks_sb", "qp0", "qp1", "qp2", "qp3", "khb0", "khb1", "khb2", "khb3",
         "vtb0", "vtb1", "vtb2", "pT0", "pT1", "pT2", "pT3", "rD0", "rD1", "oev0", "oev1")
    psfree("psS0", "psS1", "psO0", "psO1", "psDa0", "psDa1")

    pc_e_mark = {}
    if stage == 3:
        for e in range(NE):
            @blk.gpsimd
            def _(g, e=e):
                g.wait_ge(st.h, av_last[2 * min(e, 6)])
                sd_pcs.issue(g.dma_start(out=wg_scr[e * C:(e + 1) * C, :], in_=wgT[e * C:(e + 1) * C, :]), key=e)
                sd_pcs.issue(g.dma_start(out=wu_scr[e * C:(e + 1) * C, :], in_=wuT[e * C:(e + 1) * C, :]), key=e)
                sd_pcs.issue(g.dma_start(out=wd_scr[e * H_FF:(e + 1) * H_FF, :], in_=wdT[e * H_FF:(e + 1) * H_FF, :]), key=e)
            pc_e_mark[e] = sd_pcs.n - 1

        @blk.gpsimd
        def _(g):
            g.wait_ge(st.h, av_last[12])
            g.wait_ge(sd_pcd.h, 16)
            sd_pcd.inc(g.dma_start(out=shg_scr[:, :], in_=shgT[:, :]), 16)
            sd_pcd.inc(g.dma_start(out=shu_scr[:, :], in_=shuT[:, :]), 16)
            sd_pcd.inc(g.dma_start(out=shd_scr[:, :], in_=shdT[:, :]), 16)

    if stage == 1:
        @blk.gpsimd
        def _(g):
            sd_o.wait_all(g)
            g.wait_ge(st.h, p1_end_t)
            sd_out.inc(g.dma_start(out=out_own[:, :], in_=o_scr[:, :]), 16)
            g.wait_ge(sd_out.h, sd_out.val)
            if debug:
                g.wait_ge(sd_dbg.h, sd_dbg.val)
        blk.flush()
        return nc

    o_sb = alloc("o_sb", [128, 16 * T_OWN], BF16)
    wd_sb = alloc("wd_sb", [128, 16 * C], BF16)
    brep = alloc("brep", [128, C], F32)
    xt2 = [alloc(f"xt2{j}", [128, C], F32) for j in range(2)]
    x2t = [alloc(f"x2t{j}", [128, C], F32) for j in range(2)]
    lnt2 = [alloc(f"lnt2{j}", [128, C], BF16) for j in range(2)]
    stats2 = alloc("stats2", [128, 48], F32)
    mv2 = alloc("mv2", [128, 32], F32)
    negmean2 = alloc("negmean2", [128, 16], F32)
    lnvar2 = alloc("lnvar2", [128, 16], F32)
    rstd2 = alloc("rstd2", [128, 16], F32)

    psD2 = [psalloc(f"psD2{j}", [128, C]) for j in range(2)]

    xt2_idx = {}

    @blk.sync
    def _(sync):
        sd_o.wait_all(sync)
        sync.wait_ge(st.h, p1_end_t)
        sync.wait_ge(sv.h, p1_end_v)
        for cc in range(16):
            sd_d.issue(sync.dma_start(out=o_sb[:, cc * T_OWN:(cc + 1) * T_OWN],
                                      in_=o_scr[cc * 128:(cc + 1) * 128, :]))
        sync.wait_ge(sd_pcd.h, 16)
        sd_d.issue(sync.dma_start(out=wd_sb[:].rearrange("p (cc f) -> p cc f", f=C),
                                  in_=wdense_bf[:, :].rearrange("(cc p) f -> p cc f", p=128)))
        xt2_idx[0] = sd_d.issue(sync.dma_start(out=xt2[0][:], in_=x_own[0:128, :]))
        xt2_idx[1] = sd_d.issue(sync.dma_start(out=xt2[1][:], in_=x_own[128:256, :]))
    dense_last_load = sd_d.n - 1

    ms = {}

    @blk.tensor
    def _(t):
        t.wait_ge(sd_init.h, init_all)
        t.wait_ge(sd_initg.h, sd_initg.val)
        t.wait_ge(ss.h, p1_end_s)
        t.wait_ge(sv.h, p1_end_v)
        for n4 in range(4):
            MM(t, psD2[0][:, n4 * 512:(n4 + 1) * 512], ones1_sb[:], bd_sb[:, n4 * 512:(n4 + 1) * 512],
                     start=True, stop=True)
        ms["brep_t"] = st.inc(_lm[0])

    @blk.vector
    def _(v):
        v.wait_ge(st.h, ms["brep_t"])
        sv.inc(v.tensor_copy(brep[:], psD2[0][:]))
        ms["brep_v"] = sv.val

    dmm_ms = [0] * NT
    x2_ms = [0] * NT
    ln2_ms = [0] * NT
    x2_after = [0] * NT
    ln2w = [0] * NT
    for tt in range(NT):
        jj = tt % 2

        @blk.tensor
        def _(t, tt=tt, jj=jj):
            if tt == 0:
                sd_d.wait_upto(t, dense_last_load)
                t.wait_ge(sv.h, ms["brep_v"])
            if tt >= 2:
                t.wait_ge(sv.h, x2_ms[tt - 2])
            for cc in range(16):
                for n4 in range(4):
                    MM(t, psD2[jj][:, n4 * 512:(n4 + 1) * 512],
                             o_sb[:, cc * T_OWN + tt * 128:cc * T_OWN + (tt + 1) * 128],
                             wd_sb[:, cc * C + n4 * 512:cc * C + (n4 + 1) * 512],
                             start=(cc == 0), stop=(cc == 15))
            dmm_ms[tt] = st.inc(_lm[0])

        @blk.vector
        def _(v, tt=tt, jj=jj):
            v.wait_ge(st.h, dmm_ms[tt])
            if tt >= 2:
                sd_x2.wait(v, x2_after[tt - 2])
            sv.inc(v.tensor_tensor(x2t[jj][:], psD2[jj][:], brep[:], ALU.add))
            v.wait_ge(sv.h, sv.val)
            sd_d.wait(v, xt2_idx[tt])
            sv.inc(v.tensor_tensor(x2t[jj][:], x2t[jj][:], xt2[jj][:], ALU.add))
            v.wait_ge(sv.h, sv.val)
            sh = (tt % 2) * 24
            for q in range(4):
                ins = v.bn_stats(stats2[:, sh + q * 6:sh + (q + 1) * 6], x2t[jj][:, q * 512:(q + 1) * 512])
            sv.inc(ins)
            v.wait_ge(sv.h, sv.val)
            sv.inc(v.bn_aggr(mv2[:, 2 * tt:2 * tt + 2], stats2[:, sh:sh + 24]))
            ln2_ms[tt] = sv.val
            x2_ms[tt] = sv.val

        @blk.sync
        def _(sync, tt=tt, jj=jj):
            sync.wait_ge(sv.h, x2_ms[tt])
            x2_after[tt] = sd_x2.issue(sync.dma_start(out=x2_scr[tt * 128:(tt + 1) * 128, :], in_=x2t[jj][:]), key=tt)
            if tt + 2 < NT:
                xt2_idx[tt + 2] = sd_d.issue(sync.dma_start(out=xt2[jj][:], in_=x_own[(tt + 2) * 128:(tt + 3) * 128, :]))

    @blk.vector
    def _(v):
        v.wait_ge(sv.h, x2_ms[NT - 1])
        mview = mv2[:].rearrange("p (i two) -> p i two", two=2)
        v.tensor_scalar(negmean2[:], mview[:, :, 0], -1.0, None, ALU.mult)
        sv.inc(v.tensor_scalar(lnvar2[:], mview[:, :, 1], 1e-5, None, ALU.add))
    ms2_stats = sv.val

    @blk.scalar
    def _(s):
        s.wait_ge(sv.h, ms2_stats)
        ss.inc(s.activation(rstd2[:], lnvar2[:], AF.Ln))
    ms2_ln = ss.val

    @blk.vector
    def _(v):
        v.wait_ge(ss.h, ms2_ln)
        sv.inc(v.tensor_scalar(rstd2[:], rstd2[:], -0.5, None, ALU.mult))
    ms2_half = sv.val

    @blk.scalar
    def _(s):
        s.wait_ge(sv.h, ms2_half)
        ss.inc(s.activation(rstd2[:], rstd2[:], AF.Exp))
    ms2_rstd = ss.val

    ln2n_ms = [0] * NT
    x2rd = [0] * NT
    for tt in range(NT):
        jj = tt % 2

        @blk.sync
        def _(sync, tt=tt, jj=jj):
            if tt < 2:
                sync.wait_ge(sv.h, x2_ms[NT - 1])
            else:
                sync.wait_ge(sv.h, ln2n_ms[tt - 2])
            sd_x2.wait(sync, x2_after[tt])
            x2rd[tt] = sd_d.issue(sync.dma_start(out=xt2[jj][:], in_=x2_scr[tt * 128:(tt + 1) * 128, :]))

        @blk.vector
        def _(v, tt=tt, jj=jj):
            sd_d.wait(v, x2rd[tt])
            if tt == 0:
                v.wait_ge(ss.h, ms2_rstd)
            if tt >= 2:
                sd_lnw.wait(v, ln2w[tt - 2])
            sv.inc(v.tensor_scalar(lnt2[jj][:], xt2[jj][:], negmean2[:, tt:tt + 1], rstd2[:, tt:tt + 1], ALU.add, ALU.mult))
        ln2n_ms[tt] = sv.val

        @blk.sync
        def _(sync, tt=tt, jj=jj):
            sync.wait_ge(sv.h, ln2n_ms[tt])
            ln2w[tt] = sd_lnw.issue(sync.dma_start(out=ln2_tm[tt * 128:(tt + 1) * 128, :], in_=lnt2[jj][:]), key=tt)

    p2_end_t = st.val
    p2_end_v = sv.val
    p2_end_s = ss.val

    if debug:
        @blk.gpsimd
        def _(g):
            sd_x2.wait_all(g)
            sd_lnw.wait_all(g)
            sd_dbg.inc(g.dma_start(out=dbg["d_x2"][:, :], in_=x2_scr[:, :]), 16)

    free("o_sb", "wd_sb", "brep", "xt20", "xt21", "x2t0", "x2t1", "lnt20", "lnt21",
         "stats2", "mv2", "negmean2", "lnvar2", "rstd2")
    psfree("psD20", "psD21")

    if stage == 2:
        @blk.sync
        def _(sync):
            sd_x2.wait_all(sync)
            sd_lnw.wait_all(sync)
            sd_out.inc(sync.dma_start(out=out_own[:, :], in_=x2_scr[:, :]), 16)
            sync.wait_ge(sd_out.h, sd_out.val)
            if debug:
                sync.wait_ge(sd_dbg.h, sd_dbg.val)
        blk.flush()
        return nc

    warena = alloc("warena", [128, 8 * 4096], BF16)
    ln2f = alloc("ln2f", [128, 16 * 512], BF16)
    x13w = alloc("x13w", [128, 4 * 8 * 512], BF16)
    accb = alloc("accb", [128, 4 * C], F32)
    xt3 = alloc("xt3", [128, C], F32)
    s1_sb = [alloc(f"s1sb{j}", [128, 512], BF16) for j in range(2)]
    tmpb = [alloc(f"tmpb{j}", [128, 512], BF16) for j in range(2)]
    wrep = alloc("wrep", [128, 512], BF16)
    sgrep = alloc("sgrep", [128, 512], BF16)
    gsf = alloc("gsf", [8, 512], F32)
    wtm = alloc("wtm", [128, 8], F32)
    wembf = alloc("wembf", [8, 512], BF16)
    m1n = alloc("m1n", [128, 1], F32)
    m1p = alloc("m1p", [128, 1], F32)
    m2p = alloc("m2p", [128, 1], F32)
    eqb = alloc("eqb", [128, 8], F32)
    sm2 = alloc("sm2", [128, 8], F32)
    expsb = alloc("expsb", [128, 8], F32)
    ge2 = alloc("ge2", [128, 8], F32)
    wun = alloc("wun", [128, 8], F32)
    d2b = alloc("d2b", [128, 1], F32)
    rd2b = alloc("rd2b", [128, 1], F32)
    sgf = alloc("sgf", [1, 512], F32)
    sgs = alloc("sgs", [1, 512], F32)
    sgbf = alloc("sgbf", [1, 512], BF16)

    pA = psalloc("pA", [128, C])
    pB = psalloc("pB", [128, C])
    px1 = [pA[:, 0:512], pA[:, 512:1024]]
    px3 = [pA[:, 1024:1536], pA[:, 1536:2048]]
    pso2 = [pA, pB]

    def wslot(q):
        return warena[:, (q % 8) * 4096:(q % 8 + 1) * 4096]

    def gate_src(e, qh):
        if e < NE:
            return wg_scr[e * C:(e + 1) * C, qh * 256:(qh + 1) * 256]
        sp = e - NE
        return shg_scr[:, sp * H_FF + qh * 256:sp * H_FF + (qh + 1) * 256]

    def up_src(e, qh):
        if e < NE:
            return wu_scr[e * C:(e + 1) * C, qh * 256:(qh + 1) * 256]
        sp = e - NE
        return shu_scr[:, sp * H_FF + qh * 256:sp * H_FF + (qh + 1) * 256]

    def down_src(e, half):
        if e < NE:
            return wd_scr[e * H_FF + half * 512:e * H_FF + (half + 1) * 512, :]
        sp = e - NE
        return shd_scr[sp * H_FF + half * 512:sp * H_FF + (half + 1) * 512, :]

    groups = [[0, 1, 2, 3], [4, 5, 6, 7], [8, 9]]

    slot_user = {}
    wrep_user = [0]
    px1_free = {}
    px3_free = {}
    s1_free = {}
    tmp_free = {}
    pso_free = {}
    x13_free_st = [0]
    nq = [0]
    f3 = {}

    for tw in range(4):
        tw0 = tw * 512

        @blk.sync
        def _(sync, tw=tw, tw0=tw0):
            sd_x2.wait_all(sync)
            sd_lnw.wait_all(sync)
            if tw == 0:
                sync.wait_ge(st.h, p2_end_t)
                sync.wait_ge(sd_pc.h, pc_all)
            else:
                sync.wait_ge(st.h, f3[("lnlast", tw - 1)])
            for cc in range(16):
                sd_fl.issue(sync.dma_start(out=ln2f[:, cc * 512:(cc + 1) * 512],
                                           in_=ln2_tm[tw0:tw0 + 512, cc * 128:(cc + 1) * 128], transpose=True))
        f3[("lnload", tw)] = sd_fl.n - 1

        @blk.tensor
        def _(t, tw=tw):
            sd_fl.wait(t, f3[("lnload", tw)])
            if tw == 0:
                t.wait_ge(sv.h, p2_end_v)
                t.wait_ge(ss.h, p2_end_s)
                t.wait_ge(sd_initg.h, sd_initg.val)
            else:
                t.wait_ge(sv.h, f3[("accdone", tw - 1)])
                t.wait_ge(sv.h, f3[("final", tw - 1, 3)])
            for cc in range(16):
                MM(t, px1[0][0:8, :], gwT_sb[:, cc * 8:(cc + 1) * 8], ln2f[:, cc * 512:(cc + 1) * 512],
                         start=(cc == 0), stop=(cc == 15))
            f3[("gs", tw)] = st.inc(_lm[0])

        @blk.vector
        def _(v, tw=tw):
            v.wait_ge(st.h, f3[("gs", tw)])
            sv.inc(v.tensor_copy(gsf[:], px1[0][0:8, :]))
            f3[("gsf", tw)] = sv.val

        @blk.tensor
        def _(t, tw=tw):
            t.wait_ge(sv.h, f3[("gsf", tw)])
            for i in range(4):
                TR(t, px3[0][:, i * 8:(i + 1) * 8], gsf[0:8, i * 128:(i + 1) * 128], identf_sb[0:8, 0:8])
            f3[("gt", tw)] = st.inc(_lm[0])

        for i in range(4):
            @blk.vector
            def _(v, tw=tw, i=i):
                if i == 0:
                    v.wait_ge(st.h, f3[("gt", tw)])
                sp = px3[0][:, i * 8:(i + 1) * 8]
                v.reduce_max(m1n[:], sp, axis=AX, negate=True)
                sv.inc(v.reduce_max(m1p[:], sp, axis=AX))
                v.wait_ge(sv.h, sv.val)
                sv.inc(v.tensor_scalar(eqb[:], sp, m1p[:, 0:1], None, ALU.is_equal))
                v.wait_ge(sv.h, sv.val)
                sv.inc(v.scalar_tensor_tensor(sm2[:], eqb[:], -1e30, sp, ALU.mult, ALU.add))
                v.wait_ge(sv.h, sv.val)
                sv.inc(v.reduce_max(m2p[:], sm2[:], axis=AX))
                f3[("g1", tw, i)] = sv.val

            @blk.scalar
            def _(s, tw=tw, i=i):
                s.wait_ge(sv.h, f3[("g1", tw, i)])
                ss.inc(s.activation(expsb[:], px3[0][:, i * 8:(i + 1) * 8], AF.Exp, bias=m1n[:, 0:1]))
                f3[("gexp", tw, i)] = ss.val

            @blk.vector
            def _(v, tw=tw, i=i):
                v.wait_ge(ss.h, f3[("gexp", tw, i)])
                sv.inc(v.tensor_scalar(ge2[:], px3[0][:, i * 8:(i + 1) * 8], m2p[:, 0:1], None, ALU.is_ge))
                v.wait_ge(sv.h, sv.val)
                sv.inc(v.tensor_tensor(wun[:], expsb[:], ge2[:], ALU.mult))
                v.wait_ge(sv.h, sv.val)
                sv.inc(v.reduce_sum(d2b[:], wun[:], axis=AX))
                v.wait_ge(sv.h, sv.val)
                sv.inc(v.reciprocal(rd2b[:], d2b[:]))
                v.wait_ge(sv.h, sv.val)
                sv.inc(v.tensor_scalar(wtm[:], wun[:], rd2b[:, 0:1], None, ALU.mult))
                f3[("wtm", tw, i)] = sv.val

            @blk.tensor
            def _(t, tw=tw, i=i):
                t.wait_ge(sv.h, f3[("wtm", tw, i)])
                TR(t, px1[1][0:8, i * 128:(i + 1) * 128], wtm[:], identf_sb[:])
                f3[("wt", tw, i)] = st.inc(_lm[0])

            @blk.vector
            def _(v, tw=tw, i=i):
                v.wait_ge(st.h, f3[("wt", tw, i)])
                sv.inc(v.tensor_copy(wembf[0:8, i * 128:(i + 1) * 128], px1[1][0:8, i * 128:(i + 1) * 128]))
                f3[("wem", tw, i)] = sv.val

        @blk.tensor
        def _(t, tw=tw):
            t.wait_ge(sv.h, f3[("gsf", tw)])
            for cc in range(16):
                MM(t, px1[0][0:1, :], seT_sb[:, cc:cc + 1], ln2f[:, cc * 512:(cc + 1) * 512],
                         start=(cc == 0), stop=(cc == 15))
            f3[("sg", tw)] = st.inc(_lm[0])

        @blk.vector
        def _(v, tw=tw):
            v.wait_ge(st.h, f3[("sg", tw)])
            sv.inc(v.tensor_copy(sgf[:], px1[0][0:1, :]))
            f3[("sgf", tw)] = sv.val

        @blk.scalar
        def _(s, tw=tw):
            s.wait_ge(sv.h, f3[("sgf", tw)])
            ss.inc(s.activation(sgs[:], sgf[:], AF.Sigmoid))
            f3[("sgs", tw)] = ss.val

        @blk.vector
        def _(v, tw=tw):
            v.wait_ge(ss.h, f3[("sgs", tw)])
            sv.inc(v.tensor_copy(sgbf[:], sgs[:]))
            f3[("sgbf", tw)] = sv.val

        @blk.tensor
        def _(t, tw=tw):
            t.wait_ge(sv.h, f3[("sgbf", tw)])
            MM(t, px3[1][:, :], ones1_sb[:], sgbf[:], start=True, stop=True)
            f3[("sgrep_t", tw)] = st.inc(_lm[0])

        @blk.vector
        def _(v, tw=tw):
            v.wait_ge(st.h, f3[("sgrep_t", tw)])
            if ("x13", tw - 1, 9, 7) in f3:
                v.wait_ge(sg.h, f3[("x13", tw - 1, 9, 7)])
            sv.inc(v.tensor_copy(sgrep[:], px3[1][:, :]))
            f3[("sgrep", tw)] = sv.val
        px3_free[1] = f3[("sgrep", tw)]
        px1_free[0] = None
        px1_free[1] = None
        px3_free[0] = f3[("wtm", tw, 3)]
        first_of_tw = [True]

        for gi, grp in enumerate(groups):
            for ei, e in enumerate(grp):
                if e < NE:
                    @blk.tensor
                    def _(t, tw=tw, e=e):
                        for i in range(4):
                            t.wait_ge(sv.h, f3[("wem", tw, i)])
                        if px3_free[1]:
                            t.wait_ge(sv.h, px3_free[1])
                        MM(t, px3[1][:, :], onehot_sb[:, e * 128:(e + 1) * 128], wembf[:],
                                 start=True, stop=True)
                        f3[("wrep_t", tw, e)] = st.inc(_lm[0])

                    @blk.vector
                    def _(v, tw=tw, e=e):
                        v.wait_ge(st.h, f3[("wrep_t", tw, e)])
                        if wrep_user[0]:
                            v.wait_ge(sg.h, wrep_user[0])
                        sv.inc(v.tensor_copy(wrep[:], px3[1][:, :]))
                        f3[("wrep_v", tw, e)] = sv.val
                    px3_free[1] = f3[("wrep_v", tw, e)]

                for qh in range(4):
                    qg = nq[0]
                    nq[0] += 2
                    sgs_ = qg % 8
                    sus_ = (qg + 1) % 8

                    @blk.sync
                    def _(sync, tw=tw, e=e, qh=qh, sgs_=sgs_, sus_=sus_):
                        if tw == 0 and qh == 0 and stage == 3:
                            if e < NE:
                                sd_pcs.wait(sync, pc_e_mark[e])
                            else:
                                sync.wait_ge(sd_pcd.h, 64)
                        for sl in (sgs_, sus_):
                            pu = slot_user.get(sl)
                            if pu:
                                sync.wait_ge(st.h, pu)
                        sd_fw.issue(sync.dma_start(
                            out=wslot(sgs_).rearrange("p (cc f) -> p cc f", f=256),
                            in_=gate_src(e, qh).rearrange("(cc p) f -> p cc f", p=128)), key=sgs_)
                        sd_fw.issue(sync.dma_start(
                            out=wslot(sus_).rearrange("p (cc f) -> p cc f", f=256),
                            in_=up_src(e, qh).rearrange("(cc p) f -> p cc f", p=128)), key=sus_)
                    ld_i = sd_fw.n - 1

                    for hh in range(2):
                        hp = 2 * qh + hh
                        jp = hp % 2

                        @blk.tensor
                        def _(t, tw=tw, e=e, hp=hp, jp=jp, hh=hh, ld_i=ld_i, sgs_=sgs_):
                            sd_fw.wait(t, ld_i)
                            sd_fw.wait(t, ld_i - 1)
                            if first_of_tw[0]:
                                t.wait_ge(sv.h, f3[("sgrep", tw)])
                                t.wait_ge(ss.h, f3[("sgs", tw)])
                                first_of_tw[0] = False
                            if px1_free.get(jp):
                                t.wait_ge(ss.h, px1_free[jp])
                            if px3_free.get(jp):
                                t.wait_ge(sv.h, px3_free[jp])
                            for cc in range(16):
                                MM(t, px1[jp][:, :],
                                         wslot(sgs_)[:, cc * 256 + hh * 128:cc * 256 + (hh + 1) * 128],
                                         ln2f[:, cc * 512:(cc + 1) * 512], start=(cc == 0), stop=(cc == 15))
                            f3[("x1", tw, e, hp)] = st.inc(_lm[0])
                        slot_user[sgs_] = f3[("x1", tw, e, hp)]

                        @blk.scalar
                        def _(s, tw=tw, e=e, hp=hp, jp=jp):
                            s.wait_ge(st.h, f3[("x1", tw, e, hp)])
                            if s1_free.get(jp):
                                s.wait_ge(sv.h, s1_free[jp])
                            ss.inc(s.activation(s1_sb[jp][:], px1[jp][:], AF.Silu))
                            f3[("silu", tw, e, hp)] = ss.val
                        px1_free[jp] = f3[("silu", tw, e, hp)]

                        @blk.tensor
                        def _(t, tw=tw, e=e, hp=hp, jp=jp, hh=hh, sus_=sus_):
                            if px3_free.get(jp):
                                t.wait_ge(sv.h, px3_free[jp])
                            for cc in range(16):
                                MM(t, px3[jp][:, :],
                                         wslot(sus_)[:, cc * 256 + hh * 128:cc * 256 + (hh + 1) * 128],
                                         ln2f[:, cc * 512:(cc + 1) * 512], start=(cc == 0), stop=(cc == 15))
                            f3[("x3", tw, e, hp)] = st.inc(_lm[0])
                        slot_user[sus_] = f3[("x3", tw, e, hp)]

                        @blk.vector
                        def _(v, tw=tw, e=e, hp=hp, jp=jp):
                            v.wait_ge(st.h, f3[("x3", tw, e, hp)])
                            v.wait_ge(ss.h, f3[("silu", tw, e, hp)])
                            if tmp_free.get(jp):
                                v.wait_ge(sg.h, tmp_free[jp])
                            sv.inc(v.tensor_tensor(tmpb[jp][:], px3[jp][:], s1_sb[jp][:], ALU.mult))
                            f3[("tmp", tw, e, hp)] = sv.val
                        px3_free[jp] = f3[("tmp", tw, e, hp)]
                        s1_free[jp] = f3[("tmp", tw, e, hp)]

                        @blk.gpsimd
                        def _(g, tw=tw, e=e, ei=ei, hp=hp, jp=jp):
                            g.wait_ge(sv.h, f3[("tmp", tw, e, hp)])
                            if x13_free_st[0]:
                                g.wait_ge(st.h, x13_free_st[0])
                            if e < NE:
                                g.wait_ge(sv.h, f3[("wrep_v", tw, e)])
                                mulsrc = wrep
                            else:
                                g.wait_ge(sv.h, f3[("sgrep", tw)])
                                mulsrc = sgrep
                            sg.inc(g.tensor_tensor(x13w[:, (ei * 8 + hp) * 512:(ei * 8 + hp + 1) * 512],
                                                   tmpb[jp][:], mulsrc[:], ALU.mult))
                            f3[("x13", tw, e, hp)] = sg.val
                        tmp_free[jp] = f3[("x13", tw, e, hp)]
                        if hp == 7 and e < NE:
                            wrep_user[0] = f3[("x13", tw, e, 7)]

            def emit_wd_load(ei2, e2):
                half_b2 = (ei2 % 2) * 4

                @blk.sync
                def _(sync, tw=tw, e2=e2, half_b2=half_b2):
                    if tw == 0 and stage == 3:
                        if e2 < NE:
                            sd_pcs.wait(sync, pc_e_mark[e2])
                        else:
                            sync.wait_ge(sd_pcd.h, 64)
                    for sl in range(half_b2, half_b2 + 4):
                        pu = slot_user.get(sl)
                        if pu:
                            sync.wait_ge(st.h, pu)
                    for half in range(2):
                        sd_fw.issue(sync.dma_start(
                            out=warena[:, half_b2 * 4096 + half * 8192: half_b2 * 4096 + (half + 1) * 8192]
                                .rearrange("p (hc f) -> p hc f", f=C),
                            in_=down_src(e2, half).rearrange("(hc p) f -> p hc f", p=128)), key=half_b2)
                f3[("wdld", tw, e2)] = sd_fw.n - 1

            for ei in range(min(2, len(grp))):
                emit_wd_load(ei, grp[ei])

            for ei, e in enumerate(grp):
                half_base = (ei % 2) * 4
                for tt in range(4):
                    jj = tt % 2

                    @blk.tensor
                    def _(t, tw=tw, e=e, ei=ei, gi=gi, tt=tt, jj=jj, half_base=half_base):
                        sd_fw.wait(t, f3[("wdld", tw, e)])
                        t.wait_ge(sg.h, f3[("x13", tw, e, 7)])
                        if pso_free.get(jj):
                            t.wait_ge(sv.h, pso_free[jj])
                        if ei == 0 and tt == 0:
                            for jpx in (0, 1):
                                if px1_free.get(jpx):
                                    t.wait_ge(ss.h, px1_free[jpx])
                                if px3_free.get(jpx):
                                    t.wait_ge(sv.h, px3_free[jpx])
                        for hc in range(8):
                            for n4 in range(4):
                                MM(t, pso2[jj][:, n4 * 512:(n4 + 1) * 512],
                                         x13w[:, (ei * 8 + hc) * 512 + tt * 128:(ei * 8 + hc) * 512 + (tt + 1) * 128],
                                         warena[:, half_base * 4096 + hc * C + n4 * 512:
                                                half_base * 4096 + hc * C + (n4 + 1) * 512],
                                         start=(hc == 0), stop=(hc == 7))
                        f3[("dmm", tw, gi, ei, tt)] = st.inc(_lm[0])
                        if tt == 3:
                            for sl in range(half_base, half_base + 4):
                                slot_user[sl] = f3[("dmm", tw, gi, ei, tt)]

                    @blk.vector
                    def _(v, tw=tw, gi=gi, ei=ei, tt=tt, jj=jj):
                        v.wait_ge(st.h, f3[("dmm", tw, gi, ei, tt)])
                        if gi == 0 and ei == 0:
                            sv.inc(v.tensor_copy(accb[:, tt * C:(tt + 1) * C], pso2[jj][:]))
                        else:
                            sv.inc(v.tensor_tensor(accb[:, tt * C:(tt + 1) * C],
                                                   accb[:, tt * C:(tt + 1) * C], pso2[jj][:], ALU.add))
                        f3[("accadd", tw, gi, ei, tt)] = sv.val
                    pso_free[jj] = f3[("accadd", tw, gi, ei, tt)]

                if ei + 2 < len(grp):
                    emit_wd_load(ei + 2, grp[ei + 2])

            x13_free_st[0] = f3[("dmm", tw, gi, len(grp) - 1, 3)]
            px1_free[0] = None
            px1_free[1] = None
            if ("accadd", tw, gi, len(grp) - 1, 2) in f3:
                pa_guard = f3[("accadd", tw, gi, len(grp) - 1, 2)]
            else:
                pa_guard = f3[("accadd", tw, gi, len(grp) - 1, 3)]
            px3_free[0] = pa_guard
            px3_free[1] = pa_guard
            px1_guard = pa_guard

            f3[("pAfree", tw, gi)] = pa_guard


        for tt in range(4):
            @blk.sync
            def _(sync, tw=tw, tt=tt, tw0=tw0):
                if ("outdma", tw, tt - 1) in f3:
                    sync.wait_ge(sd_out.h, f3[("outdma", tw, tt - 1)])
                elif ("outdma", tw - 1, 3) in f3:
                    sync.wait_ge(sd_out.h, f3[("outdma", tw - 1, 3)])
                f3[("x3ld", tw, tt)] = sd_fx.issue(
                    sync.dma_start(out=xt3[:], in_=x2_scr[tw0 + tt * 128:tw0 + (tt + 1) * 128, :]))

            @blk.vector
            def _(v, tw=tw, tt=tt):
                last_gi = len(groups) - 1
                last_ei = len(groups[last_gi]) - 1
                v.wait_ge(sv.h, f3[("accadd", tw, last_gi, last_ei, tt)])
                sd_fx.wait(v, f3[("x3ld", tw, tt)])
                sv.inc(v.tensor_tensor(xt3[:], xt3[:], accb[:, tt * C:(tt + 1) * C], ALU.add))
                f3[("final", tw, tt)] = sv.val

            @blk.sync
            def _(sync, tw=tw, tt=tt, tw0=tw0):
                sync.wait_ge(sv.h, f3[("final", tw, tt)])
                sd_out.inc(sync.dma_start(out=out_own[tw0 + tt * 128:tw0 + (tt + 1) * 128, :], in_=xt3[:]), 16)
                f3[("outdma", tw, tt)] = sd_out.val

        f3[("accdone", tw)] = f3[("accadd", tw, len(groups) - 1, len(groups[-1]) - 1, 3)]
        f3[("lnlast", tw)] = f3[("x3", tw, 9, 7)]

    @blk.sync
    def _(sync):
        sync.wait_ge(sd_out.h, sd_out.val)
        if debug:
            sync.wait_ge(sd_dbg.h, sd_dbg.val)

    blk.flush()
    return nc



def _host_arrays(inputs, c):
    b, s = c // 4, c % 4
    x = inputs["L0_attn_hidden_states"]
    rows = ((np.arange(NT) * 4 + s)[:, None] * 128 + np.arange(128)[None, :]).ravel()
    d = {}
    d["x_own"] = np.ascontiguousarray(x[b][rows]).astype(np.float32, copy=False)

    inv = (1.0 / (10000.0 ** (np.arange(0, ROT, 2, dtype=np.float64) / ROT))).astype(np.float64)
    emb = rows[:, None].astype(np.float64) * inv[None, :]
    embf = np.concatenate([emb, emb], axis=1)
    d["cos_own"] = np.ascontiguousarray(np.cos(embf).T).astype(ml_dtypes.bfloat16)
    d["sin_own"] = np.ascontiguousarray(np.sin(embf).T).astype(ml_dtypes.bfloat16)

    tri = np.triu(np.ones((128, 128), dtype=np.float32))
    mk = np.zeros((8, 128, 1024), dtype=np.float32)
    for pp in range(8):
        for half in range(2):
            dd = 2 * pp + half
            for j in range(4):
                grel = 4 * j + s
                blkm = mk[pp][:, half * 512 + j * 128: half * 512 + (j + 1) * 128]
                if dd < grel:
                    blkm[:] = 1.0
                elif dd == grel:
                    blkm[:] = tri
    d["maskp"] = mk.reshape(8, 128 * 1024).astype(ml_dtypes.bfloat16)
    return d


def _shared_arrays(inputs):
    f32 = np.float32
    d = {}
    d["wqkv"] = np.ascontiguousarray(inputs["L0_attn_W_qkv"], dtype=f32)
    d["bqkv_col"] = np.ascontiguousarray(np.asarray(inputs["L0_attn_b_qkv"], dtype=f32).reshape(48, 128).T)
    d["wdense"] = np.ascontiguousarray(inputs["L0_attn_W_dense"], dtype=f32)
    d["bd_row"] = np.ascontiguousarray(np.asarray(inputs["L0_attn_b_dense"], dtype=f32)[None, :])
    d["gwT"] = np.ascontiguousarray(np.asarray(inputs["L0_ffn_gate_w"], dtype=f32).T)
    d["wgT"] = np.ascontiguousarray(np.asarray(inputs["L0_ffn_w_gate"], dtype=f32).transpose(0, 2, 1)).reshape(NE * C, H_FF)
    d["wuT"] = np.ascontiguousarray(np.asarray(inputs["L0_ffn_w_up"], dtype=f32).transpose(0, 2, 1)).reshape(NE * C, H_FF)
    d["wdT"] = np.ascontiguousarray(np.asarray(inputs["L0_ffn_w_down"], dtype=f32).transpose(0, 2, 1)).reshape(NE * H_FF, C)
    d["shgT"] = np.ascontiguousarray(np.asarray(inputs["L0_ffn_sh_gate"], dtype=f32).T)
    d["shuT"] = np.ascontiguousarray(np.asarray(inputs["L0_ffn_sh_up"], dtype=f32).T)
    d["shdT"] = np.ascontiguousarray(np.asarray(inputs["L0_ffn_sh_down"], dtype=f32).T)
    d["seT_col"] = np.ascontiguousarray(np.asarray(inputs["L0_ffn_se_gate_w"], dtype=f32).reshape(16, 128).T)
    d["ident128"] = np.eye(128, dtype=ml_dtypes.bfloat16)
    d["identf32"] = np.eye(128, dtype=f32)
    d["ones1"] = np.ones((1, 128), dtype=ml_dtypes.bfloat16)
    d["ones128"] = np.ones((128, 128), dtype=ml_dtypes.bfloat16)
    oh = np.zeros((8, 8 * 128), dtype=np.float32)
    for e in range(8):
        oh[e, e * 128:(e + 1) * 128] = 1.0
    d["onehot8"] = oh.astype(ml_dtypes.bfloat16)
    R = np.zeros((ROT, ROT), dtype=np.float32)
    for m in range(16):
        R[m, m + 16] = -1.0
        R[m + 16, m] = 1.0
    d["rpermT"] = np.ascontiguousarray(R.T).astype(ml_dtypes.bfloat16)
    d["cm8"] = np.full((128, 1), -8.0, dtype=f32)
    return d


_NC_CACHE = {}


def _get_nc(debug=False):
    key = bool(debug)
    if key not in _NC_CACHE:
        _NC_CACHE[key] = build_nc(debug=key)
    return _NC_CACHE[key]


def kernel(**inputs):
    nc = _get_nc(debug=False)
    shared = _shared_arrays(inputs)
    in_maps = []
    for c in range(NCORE):
        m = dict(shared)
        m.update(_host_arrays(inputs, c))
        in_maps.append(m)
    res = run_bass_kernel_spmd(nc, in_maps, core_ids=list(range(NCORE))).results

    x = inputs["L0_attn_hidden_states"]
    out = np.empty((2, S, C), dtype=np.float32)
    for c in range(NCORE):
        b, s = c // 4, c % 4
        rows = ((np.arange(NT) * 4 + s)[:, None] * 128 + np.arange(128)[None, :]).ravel()
        out[b][rows] = res[c]["out_own"]
    return out


# version 12
# speedup vs baseline: 1.0961x; 1.0709x over previous
import math

import numpy as np
import ml_dtypes

import concourse.bass as bass
import concourse.mybir as mybir
from concourse.bass_utils import run_bass_kernel_spmd
from concourse.alu_op_type import AluOpType as ALU

F32 = mybir.dt.float32
BF16 = mybir.dt.bfloat16
AF = mybir.ActivationFunctionType
AX = mybir.AxisListType.X

C = 2048
S = 8192
NH = 16
HD = 128
ROT = 32
T_OWN = 2048
NT = 16
H_FF = 1024
NE = 8
HS = 2048
SCALE = 1.0 / math.sqrt(HD)
NCORE = 8




class _FakeIns:
    __slots__ = ("incs",)

    def __init__(self):
        self.incs = []

    def then_inc(self, sem, val=None):
        self.incs.append((sem, val))
        return self


class _Rec:

    def __init__(self, ops):
        self._ops = ops

    def __getattr__(self, m):
        ops = self._ops

        def call(*a, **kw):
            fi = _FakeIns()
            ops.append((m, a, kw, fi))
            return fi
        return call


class DeferredBlock:
    ENGINES = ("sync", "tensor", "vector", "scalar", "gpsimd")

    def __init__(self, nc):
        self.nc = nc
        self.ops = {e: [] for e in self.ENGINES}

    def _mk(self, engname):
        rec = _Rec(self.ops[engname])

        def deco(fn):
            fn(rec)
        return deco

    def __getattr__(self, engname):
        if engname in self.ENGINES:
            return self._mk(engname)
        raise AttributeError(engname)

    def flush(self):
        nc = self.nc
        with nc.Block() as blk:
            for engname in self.ENGINES:
                ops = self.ops[engname]
                if not ops:
                    continue

                def emit(eng, ops=ops):
                    for m, a, kw, fi in ops:
                        ins = getattr(eng, m)(*a, **kw)
                        for sem, val in fi.incs:
                            if val is None:
                                ins.then_inc(sem)
                            else:
                                ins.then_inc(sem, val)
                getattr(blk, engname)(emit)
        for e in self.ENGINES:
            self.ops[e] = []


class Sem:
    def __init__(self, nc, name):
        self.h = nc.semaphore(name).__enter__()
        self.val = 0

    def inc(self, ins, by=1):
        ins.then_inc(self.h, by)
        self.val += by
        return self.val


class DmaStream:

    def __init__(self, nc, name, k=2):
        self.sems = [Sem(nc, f"{name}{j}") for j in range(k)]
        self.k = k
        self.n = 0
        self.marks = []

    def issue(self, ins, key=None):
        kj = (self.n if key is None else key) % self.k
        s = self.sems[kj]
        s.inc(ins, 16)
        self.marks.append((kj, s.val))
        self.n += 1
        return self.n - 1

    def wait(self, eng, i):
        j, v = self.marks[i]
        eng.wait_ge(self.sems[j].h, v)

    def wait_upto(self, eng, i):
        last = {}
        for j, v in self.marks[:i + 1]:
            last[j] = v
        for j, v in last.items():
            eng.wait_ge(self.sems[j].h, v)

    def wait_all(self, eng):
        for s in self.sems:
            if s.val:
                eng.wait_ge(s.h, s.val)

    def guard(self, eng, key):
        s = self.sems[key % self.k]
        if s.val:
            eng.wait_ge(s.h, s.val)


def build_nc(debug=False, stage=3):
    nc = bass.Bass("TRN2", target_bir_lowering=False)
    _lm = [None]

    def MM(t, *a, **kw):
        _lm[0] = t.matmul(*a, **kw)
        return _lm[0]

    def TR(t, *a, **kw):
        _lm[0] = t.transpose(*a, **kw)
        return _lm[0]

    def din(name, shape, dt=F32):
        return nc.dram_tensor(name, shape, dt, kind="ExternalInput").ap()

    x_own = din("x_own", [T_OWN, C])
    wqkv = din("wqkv", [C, 3 * C])
    bqkv_col = din("bqkv_col", [128, 48])
    wdense = din("wdense", [C, C])
    bd_row = din("bd_row", [1, C])
    gwT = din("gwT", [C, NE])
    wgT = din("wgT", [NE * C, H_FF])
    wuT = din("wuT", [NE * C, H_FF])
    wdT = din("wdT", [NE * H_FF, C])
    shgT = din("shgT", [C, HS])
    shuT = din("shuT", [C, HS])
    shdT = din("shdT", [HS, C])
    seT_col = din("seT_col", [128, 16])
    cos_own = din("cos_own", [ROT, T_OWN], BF16)
    sin_own = din("sin_own", [ROT, T_OWN], BF16)
    maskp = din("maskp", [8, 128 * 1024], BF16)
    ident128 = din("ident128", [128, 128], BF16)
    identf32 = din("identf32", [128, 128], F32)
    ones1 = din("ones1", [1, 128], BF16)
    ones128 = din("ones128", [128, 128], BF16)
    onehot8 = din("onehot8", [8, 8 * 128], BF16)
    rpermT = din("rpermT", [ROT, ROT], BF16)
    cm8 = din("cm8", [128, 1])

    out_own = nc.dram_tensor("out_own", [T_OWN, C], F32, kind="ExternalOutput").ap()

    def dscr(name, shape, dt):
        return nc.dram_tensor(name, shape, dt).ap()

    wqkv_bf = dscr("wqkv_bf", [C, 3 * C], BF16)
    wdense_bf = dscr("wdense_bf", [C, C], BF16)
    ln1_tm = dscr("ln1_tm", [T_OWN, C], BF16)
    cc_ink = dscr("cc_ink", [C, T_OWN], BF16)
    cc_invT = dscr("cc_invT", [T_OWN, C], BF16)
    q_scr = dscr("q_scr", [C, T_OWN], BF16)
    gk = [dscr(f"gk{g}", [4 * 512, T_OWN], BF16) for g in range(4)]
    gvT = [dscr(f"gvT{t}", [4 * 512, C], BF16) for t in range(4)]
    o_scr = dscr("o_scr", [C, T_OWN], BF16)
    x2_scr = dscr("x2_scr", [T_OWN, C], F32)
    ln2_tm = dscr("ln2_tm", [T_OWN, C], BF16)
    wg_scr = dscr("wg_scr", [NE * C, H_FF], BF16)
    wu_scr = dscr("wu_scr", [NE * C, H_FF], BF16)
    wd_scr = dscr("wd_scr", [NE * H_FF, C], BF16)
    shg_scr = dscr("shg_scr", [C, HS], BF16)
    shu_scr = dscr("shu_scr", [C, HS], BF16)
    shd_scr = dscr("shd_scr", [HS, C], BF16)

    dbg = {}
    if debug:
        for nm, shp in [("d_ln1", [T_OWN, C]), ("d_k", [C, T_OWN]), ("d_v", [C, T_OWN]),
                        ("d_q", [C, T_OWN]), ("d_o", [C, T_OWN]), ("d_x2", [T_OWN, C])]:
            dbg[nm] = nc.dram_tensor(nm, shp, F32, kind="ExternalOutput").ap()

    live = {}
    pslive = {}

    def alloc(name, shape, dt):
        t = nc.sbuf_tensor(name, shape, dt)
        h = t.__enter__()
        live[name] = t
        return h

    def free(*names):
        order = list(live)
        for n in sorted(names, key=order.index, reverse=True):
            live.pop(n).__exit__(None, None, None)

    def psalloc(name, shape):
        t = nc.psum_tensor(name, shape, F32)
        h = t.__enter__()
        pslive[name] = t
        return h

    def psfree(*names):
        order = list(pslive)
        for n in sorted(names, key=order.index, reverse=True):
            pslive.pop(n).__exit__(None, None, None)

    ident_sb = alloc("ident_sb", [128, 128], BF16)
    identf_sb = alloc("identf_sb", [128, 128], F32)
    ones1_sb = alloc("ones1_sb", [1, 128], BF16)
    ones128_sb = alloc("ones128_sb", [128, 128], BF16)
    onehot_sb = alloc("onehot_sb", [8, 8 * 128], BF16)
    rperm_sb = alloc("rperm_sb", [ROT, ROT], BF16)
    bqkv_sb = alloc("bqkv_sb", [128, 48], F32)
    cm8_sb = alloc("cm8_sb", [128, 1], F32)
    gwT_sb = alloc("gwT_sb", [128, 16 * NE], BF16)
    seT_sb = alloc("seT_sb", [128, 16], BF16)
    bd_sb = alloc("bd_sb", [1, C], BF16)

    sd_init = Sem(nc, "init")
    sd_initg = Sem(nc, "initg")
    sd_init2 = Sem(nc, "init2")
    sd_pcq = Sem(nc, "pcq")
    sd_pcd = Sem(nc, "pcd")
    sd_pc = Sem(nc, "pcw")
    sd_x = DmaStream(nc, "sdx", 2)
    sd_ln1w = DmaStream(nc, "ln1w", 2)
    sd_ln1r = DmaStream(nc, "ln1r", 2)
    sd_w = DmaStream(nc, "sdw", 2)
    sd_kv = DmaStream(nc, "kvw", 2)
    sd_vw = DmaStream(nc, "vww", 2)
    cck = [Sem(nc, f"cck{g}") for g in range(4)]
    ccv = [Sem(nc, f"ccv{g}") for g in range(4)]
    sv = Sem(nc, "sv")
    ss = Sem(nc, "ss")
    st = Sem(nc, "st")
    sg = Sem(nc, "sg")
    sd_kh = DmaStream(nc, "kh", 4)
    sd_vh = DmaStream(nc, "vh", 3)
    sd_o = DmaStream(nc, "ow", 2)
    sd_d = DmaStream(nc, "dd", 2)
    sd_x2 = DmaStream(nc, "x2w", 2)
    sd_lnw = DmaStream(nc, "lnw", 2)
    sd_fw = DmaStream(nc, "ffw", 8)
    sd_pcs = DmaStream(nc, "pcs", 8)
    sd_fl = DmaStream(nc, "ffl", 1)
    sd_fx = DmaStream(nc, "ffx", 1)
    sd_out = Sem(nc, "outw")
    sd_dbg = Sem(nc, "dbgw")

    blk = DeferredBlock(nc)

    cosb = alloc("cosb", [ROT, T_OWN], BF16)
    sinb = alloc("sinb", [ROT, T_OWN], BF16)
    ln1_fm = alloc("ln1_fm", [128, 16 * T_OWN], BF16)
    xt = [alloc(f"xt{j}", [128, C], F32) for j in range(2)]
    lnt = [alloc(f"lnt{j}", [128, C], BF16) for j in range(2)]
    stats = alloc("stats", [128, 48], F32)
    mv = alloc("mv", [128, 32], F32)
    negmean = alloc("negmean", [128, 16], F32)
    lnvar = alloc("lnvar", [128, 16], F32)
    rstd = alloc("rstd", [128, 16], F32)

    @blk.sync
    def _(sync):
        for dst, src in [(ident_sb, ident128), (identf_sb, identf32), (ones1_sb, ones1),
                         (ones128_sb, ones128), (onehot_sb, onehot8), (rperm_sb, rpermT),
                         (bqkv_sb, bqkv_col), (cm8_sb, cm8), (cosb, cos_own), (sinb, sin_own)]:
            sd_init.inc(sync.dma_start(out=dst[:], in_=src[:, :]), 16)
        sd_x.issue(sync.dma_start(out=xt[0][:], in_=x_own[0:128, :]))
        sd_x.issue(sync.dma_start(out=xt[1][:], in_=x_own[128:256, :]))

    @blk.gpsimd
    def _(g):
        sd_initg.inc(g.dma_start(out=gwT_sb[:].rearrange("p (cc e) -> p cc e", e=NE),
                                 in_=gwT[:, :].rearrange("(cc p) e -> p cc e", p=128)), 16)
        sd_initg.inc(g.dma_start(out=seT_sb[:], in_=seT_col[:, :]), 16)
        sd_initg.inc(g.dma_start(out=bd_sb[:], in_=bd_row[:, :]), 16)
        sd_pcq.inc(g.dma_start(out=wqkv_bf[:, :], in_=wqkv[:, :]), 16)
    pc_all = None

    stat_ms = [0] * NT
    for i in range(NT):
        @blk.vector
        def _(v, i=i):
            sd_x.wait(v, i)
            if i >= 2:
                v.wait_ge(sv.h, stat_ms[i - 2])
            sh = (i % 2) * 24
            for q in range(4):
                ins = v.bn_stats(stats[:, sh + q * 6:sh + (q + 1) * 6], xt[i % 2][:, q * 512:(q + 1) * 512])
            sv.inc(ins)
            v.wait_ge(sv.h, sv.val)
            sv.inc(v.bn_aggr(mv[:, 2 * i:2 * i + 2], stats[:, sh:sh + 24]))
        stat_ms[i] = sv.val

        @blk.sync
        def _(sync, i=i):
            if i + 2 < NT:
                sync.wait_ge(sv.h, stat_ms[i])
                sd_x.issue(sync.dma_start(out=xt[i % 2][:], in_=x_own[(i + 2) * 128:(i + 3) * 128, :]))

    @blk.vector
    def _(v):
        v.wait_ge(sv.h, stat_ms[NT - 1])
        mview = mv[:].rearrange("p (i two) -> p i two", two=2)
        v.tensor_scalar(negmean[:], mview[:, :, 0], -1.0, None, ALU.mult)
        sv.inc(v.tensor_scalar(lnvar[:], mview[:, :, 1], 1e-5, None, ALU.add))
    ms_lnstats = sv.val

    @blk.scalar
    def _(s):
        s.wait_ge(sv.h, ms_lnstats)
        ss.inc(s.activation(rstd[:], lnvar[:], AF.Ln))
    ms_ln = ss.val

    @blk.vector
    def _(v):
        v.wait_ge(ss.h, ms_ln)
        sv.inc(v.tensor_scalar(rstd[:], rstd[:], -0.5, None, ALU.mult))
    ms_half = sv.val

    @blk.scalar
    def _(s):
        s.wait_ge(sv.h, ms_half)
        ss.inc(s.activation(rstd[:], rstd[:], AF.Exp))
    ms_rstd = ss.val

    norm_ms = [0] * NT
    x2nd = [0] * NT
    for i in range(NT):
        @blk.sync
        def _(sync, i=i):
            if i < 2:
                sync.wait_ge(sv.h, stat_ms[NT - 1])
            else:
                sync.wait_ge(sv.h, norm_ms[i - 2])
            x2nd[i] = sd_x.issue(sync.dma_start(out=xt[i % 2][:], in_=x_own[i * 128:(i + 1) * 128, :]))

        @blk.vector
        def _(v, i=i):
            sd_x.wait(v, x2nd[i])
            if i == 0:
                v.wait_ge(ss.h, ms_rstd)
            if i >= 2:
                sd_ln1w.wait(v, i - 2)
            sv.inc(v.tensor_scalar(lnt[i % 2][:], xt[i % 2][:], negmean[:, i:i + 1], rstd[:, i:i + 1], ALU.add, ALU.mult))
        norm_ms[i] = sv.val

        @blk.sync
        def _(sync, i=i):
            sync.wait_ge(sv.h, norm_ms[i])
            sd_ln1w.issue(sync.dma_start(out=ln1_tm[i * 128:(i + 1) * 128, :], in_=lnt[i % 2][:]), key=i)

    @blk.sync
    def _(sync):
        sd_ln1w.wait_all(sync)
        for cc in range(16):
            sd_ln1r.issue(sync.dma_start(out=ln1_fm[:, cc * T_OWN:(cc + 1) * T_OWN],
                                         in_=ln1_tm[:, cc * 128:(cc + 1) * 128], transpose=True))

    free("xt0", "xt1", "lnt0", "lnt1", "stats", "mv", "negmean", "lnvar", "rstd")

    wsl = [alloc(f"wsl{j}", [128, 16 * 384], BF16) for j in range(2)]
    k_sb = [alloc(f"ksb{j}", [128, C], BF16) for j in range(2)]
    q_sb = [alloc(f"qsb{j}", [128, C], BF16) for j in range(2)]
    v_sbf = alloc("v_sbf", [128, C], F32)
    v_sbT = [alloc(f"vsbT{j}", [128, C], BF16) for j in range(2)]
    rt1 = alloc("rt1", [ROT, C], BF16)
    rt2 = alloc("rt2", [ROT, C], BF16)

    psA = psalloc("psA", [128, C])
    psB = psalloc("psB", [128, C])

    n_heads_emit = {10: 0, 11: 1, 12: 2, 13: 1, 14: 1, 15: 1, 16: 1}.get(stage, NH)
    head_parts = {13: ("q",), 14: ("q", "k"), 15: ("q", "k", "rope"), 16: ("q",)}.get(stage, ("q", "k", "rope", "v", "dma"))
    q_mm_single = (stage == 16)

    wslv = [alloc(f"wslv{j}", [128, 16 * 128], BF16) for j in range(2)]
    vq_mm = {}
    vq_ev = {}
    vq_tr = {}
    vq_te = {}
    vq_dma = {}
    if "v" in head_parts:
        @blk.sync
        def _(sync):
            sync.wait_ge(sd_pcq.h, 16)
            for h in (0, 1):
                sd_w.issue(sync.dma_start(
                    out=wslv[h % 2][:].rearrange("p (cc d) -> p cc d", d=128),
                    in_=wqkv_bf[:, h * 384 + 256:h * 384 + 384].rearrange("(cc p) d -> p cc d", p=128)), key=h)

        for tq in range(4):
            for h in range(NH):
                jv = h % 2
                pv_, po_ = (psA, psB) if h % 2 == 0 else (psB, psA)

                @blk.tensor
                def _(t, tq=tq, h=h, jv=jv, pv_=pv_):
                    if tq == 0 and h == 0:
                        t.wait_ge(sd_init.h, sd_init.val)
                        t.wait_ge(sd_initg.h, sd_initg.val)
                        sd_ln1r.wait_all(t)
                    sd_w.wait(t, 16 * tq + h)
                    k2 = 16 * tq + h
                    if k2 >= 2:
                        pe = vq_te.get(k2 - 2)
                        if pe:
                            t.wait_ge(ss.h, pe)
                    for cc in range(16):
                        MM(t, pv_[:, 0:512], wslv[jv][:, cc * 128:(cc + 1) * 128],
                           ln1_fm[:, cc * T_OWN + tq * 512:cc * T_OWN + tq * 512 + 512],
                           start=(cc == 0), stop=(cc == 15))
                    vq_mm[16 * tq + h] = st.inc(_lm[0])

                @blk.scalar
                def _(s, tq=tq, h=h, jv=jv, pv_=pv_):
                    k2 = 16 * tq + h
                    s.wait_ge(st.h, vq_mm[k2])
                    if k2 >= 2:
                        pe = vq_tr.get(k2 - 2)
                        if pe:
                            s.wait_ge(st.h, pe)
                    ss.inc(s.activation(v_sbf[:, jv * 512:(jv + 1) * 512], pv_[:, 0:512], AF.Identity,
                                        bias=bqkv_sb[:, 3 * h + 2:3 * h + 3]))
                    vq_ev[k2] = ss.val

                @blk.tensor
                def _(t, tq=tq, h=h, jv=jv, pv_=pv_):
                    k2 = 16 * tq + h
                    t.wait_ge(ss.h, vq_ev[k2])
                    if k2 >= 2:
                        pe = vq_te.get(k2 - 2)
                        if pe:
                            t.wait_ge(ss.h, pe)
                    for tl in range(4):
                        TR(t, pv_[:, 512 + tl * 128:512 + (tl + 1) * 128],
                           v_sbf[:, jv * 512 + tl * 128:jv * 512 + (tl + 1) * 128], identf_sb[:])
                    vq_tr[k2] = st.inc(_lm[0])

                @blk.scalar
                def _(s, tq=tq, h=h, jv=jv, pv_=pv_):
                    k2 = 16 * tq + h
                    s.wait_ge(st.h, vq_tr[k2])
                    if k2 >= 2:
                        pd = vq_dma.get(k2 - 2)
                        if pd is not None:
                            sd_vw.wait(s, pd)
                    ss.inc(s.activation(v_sbT[jv][:, 0:512], pv_[:, 512:1024], AF.Identity))
                    vq_te[k2] = ss.val

                @blk.sync
                def _(sync, tq=tq, h=h, jv=jv):
                    k2 = 16 * tq + h
                    sync.wait_ge(ss.h, vq_te[k2])
                    sd_vw.guard(sync, k2)
                    vq_dma[k2] = sd_vw.issue(sync.dma_start(
                        out=cc_invT[tq * 512:(tq + 1) * 512, h * 128:(h + 1) * 128]
                            .rearrange("(tl p) d -> p tl d", p=128),
                        in_=v_sbT[jv][:, 0:512].rearrange("p (tl d) -> p tl d", d=128)), key=k2)
                    k3 = k2 + 2
                    if k3 < 64:
                        h3 = k3 % 16
                        sync.wait_ge(st.h, vq_mm[k2])
                        sd_w.issue(sync.dma_start(
                            out=wslv[k3 % 2][:].rearrange("p (cc d) -> p cc d", d=128),
                            in_=wqkv_bf[:, h3 * 384 + 256:h3 * 384 + 384].rearrange("(cc p) d -> p cc d", p=128)), key=k3)

            if stage not in (-1, 10, 11, 12):
                @blk.gpsimd
                def _(g, tq=tq):
                    sd_vw.wait_upto(g, vq_dma[16 * tq + 15])
                    g.collective_compute(
                        "AllGather", ALU.bypass,
                        replica_groups=[[0, 1, 2, 3], [4, 5, 6, 7]],
                        ins=[cc_invT[tq * 512:(tq + 1) * 512, :]],
                        outs=[gvT[tq][:, :]],
                    ).then_inc(ccv[tq].h)
    n_w_loads = sd_w.n


    q_ready = [0] * NH
    q_evac = [0] * NH
    q_dma = [0] * NH
    vT_ready = [0] * NH
    vT_evac = [0] * NH
    vT_dma = [0] * NH
    k_ready = [0] * NH
    k_evac = [0] * NH
    rq_ready = [0] * NH
    rk_ready = [0] * NH
    qrot = [0] * NH
    krot = [0] * NH
    v_ready = [0] * NH
    v_evac = [0] * NH
    kv_after = [0] * NH

    @blk.sync
    def _(sync):
        sync.wait_ge(sd_pcq.h, 16)
        for h in (0, 1):
            sd_w.issue(sync.dma_start(
                out=wsl[h % 2][:].rearrange("p (cc d) -> p cc d", d=384),
                in_=wqkv_bf[:, h * 384:(h + 1) * 384].rearrange("(cc p) d -> p cc d", p=128)), key=64 + h)

    def wslice(j, cc, which):
        return wsl[j][:, cc * 384 + which * 128: cc * 384 + (which + 1) * 128]

    for h in range(n_heads_emit):
        j = h % 2
        psQ, psK = (psA, psB) if h % 2 == 0 else (psB, psA)

        @blk.tensor
        def _(t, h=h, j=j, psQ=psQ, psK=psK):
            if h == 0:
                t.wait_ge(sd_init.h, sd_init.val)
                t.wait_ge(sd_initg.h, sd_initg.val)
                sd_ln1r.wait_all(t)
                if "v" in head_parts:
                    t.wait_ge(ss.h, vq_te[63])
            sd_w.wait(t, n_w_loads + h)
            if h >= 1:
                t.wait_ge(sv.h, krot[h - 1])
            for n4 in range(4):
                for cc in range(16):
                    MM(t, psQ[:, n4 * 512:(n4 + 1) * 512], wslice(j, cc, 0),
                             ln1_fm[:, cc * T_OWN + n4 * 512:cc * T_OWN + (n4 + 1) * 512],
                             start=(cc == 0), stop=(cc == 15))
            q_ready[h] = st.inc(_lm[0])
            if "k" not in head_parts:
                return
            if h >= 1:
                t.wait_ge(sv.h, qrot[h - 1])
            for n4 in range(4):
                for cc in range(16):
                    MM(t, psK[:, n4 * 512:(n4 + 1) * 512], wslice(j, cc, 1),
                             ln1_fm[:, cc * T_OWN + n4 * 512:cc * T_OWN + (n4 + 1) * 512],
                             start=(cc == 0), stop=(cc == 15))
            k_ready[h] = st.inc(_lm[0])

        @blk.vector
        def _(v, h=h, j=j, psQ=psQ):
            v.wait_ge(st.h, q_ready[h])
            if h >= 2:
                sd_kv.wait(v, kv_after[h - 2])
            sv.inc(v.tensor_scalar(q_sb[j][:], psQ[:], bqkv_sb[:, 3 * h:3 * h + 1], SCALE, ALU.add, ALU.mult))
            q_evac[h] = sv.val

        @blk.scalar
        def _(s, h=h, j=j, psK=psK):
            if "k" not in head_parts:
                return
            s.wait_ge(st.h, k_ready[h])
            if h >= 2:
                sd_kv.wait(s, kv_after[h - 2])
            ss.inc(s.activation(k_sb[j][:], psK[:], AF.Identity, bias=bqkv_sb[:, 3 * h + 1:3 * h + 2]))
            k_evac[h] = ss.val

        @blk.tensor
        def _(t, h=h, j=j, psQ=psQ, psK=psK):
            if "rope" not in head_parts:
                return
            t.wait_ge(sv.h, q_evac[h])
            for n4 in range(4):
                MM(t, psQ[0:ROT, n4 * 512:(n4 + 1) * 512], rperm_sb[:],
                         q_sb[j][0:ROT, n4 * 512:(n4 + 1) * 512], start=True, stop=True)
            rq_ready[h] = st.inc(_lm[0])
            t.wait_ge(ss.h, k_evac[h])
            for n4 in range(4):
                MM(t, psK[0:ROT, n4 * 512:(n4 + 1) * 512], rperm_sb[:],
                         k_sb[j][0:ROT, n4 * 512:(n4 + 1) * 512], start=True, stop=True)
            rk_ready[h] = st.inc(_lm[0])

        @blk.vector
        def _(v, h=h, j=j, psQ=psQ, psK=psK):
            if "rope" not in head_parts:
                return
            v.wait_ge(st.h, rq_ready[h])
            if h >= 1:
                v.wait_ge(sv.h, krot[h - 1])
            v.tensor_tensor(rt1[:], q_sb[j][0:ROT, :], cosb[:], ALU.mult)
            sv.inc(v.tensor_tensor(rt2[:], psQ[0:ROT, :], sinb[:], ALU.mult))
            v.wait_ge(sv.h, sv.val)
            sv.inc(v.tensor_tensor(q_sb[j][0:ROT, :], rt1[:], rt2[:], ALU.add))
            qrot[h] = sv.val
            v.wait_ge(st.h, rk_ready[h])
            v.wait_ge(sv.h, qrot[h])
            v.tensor_tensor(rt1[:], k_sb[j][0:ROT, :], cosb[:], ALU.mult)
            sv.inc(v.tensor_tensor(rt2[:], psK[0:ROT, :], sinb[:], ALU.mult))
            v.wait_ge(sv.h, sv.val)
            sv.inc(v.tensor_tensor(k_sb[j][0:ROT, :], rt1[:], rt2[:], ALU.add))
            krot[h] = sv.val


        @blk.sync
        def _(sync, h=h, j=j):
            if "dma" not in head_parts:
                return
            sync.wait_ge(sv.h, krot[h])
            sd_kv.guard(sync, h)
            sd_kv.issue(sync.dma_start(out=cc_ink[h * 128:(h + 1) * 128, :], in_=k_sb[j][:]), key=h)
            sync.wait_ge(sv.h, qrot[h])
            q_dma[h] = sd_kv.issue(sync.dma_start(out=q_scr[h * 128:(h + 1) * 128, :], in_=q_sb[j][:]), key=h)
            kv_after[h] = sd_kv.n - 1
            if h + 2 < NH:
                sync.wait_ge(st.h, k_ready[h] if "k" in head_parts else q_ready[h])
                sd_w.issue(sync.dma_start(
                    out=wsl[j][:].rearrange("p (cc d) -> p cc d", d=384),
                    in_=wqkv_bf[:, (h + 2) * 384:(h + 3) * 384].rearrange("(cc p) d -> p cc d", p=128)), key=64 + h)

        if h % 4 == 3 and stage not in (-1, 10, 11, 12):
            g4 = h // 4

            @blk.gpsimd
            def _(g, g4=g4, h=h):
                sd_kv.wait_upto(g, sd_kv.n - 1)
                g.collective_compute(
                    "AllGather", ALU.bypass,
                    replica_groups=[[0, 1, 2, 3], [4, 5, 6, 7]],
                    ins=[cc_ink[g4 * 512:(g4 + 1) * 512, :]],
                    outs=[gk[g4][:, :]],
                ).then_inc(cck[g4].h)

    p0_end_t = st.val
    p0_end_s = ss.val
    p0_end_v = sv.val
    @blk.gpsimd
    def _(g):
        if stage not in (-1, 10, 11, 12):
            g.wait_ge((ccv if stage == 3 else cck)[3].h, 1)
        sd_pcd.inc(g.dma_start(out=wdense_bf[:, :], in_=wdense[:, :]), 16)
        if stage == 3:
            return
        for e in range(NE):
            sd_pc.inc(g.dma_start(out=wg_scr[e * C:(e + 1) * C, :], in_=wgT[e * C:(e + 1) * C, :]), 16)
            sd_pc.inc(g.dma_start(out=wu_scr[e * C:(e + 1) * C, :], in_=wuT[e * C:(e + 1) * C, :]), 16)
            sd_pc.inc(g.dma_start(out=wd_scr[e * H_FF:(e + 1) * H_FF, :], in_=wdT[e * H_FF:(e + 1) * H_FF, :]), 16)
        sd_pc.inc(g.dma_start(out=shg_scr[:, :], in_=shgT[:, :]), 16)
        sd_pc.inc(g.dma_start(out=shu_scr[:, :], in_=shuT[:, :]), 16)
        sd_pc.inc(g.dma_start(out=shd_scr[:, :], in_=shdT[:, :]), 16)
    pc_all = sd_pc.val


    if debug and stage not in (13, 14, 15, 16):
        @blk.gpsimd
        def _(g):
            sd_ln1w.wait_all(g)
            sd_dbg.inc(g.dma_start(out=dbg["d_ln1"][:, :], in_=ln1_tm[:, :]), 16)
            sd_kv.wait_all(g)
            sd_vw.wait_all(g)
            sd_dbg.inc(g.dma_start(out=dbg["d_k"][:, :], in_=cc_ink[:, :]), 16)
            sd_dbg.inc(g.dma_start(out=dbg["d_v"][:, :], in_=cc_invT[:, :]), 16)
            sd_dbg.inc(g.dma_start(out=dbg["d_q"][:, :], in_=q_scr[:, :]), 16)

    free("cosb", "sinb", "ln1_fm", "wsl0", "wsl1", "ksb0", "ksb1", "qsb0", "qsb1",
         "v_sbf", "vsbT0", "vsbT1", "rt1", "rt2", "wslv0", "wslv1")
    psfree("psA", "psB")

    if stage in (0, -1, 10, 11, 12, 13, 14, 15, 16):
        @blk.gpsimd
        def _(g):
            sd_ln1w.wait_all(g)
            sd_kv.wait_all(g)
            if stage in (13, 14, 15, 16):
                g.wait_ge(sv.h, q_evac[0])
            if stage in (14, 15):
                g.wait_ge(ss.h, k_evac[0])
            if stage == 15:
                g.wait_ge(sv.h, krot[0])
            if stage == 10:
                g.wait_ge(sd_w.sems[0].h, sd_w.sems[0].val)
                g.wait_ge(sd_w.sems[1].h, sd_w.sems[1].val)
                for sY in sd_ln1r.sems:
                    g.wait_ge(sY.h, sY.val)
            if stage == 0:
                for g4 in range(4):
                    g.wait_ge(cck[g4].h, 1)
                    g.wait_ge(ccv[g4].h, 1)
            for hh in range(NH):
                g.wait_ge(sv.h, qrot[hh])
            sd_out.inc(g.dma_start(out=out_own[:, :], in_=ln1_tm[:, :]), 16)
            g.wait_ge(sd_out.h, sd_out.val)
            if debug:
                g.wait_ge(sd_dbg.h, sd_dbg.val)
        blk.flush()
        return nc

    masks_sb = alloc("masks_sb", [128, 8 * 1024], BF16)
    qp_buf = [alloc(f"qp{j}", [128, T_OWN], BF16) for j in range(4)]
    k_hbuf = [alloc(f"khb{j}", [128, 4 * T_OWN], BF16) for j in range(4)]
    vt_buf = [alloc(f"vtb{j}", [128, 64 * 128], BF16) for j in range(3)]
    pT = [alloc(f"pT{j}", [128, 1024], BF16) for j in range(4)]
    rD2 = [alloc(f"rD{j}", [128, 512], F32) for j in range(2)]
    o_ev = [alloc(f"oev{j}", [128, 512], BF16) for j in range(2)]

    @blk.sync
    def _(sync):
        sd_init.inc(sync.dma_start(out=masks_sb[:].rearrange("p (m f) -> p m f", f=1024),
                                   in_=maskp[:, :].rearrange("m (p f) -> p m f", f=1024)), 16)
    init_all = sd_init.val

    psS = [psalloc(f"psS{j}", [128, 1024]) for j in range(2)]
    psO2 = [psalloc(f"psO{j}", [128, 512]) for j in range(2)]
    psDa = [psalloc(f"psDa{j}", [128, 512]) for j in range(2)]

    def kcol(kt):
        return (kt % 4) * T_OWN + (kt // 4) * 128

    kq_load = [0] * NH
    vt_load = {}
    av_last = [0] * NH
    s_ready = {}
    exp_ms = {}
    mask_ms = {}
    dadd_ms = {}
    av_done = {}
    o_ev_done = {}
    o_dma_mark = {}
    rd_done = {}
    rd_prev = [0, 0]

    def issue_kq(h):
        j3 = h % 4
        g4 = h // 4

        @blk.sync
        def _(sync, h=h, j3=j3, g4=g4):
            sync.wait_ge(cck[g4].h, 1)
            if h < 4:
                sync.wait_ge(st.h, p0_end_t)
                sync.wait_ge(ss.h, p0_end_s)
                sync.wait_ge(sv.h, p0_end_v)
                sd_kv.wait_all(sync)
                sd_vw.wait_all(sync)
                sd_w.wait_all(sync)
            else:
                sync.wait_ge(st.h, av_last[h - 4])
            sd_kh.guard(sync, h)
            sd_kh.issue(sync.dma_start(out=qp_buf[j3][:], in_=q_scr[h * 128:(h + 1) * 128, :]), key=h)
            hl = h % 4
            for r in range(4):
                sd_kh.issue(sync.dma_start(out=k_hbuf[j3][:, r * T_OWN:(r + 1) * T_OWN],
                                           in_=gk[g4][r * 512 + hl * 128:r * 512 + (hl + 1) * 128, :]), key=h)
            kq_load[h] = sd_kh.n - 1

    def issue_vt(h, w):
        j3 = h % 3

        @blk.sync
        def _(sync, h=h, w=w, j3=j3):
            sync.wait_ge(ccv[w].h, 1)
            if h >= 3:
                sync.wait_ge(st.h, av_last[h - 3])
            sd_vh.guard(sync, h % 3)
            for r in range(4):
                base = (16 * w + r) * 128
                dst = vt_buf[j3][:].rearrange("p (kt d) -> p kt d", d=128)[:, 16 * w + r:16 * w + r + 13:4, :]
                srcv = gvT[w][r * 512:(r + 1) * 512, h * 128:(h + 1) * 128] \
                    .rearrange("(jj p) d -> p jj d", p=128)
                sd_vh.issue(sync.dma_start(out=dst, in_=srcv), key=h % 3)
            vt_load[(h, w)] = sd_vh.n - 1

    issue_kq(0)
    issue_kq(1)
    issue_kq(2)
    issue_kq(3)

    for m in range(NH // 2):
        hA, hB = 2 * m, 2 * m + 1
        jA, jB = hA % 4, hB % 4
        vA, vB = hA % 3, hB % 3

        issue_vt(hA, 0)
        issue_vt(hB, 0)

        @blk.tensor
        def _(t, hA=hA, hB=hB):
            sd_kh.wait(t, kq_load[hA])
            sd_kh.wait(t, kq_load[hB])
            if hA == 0:
                t.wait_ge(sd_init.h, init_all)
                t.wait_ge(ss.h, p0_end_s)
                t.wait_ge(sv.h, p0_end_v)

        for w in range(4):
            npair = 8 * (w + 1)

            def emit_S(t, head, pair, w=w):
                jj = head % 4
                hp = head % 2
                if pair >= 1:
                    pk = exp_ms.get((head, w, pair - 1))
                elif w >= 1:
                    pk = exp_ms.get((head, w - 1, 8 * w - 1))
                else:
                    pk = exp_ms.get((head - 2, 3, 31))
                if pk:
                    t.wait_ge(ss.h, pk)
                qs = w * 512
                for half in range(2):
                    kt = 2 * pair + half
                    MM(t, psS[hp][:, half * 512:(half + 1) * 512],
                       k_hbuf[jj][:, kcol(kt):kcol(kt) + 128],
                       qp_buf[jj][:, qs:qs + 512], start=True, stop=True)
                s_ready[(head, w, pair)] = st.inc(_lm[0])

            @blk.tensor
            def _(t, hA=hA, hB=hB, w=w):
                sd_vh.wait(t, vt_load[(hA, w)])
                sd_vh.wait(t, vt_load[(hB, w)])
                emit_S(t, hA, 0)
                emit_S(t, hB, 0)

            for p in range(npair):
                for head in (hA, hB):
                    hp = head % 2

                    @blk.scalar
                    def _(s, head=head, hp=hp, w=w, p=p):
                        s.wait_ge(st.h, s_ready[(head, w, p)])
                        pprev = []
                        for back in (2,):
                            q_ = p - back
                            if q_ >= 0:
                                pprev.append((head, w, q_))
                            elif w >= 1:
                                pprev.append((head, w - 1, 8 * w + q_))
                            else:
                                pprev.append((head - 2, 3, 32 + q_))
                        for key in pprev:
                            pa = av_done.get(key)
                            if pa:
                                s.wait_ge(st.h, pa)
                        ss.inc(s.activation(pT[2 * hp + p % 2][:], psS[hp][:], AF.Exp))
                        exp_ms[(head, w, p)] = ss.val

                if p + 1 < npair:
                    @blk.tensor
                    def _(t, hA=hA, hB=hB, p=p):
                        emit_S(t, hA, p + 1)
                        emit_S(t, hB, p + 1)

                for head in (hA, hB):
                    hp = head % 2
                    if p >= 8 * w:
                        @blk.vector
                        def _(v, head=head, hp=hp, w=w, p=p):
                            v.wait_ge(ss.h, exp_ms[(head, w, p)])
                            mi = p - 8 * w
                            pb = pT[2 * hp + p % 2]
                            sv.inc(v.tensor_tensor(pb[:], pb[:],
                                                   masks_sb[:, mi * 1024:(mi + 1) * 1024], ALU.mult))
                            mask_ms[(head, w, p)] = sv.val

                for head in (hA, hB):
                    hp = head % 2
                    j3 = head % 3

                    @blk.tensor
                    def _(t, head=head, hp=hp, j3=j3, w=w, p=p, npair=npair):
                        if (head, w, p) in mask_ms:
                            t.wait_ge(sv.h, mask_ms[(head, w, p)])
                        else:
                            t.wait_ge(ss.h, exp_ms[(head, w, p)])
                        if p == 0:
                            pv = o_ev_done.get((head, w - 1), o_ev_done.get((head - 2, 3)))
                            if pv:
                                t.wait_ge(sv.h, pv)
                            if rd_prev[hp]:
                                t.wait_ge(sv.h, rd_prev[hp])
                        for half in range(2):
                            kt = 2 * p + half
                            MM(t, psO2[hp][:], vt_buf[j3][:, kt * 128:(kt + 1) * 128],
                               pT[2 * hp + p % 2][:, half * 512:(half + 1) * 512],
                               start=(p == 0 and half == 0), stop=(p == npair - 1 and half == 1))
                        for half in range(2):
                            MM(t, psDa[hp][:], ones128_sb[:],
                               pT[2 * hp + p % 2][:, half * 512:(half + 1) * 512],
                               start=(p == 0 and half == 0), stop=(p == npair - 1 and half == 1))
                        av_done[(head, w, p)] = st.inc(_lm[0])

            if w + 1 < 4:
                issue_vt(hA, w + 1)
                issue_vt(hB, w + 1)

            for head in (hA, hB):
                hp = head % 2

                @blk.vector
                def _(v, head=head, hp=hp, w=w, npair=npair):
                    v.wait_ge(st.h, av_done[(head, w, npair - 1)])
                    sv.inc(v.reciprocal(rD2[hp][:], psDa[hp][:]))
                    rd_done[(head, w)] = sv.val
                    rd_prev[hp] = sv.val
                    v.wait_ge(sv.h, sv.val)
                    pm = o_dma_mark.get((head, w - 1), o_dma_mark.get((head - 2, 3)))
                    if pm is not None:
                        sd_o.wait(v, pm)
                    sv.inc(v.tensor_tensor(o_ev[hp][:], psO2[hp][:], rD2[hp][:], ALU.mult))
                    o_ev_done[(head, w)] = sv.val

                @blk.sync
                def _(sync, head=head, hp=hp, w=w):
                    sync.wait_ge(sv.h, o_ev_done[(head, w)])
                    sd_o.issue(sync.dma_start(out=o_scr[head * 128:(head + 1) * 128, w * 512:(w + 1) * 512],
                                              in_=o_ev[hp][:]), key=hp)
                    o_dma_mark[(head, w)] = sd_o.n - 1

        av_last[hA] = av_done[(hA, 3, 8 * 4 - 1)]
        av_last[hB] = av_done[(hB, 3, 8 * 4 - 1)]
        if 2 * m + 4 < NH:
            issue_kq(2 * m + 4)
        if 2 * m + 5 < NH:
            issue_kq(2 * m + 5)

    p1_end_t = st.val
    p1_end_v = sv.val
    p1_end_s = ss.val

    if debug:
        @blk.gpsimd
        def _(g):
            sd_o.wait_all(g)
            sd_dbg.inc(g.dma_start(out=dbg["d_o"][:, :], in_=o_scr[:, :]), 16)

    free("masks_sb", "qp0", "qp1", "qp2", "qp3", "khb0", "khb1", "khb2", "khb3",
         "vtb0", "vtb1", "vtb2", "pT0", "pT1", "pT2", "pT3", "rD0", "rD1", "oev0", "oev1")
    psfree("psS0", "psS1", "psO0", "psO1", "psDa0", "psDa1")

    pc_e_mark = {}
    if stage == 3:
        pc_gate = [(0, 1), (2, 1), (4, 1), (6, 1), (8, 1), (10, 1), (12, 1), (12, 2)]
        for e in range(NE):
            @blk.gpsimd
            def _(g, e=e):
                g.wait_ge(sv.h, o_ev_done[pc_gate[e]])
                sd_pcs.issue(g.dma_start(out=wg_scr[e * C:(e + 1) * C, :], in_=wgT[e * C:(e + 1) * C, :]), key=e)
                sd_pcs.issue(g.dma_start(out=wu_scr[e * C:(e + 1) * C, :], in_=wuT[e * C:(e + 1) * C, :]), key=e)
                sd_pcs.issue(g.dma_start(out=wd_scr[e * H_FF:(e + 1) * H_FF, :], in_=wdT[e * H_FF:(e + 1) * H_FF, :]), key=e)
            pc_e_mark[e] = sd_pcs.n - 1

        @blk.gpsimd
        def _(g):
            g.wait_ge(sv.h, o_ev_done[(13, 2)])
            g.wait_ge(sd_pcd.h, 16)
            sd_pcd.inc(g.dma_start(out=shg_scr[:, :], in_=shgT[:, :]), 16)
            sd_pcd.inc(g.dma_start(out=shu_scr[:, :], in_=shuT[:, :]), 16)
            sd_pcd.inc(g.dma_start(out=shd_scr[:, :], in_=shdT[:, :]), 16)

    if stage == 1:
        @blk.gpsimd
        def _(g):
            sd_o.wait_all(g)
            g.wait_ge(st.h, p1_end_t)
            sd_out.inc(g.dma_start(out=out_own[:, :], in_=o_scr[:, :]), 16)
            g.wait_ge(sd_out.h, sd_out.val)
            if debug:
                g.wait_ge(sd_dbg.h, sd_dbg.val)
        blk.flush()
        return nc

    o_sb = alloc("o_sb", [128, 16 * T_OWN], BF16)
    wd_sb = alloc("wd_sb", [128, 16 * C], BF16)
    brep = alloc("brep", [128, C], F32)
    xt2 = [alloc(f"xt2{j}", [128, C], F32) for j in range(2)]
    x2t = [alloc(f"x2t{j}", [128, C], F32) for j in range(2)]
    lnt2 = [alloc(f"lnt2{j}", [128, C], BF16) for j in range(2)]
    stats2 = alloc("stats2", [128, 48], F32)
    mv2 = alloc("mv2", [128, 32], F32)
    negmean2 = alloc("negmean2", [128, 16], F32)
    lnvar2 = alloc("lnvar2", [128, 16], F32)
    rstd2 = alloc("rstd2", [128, 16], F32)

    psD2 = [psalloc(f"psD2{j}", [128, C]) for j in range(2)]

    xt2_idx = {}

    @blk.sync
    def _(sync):
        sd_o.wait_all(sync)
        sync.wait_ge(st.h, p1_end_t)
        sync.wait_ge(sv.h, p1_end_v)
        for cc in range(16):
            sd_d.issue(sync.dma_start(out=o_sb[:, cc * T_OWN:(cc + 1) * T_OWN],
                                      in_=o_scr[cc * 128:(cc + 1) * 128, :]))
        sync.wait_ge(sd_pcd.h, 16)
        sd_d.issue(sync.dma_start(out=wd_sb[:].rearrange("p (cc f) -> p cc f", f=C),
                                  in_=wdense_bf[:, :].rearrange("(cc p) f -> p cc f", p=128)))
        xt2_idx[0] = sd_d.issue(sync.dma_start(out=xt2[0][:], in_=x_own[0:128, :]))
        xt2_idx[1] = sd_d.issue(sync.dma_start(out=xt2[1][:], in_=x_own[128:256, :]))
    dense_last_load = sd_d.n - 1

    ms = {}

    @blk.tensor
    def _(t):
        t.wait_ge(sd_init.h, init_all)
        t.wait_ge(sd_initg.h, sd_initg.val)
        t.wait_ge(ss.h, p1_end_s)
        t.wait_ge(sv.h, p1_end_v)
        for n4 in range(4):
            MM(t, psD2[0][:, n4 * 512:(n4 + 1) * 512], ones1_sb[:], bd_sb[:, n4 * 512:(n4 + 1) * 512],
                     start=True, stop=True)
        ms["brep_t"] = st.inc(_lm[0])

    @blk.vector
    def _(v):
        v.wait_ge(st.h, ms["brep_t"])
        sv.inc(v.tensor_copy(brep[:], psD2[0][:]))
        ms["brep_v"] = sv.val

    dmm_ms = [0] * NT
    x2_ms = [0] * NT
    ln2_ms = [0] * NT
    x2_after = [0] * NT
    ln2w = [0] * NT
    for tt in range(NT):
        jj = tt % 2

        @blk.tensor
        def _(t, tt=tt, jj=jj):
            if tt == 0:
                sd_d.wait_upto(t, dense_last_load)
                t.wait_ge(sv.h, ms["brep_v"])
            if tt >= 2:
                t.wait_ge(sv.h, x2_ms[tt - 2])
            for cc in range(16):
                for n4 in range(4):
                    MM(t, psD2[jj][:, n4 * 512:(n4 + 1) * 512],
                             o_sb[:, cc * T_OWN + tt * 128:cc * T_OWN + (tt + 1) * 128],
                             wd_sb[:, cc * C + n4 * 512:cc * C + (n4 + 1) * 512],
                             start=(cc == 0), stop=(cc == 15))
            dmm_ms[tt] = st.inc(_lm[0])

        @blk.vector
        def _(v, tt=tt, jj=jj):
            v.wait_ge(st.h, dmm_ms[tt])
            if tt >= 2:
                sd_x2.wait(v, x2_after[tt - 2])
            sv.inc(v.tensor_tensor(x2t[jj][:], psD2[jj][:], brep[:], ALU.add))
            v.wait_ge(sv.h, sv.val)
            sd_d.wait(v, xt2_idx[tt])
            sv.inc(v.tensor_tensor(x2t[jj][:], x2t[jj][:], xt2[jj][:], ALU.add))
            v.wait_ge(sv.h, sv.val)
            sh = (tt % 2) * 24
            for q in range(4):
                ins = v.bn_stats(stats2[:, sh + q * 6:sh + (q + 1) * 6], x2t[jj][:, q * 512:(q + 1) * 512])
            sv.inc(ins)
            v.wait_ge(sv.h, sv.val)
            sv.inc(v.bn_aggr(mv2[:, 2 * tt:2 * tt + 2], stats2[:, sh:sh + 24]))
            ln2_ms[tt] = sv.val
            x2_ms[tt] = sv.val

        @blk.sync
        def _(sync, tt=tt, jj=jj):
            sync.wait_ge(sv.h, x2_ms[tt])
            x2_after[tt] = sd_x2.issue(sync.dma_start(out=x2_scr[tt * 128:(tt + 1) * 128, :], in_=x2t[jj][:]), key=tt)
            if tt + 2 < NT:
                xt2_idx[tt + 2] = sd_d.issue(sync.dma_start(out=xt2[jj][:], in_=x_own[(tt + 2) * 128:(tt + 3) * 128, :]))

    @blk.vector
    def _(v):
        v.wait_ge(sv.h, x2_ms[NT - 1])
        mview = mv2[:].rearrange("p (i two) -> p i two", two=2)
        v.tensor_scalar(negmean2[:], mview[:, :, 0], -1.0, None, ALU.mult)
        sv.inc(v.tensor_scalar(lnvar2[:], mview[:, :, 1], 1e-5, None, ALU.add))
    ms2_stats = sv.val

    @blk.scalar
    def _(s):
        s.wait_ge(sv.h, ms2_stats)
        ss.inc(s.activation(rstd2[:], lnvar2[:], AF.Ln))
    ms2_ln = ss.val

    @blk.vector
    def _(v):
        v.wait_ge(ss.h, ms2_ln)
        sv.inc(v.tensor_scalar(rstd2[:], rstd2[:], -0.5, None, ALU.mult))
    ms2_half = sv.val

    @blk.scalar
    def _(s):
        s.wait_ge(sv.h, ms2_half)
        ss.inc(s.activation(rstd2[:], rstd2[:], AF.Exp))
    ms2_rstd = ss.val

    ln2n_ms = [0] * NT
    x2rd = [0] * NT
    for tt in range(NT):
        jj = tt % 2

        @blk.sync
        def _(sync, tt=tt, jj=jj):
            if tt < 2:
                sync.wait_ge(sv.h, x2_ms[NT - 1])
            else:
                sync.wait_ge(sv.h, ln2n_ms[tt - 2])
            sd_x2.wait(sync, x2_after[tt])
            x2rd[tt] = sd_d.issue(sync.dma_start(out=xt2[jj][:], in_=x2_scr[tt * 128:(tt + 1) * 128, :]))

        @blk.vector
        def _(v, tt=tt, jj=jj):
            sd_d.wait(v, x2rd[tt])
            if tt == 0:
                v.wait_ge(ss.h, ms2_rstd)
            if tt >= 2:
                sd_lnw.wait(v, ln2w[tt - 2])
            sv.inc(v.tensor_scalar(lnt2[jj][:], xt2[jj][:], negmean2[:, tt:tt + 1], rstd2[:, tt:tt + 1], ALU.add, ALU.mult))
        ln2n_ms[tt] = sv.val

        @blk.sync
        def _(sync, tt=tt, jj=jj):
            sync.wait_ge(sv.h, ln2n_ms[tt])
            ln2w[tt] = sd_lnw.issue(sync.dma_start(out=ln2_tm[tt * 128:(tt + 1) * 128, :], in_=lnt2[jj][:]), key=tt)

    p2_end_t = st.val
    p2_end_v = sv.val
    p2_end_s = ss.val

    if debug:
        @blk.gpsimd
        def _(g):
            sd_x2.wait_all(g)
            sd_lnw.wait_all(g)
            sd_dbg.inc(g.dma_start(out=dbg["d_x2"][:, :], in_=x2_scr[:, :]), 16)

    free("o_sb", "wd_sb", "brep", "xt20", "xt21", "x2t0", "x2t1", "lnt20", "lnt21",
         "stats2", "mv2", "negmean2", "lnvar2", "rstd2")
    psfree("psD20", "psD21")

    if stage == 2:
        @blk.sync
        def _(sync):
            sd_x2.wait_all(sync)
            sd_lnw.wait_all(sync)
            sd_out.inc(sync.dma_start(out=out_own[:, :], in_=x2_scr[:, :]), 16)
            sync.wait_ge(sd_out.h, sd_out.val)
            if debug:
                sync.wait_ge(sd_dbg.h, sd_dbg.val)
        blk.flush()
        return nc

    warena = alloc("warena", [128, 8 * 4096], BF16)
    ln2f = alloc("ln2f", [128, 16 * 512], BF16)
    x13w = alloc("x13w", [128, 4 * 8 * 512], BF16)
    accb = alloc("accb", [128, 4 * C], F32)
    xt3 = alloc("xt3", [128, C], F32)
    s1_sb = [alloc(f"s1sb{j}", [128, 512], BF16) for j in range(2)]
    tmpb = [alloc(f"tmpb{j}", [128, 512], BF16) for j in range(2)]
    wrep = alloc("wrep", [128, 512], BF16)
    sgrep = alloc("sgrep", [128, 512], BF16)
    gsf = alloc("gsf", [8, 512], F32)
    wtm = alloc("wtm", [128, 8], F32)
    wembf = alloc("wembf", [8, 512], BF16)
    m1n = alloc("m1n", [128, 1], F32)
    m1p = alloc("m1p", [128, 1], F32)
    m2p = alloc("m2p", [128, 1], F32)
    eqb = alloc("eqb", [128, 8], F32)
    sm2 = alloc("sm2", [128, 8], F32)
    expsb = alloc("expsb", [128, 8], F32)
    ge2 = alloc("ge2", [128, 8], F32)
    wun = alloc("wun", [128, 8], F32)
    d2b = alloc("d2b", [128, 1], F32)
    rd2b = alloc("rd2b", [128, 1], F32)
    sgf = alloc("sgf", [1, 512], F32)
    sgs = alloc("sgs", [1, 512], F32)
    sgbf = alloc("sgbf", [1, 512], BF16)

    pA = psalloc("pA", [128, C])
    pB = psalloc("pB", [128, C])
    px1 = [pA[:, 0:512], pA[:, 512:1024]]
    px3 = [pA[:, 1024:1536], pA[:, 1536:2048]]
    pso2 = [pA, pB]

    def wslot(q):
        return warena[:, (q % 8) * 4096:(q % 8 + 1) * 4096]

    def gate_src(e, qh):
        if e < NE:
            return wg_scr[e * C:(e + 1) * C, qh * 256:(qh + 1) * 256]
        sp = e - NE
        return shg_scr[:, sp * H_FF + qh * 256:sp * H_FF + (qh + 1) * 256]

    def up_src(e, qh):
        if e < NE:
            return wu_scr[e * C:(e + 1) * C, qh * 256:(qh + 1) * 256]
        sp = e - NE
        return shu_scr[:, sp * H_FF + qh * 256:sp * H_FF + (qh + 1) * 256]

    def down_src(e, half):
        if e < NE:
            return wd_scr[e * H_FF + half * 512:e * H_FF + (half + 1) * 512, :]
        sp = e - NE
        return shd_scr[sp * H_FF + half * 512:sp * H_FF + (half + 1) * 512, :]

    groups = [[0, 1, 2, 3], [4, 5, 6, 7], [8, 9]]

    slot_user = {}
    wrep_user = [0]
    px1_free = {}
    px3_free = {}
    s1_free = {}
    tmp_free = {}
    pso_free = {}
    x13_free_st = [0]
    nq = [0]
    f3 = {}

    for tw in range(4):
        tw0 = tw * 512

        @blk.sync
        def _(sync, tw=tw, tw0=tw0):
            sd_x2.wait_all(sync)
            sd_lnw.wait_all(sync)
            if tw == 0:
                sync.wait_ge(st.h, p2_end_t)
                sync.wait_ge(sd_pc.h, pc_all)
            else:
                sync.wait_ge(st.h, f3[("lnlast", tw - 1)])
            for cc in range(16):
                sd_fl.issue(sync.dma_start(out=ln2f[:, cc * 512:(cc + 1) * 512],
                                           in_=ln2_tm[tw0:tw0 + 512, cc * 128:(cc + 1) * 128], transpose=True))
        f3[("lnload", tw)] = sd_fl.n - 1

        @blk.tensor
        def _(t, tw=tw):
            sd_fl.wait(t, f3[("lnload", tw)])
            if tw == 0:
                t.wait_ge(sv.h, p2_end_v)
                t.wait_ge(ss.h, p2_end_s)
                t.wait_ge(sd_initg.h, sd_initg.val)
            else:
                t.wait_ge(sv.h, f3[("accdone", tw - 1)])
                t.wait_ge(sv.h, f3[("final", tw - 1, 3)])
            for cc in range(16):
                MM(t, px1[0][0:8, :], gwT_sb[:, cc * 8:(cc + 1) * 8], ln2f[:, cc * 512:(cc + 1) * 512],
                         start=(cc == 0), stop=(cc == 15))
            f3[("gs", tw)] = st.inc(_lm[0])

        @blk.vector
        def _(v, tw=tw):
            v.wait_ge(st.h, f3[("gs", tw)])
            sv.inc(v.tensor_copy(gsf[:], px1[0][0:8, :]))
            f3[("gsf", tw)] = sv.val

        @blk.tensor
        def _(t, tw=tw):
            t.wait_ge(sv.h, f3[("gsf", tw)])
            for i in range(4):
                TR(t, px3[0][:, i * 8:(i + 1) * 8], gsf[0:8, i * 128:(i + 1) * 128], identf_sb[0:8, 0:8])
            f3[("gt", tw)] = st.inc(_lm[0])

        for i in range(4):
            @blk.vector
            def _(v, tw=tw, i=i):
                if i == 0:
                    v.wait_ge(st.h, f3[("gt", tw)])
                sp = px3[0][:, i * 8:(i + 1) * 8]
                v.reduce_max(m1n[:], sp, axis=AX, negate=True)
                sv.inc(v.reduce_max(m1p[:], sp, axis=AX))
                v.wait_ge(sv.h, sv.val)
                sv.inc(v.tensor_scalar(eqb[:], sp, m1p[:, 0:1], None, ALU.is_equal))
                v.wait_ge(sv.h, sv.val)
                sv.inc(v.scalar_tensor_tensor(sm2[:], eqb[:], -1e30, sp, ALU.mult, ALU.add))
                v.wait_ge(sv.h, sv.val)
                sv.inc(v.reduce_max(m2p[:], sm2[:], axis=AX))
                f3[("g1", tw, i)] = sv.val

            @blk.scalar
            def _(s, tw=tw, i=i):
                s.wait_ge(sv.h, f3[("g1", tw, i)])
                ss.inc(s.activation(expsb[:], px3[0][:, i * 8:(i + 1) * 8], AF.Exp, bias=m1n[:, 0:1]))
                f3[("gexp", tw, i)] = ss.val

            @blk.vector
            def _(v, tw=tw, i=i):
                v.wait_ge(ss.h, f3[("gexp", tw, i)])
                sv.inc(v.tensor_scalar(ge2[:], px3[0][:, i * 8:(i + 1) * 8], m2p[:, 0:1], None, ALU.is_ge))
                v.wait_ge(sv.h, sv.val)
                sv.inc(v.tensor_tensor(wun[:], expsb[:], ge2[:], ALU.mult))
                v.wait_ge(sv.h, sv.val)
                sv.inc(v.reduce_sum(d2b[:], wun[:], axis=AX))
                v.wait_ge(sv.h, sv.val)
                sv.inc(v.reciprocal(rd2b[:], d2b[:]))
                v.wait_ge(sv.h, sv.val)
                sv.inc(v.tensor_scalar(wtm[:], wun[:], rd2b[:, 0:1], None, ALU.mult))
                f3[("wtm", tw, i)] = sv.val

            @blk.tensor
            def _(t, tw=tw, i=i):
                t.wait_ge(sv.h, f3[("wtm", tw, i)])
                TR(t, px1[1][0:8, i * 128:(i + 1) * 128], wtm[:], identf_sb[:])
                f3[("wt", tw, i)] = st.inc(_lm[0])

            @blk.vector
            def _(v, tw=tw, i=i):
                v.wait_ge(st.h, f3[("wt", tw, i)])
                sv.inc(v.tensor_copy(wembf[0:8, i * 128:(i + 1) * 128], px1[1][0:8, i * 128:(i + 1) * 128]))
                f3[("wem", tw, i)] = sv.val

        @blk.tensor
        def _(t, tw=tw):
            t.wait_ge(sv.h, f3[("gsf", tw)])
            for cc in range(16):
                MM(t, px1[0][0:1, :], seT_sb[:, cc:cc + 1], ln2f[:, cc * 512:(cc + 1) * 512],
                         start=(cc == 0), stop=(cc == 15))
            f3[("sg", tw)] = st.inc(_lm[0])

        @blk.vector
        def _(v, tw=tw):
            v.wait_ge(st.h, f3[("sg", tw)])
            sv.inc(v.tensor_copy(sgf[:], px1[0][0:1, :]))
            f3[("sgf", tw)] = sv.val

        @blk.scalar
        def _(s, tw=tw):
            s.wait_ge(sv.h, f3[("sgf", tw)])
            ss.inc(s.activation(sgs[:], sgf[:], AF.Sigmoid))
            f3[("sgs", tw)] = ss.val

        @blk.vector
        def _(v, tw=tw):
            v.wait_ge(ss.h, f3[("sgs", tw)])
            sv.inc(v.tensor_copy(sgbf[:], sgs[:]))
            f3[("sgbf", tw)] = sv.val

        @blk.tensor
        def _(t, tw=tw):
            t.wait_ge(sv.h, f3[("sgbf", tw)])
            MM(t, px3[1][:, :], ones1_sb[:], sgbf[:], start=True, stop=True)
            f3[("sgrep_t", tw)] = st.inc(_lm[0])

        @blk.vector
        def _(v, tw=tw):
            v.wait_ge(st.h, f3[("sgrep_t", tw)])
            if ("x13", tw - 1, 9, 7) in f3:
                v.wait_ge(sg.h, f3[("x13", tw - 1, 9, 7)])
            sv.inc(v.tensor_copy(sgrep[:], px3[1][:, :]))
            f3[("sgrep", tw)] = sv.val
        px3_free[1] = f3[("sgrep", tw)]
        px1_free[0] = None
        px1_free[1] = None
        px3_free[0] = f3[("wtm", tw, 3)]
        first_of_tw = [True]

        for gi, grp in enumerate(groups):
            for ei, e in enumerate(grp):
                if e < NE:
                    @blk.tensor
                    def _(t, tw=tw, e=e):
                        for i in range(4):
                            t.wait_ge(sv.h, f3[("wem", tw, i)])
                        if px3_free[1]:
                            t.wait_ge(sv.h, px3_free[1])
                        MM(t, px3[1][:, :], onehot_sb[:, e * 128:(e + 1) * 128], wembf[:],
                                 start=True, stop=True)
                        f3[("wrep_t", tw, e)] = st.inc(_lm[0])

                    @blk.vector
                    def _(v, tw=tw, e=e):
                        v.wait_ge(st.h, f3[("wrep_t", tw, e)])
                        if wrep_user[0]:
                            v.wait_ge(sg.h, wrep_user[0])
                        sv.inc(v.tensor_copy(wrep[:], px3[1][:, :]))
                        f3[("wrep_v", tw, e)] = sv.val
                    px3_free[1] = f3[("wrep_v", tw, e)]

                for qh in range(4):
                    qg = nq[0]
                    nq[0] += 2
                    sgs_ = qg % 8
                    sus_ = (qg + 1) % 8

                    @blk.sync
                    def _(sync, tw=tw, e=e, qh=qh, sgs_=sgs_, sus_=sus_):
                        if tw == 0 and qh == 0 and stage == 3:
                            if e < NE:
                                sd_pcs.wait(sync, pc_e_mark[e])
                            else:
                                sync.wait_ge(sd_pcd.h, 64)
                        for sl in (sgs_, sus_):
                            pu = slot_user.get(sl)
                            if pu:
                                sync.wait_ge(st.h, pu)
                        sd_fw.issue(sync.dma_start(
                            out=wslot(sgs_).rearrange("p (cc f) -> p cc f", f=256),
                            in_=gate_src(e, qh).rearrange("(cc p) f -> p cc f", p=128)), key=sgs_)
                        sd_fw.issue(sync.dma_start(
                            out=wslot(sus_).rearrange("p (cc f) -> p cc f", f=256),
                            in_=up_src(e, qh).rearrange("(cc p) f -> p cc f", p=128)), key=sus_)
                    ld_i = sd_fw.n - 1

                    for hh in range(2):
                        hp = 2 * qh + hh
                        jp = hp % 2

                        @blk.tensor
                        def _(t, tw=tw, e=e, hp=hp, jp=jp, hh=hh, ld_i=ld_i, sgs_=sgs_):
                            sd_fw.wait(t, ld_i)
                            sd_fw.wait(t, ld_i - 1)
                            if first_of_tw[0]:
                                t.wait_ge(sv.h, f3[("sgrep", tw)])
                                t.wait_ge(ss.h, f3[("sgs", tw)])
                                first_of_tw[0] = False
                            if px1_free.get(jp):
                                t.wait_ge(ss.h, px1_free[jp])
                            if px3_free.get(jp):
                                t.wait_ge(sv.h, px3_free[jp])
                            for cc in range(16):
                                MM(t, px1[jp][:, :],
                                         wslot(sgs_)[:, cc * 256 + hh * 128:cc * 256 + (hh + 1) * 128],
                                         ln2f[:, cc * 512:(cc + 1) * 512], start=(cc == 0), stop=(cc == 15))
                            f3[("x1", tw, e, hp)] = st.inc(_lm[0])
                        slot_user[sgs_] = f3[("x1", tw, e, hp)]

                        @blk.scalar
                        def _(s, tw=tw, e=e, hp=hp, jp=jp):
                            s.wait_ge(st.h, f3[("x1", tw, e, hp)])
                            if s1_free.get(jp):
                                s.wait_ge(sv.h, s1_free[jp])
                            ss.inc(s.activation(s1_sb[jp][:], px1[jp][:], AF.Silu))
                            f3[("silu", tw, e, hp)] = ss.val
                        px1_free[jp] = f3[("silu", tw, e, hp)]

                        @blk.tensor
                        def _(t, tw=tw, e=e, hp=hp, jp=jp, hh=hh, sus_=sus_):
                            if px3_free.get(jp):
                                t.wait_ge(sv.h, px3_free[jp])
                            for cc in range(16):
                                MM(t, px3[jp][:, :],
                                         wslot(sus_)[:, cc * 256 + hh * 128:cc * 256 + (hh + 1) * 128],
                                         ln2f[:, cc * 512:(cc + 1) * 512], start=(cc == 0), stop=(cc == 15))
                            f3[("x3", tw, e, hp)] = st.inc(_lm[0])
                        slot_user[sus_] = f3[("x3", tw, e, hp)]

                        @blk.vector
                        def _(v, tw=tw, e=e, hp=hp, jp=jp):
                            v.wait_ge(st.h, f3[("x3", tw, e, hp)])
                            v.wait_ge(ss.h, f3[("silu", tw, e, hp)])
                            if tmp_free.get(jp):
                                v.wait_ge(sg.h, tmp_free[jp])
                            sv.inc(v.tensor_tensor(tmpb[jp][:], px3[jp][:], s1_sb[jp][:], ALU.mult))
                            f3[("tmp", tw, e, hp)] = sv.val
                        px3_free[jp] = f3[("tmp", tw, e, hp)]
                        s1_free[jp] = f3[("tmp", tw, e, hp)]

                        @blk.gpsimd
                        def _(g, tw=tw, e=e, ei=ei, hp=hp, jp=jp):
                            g.wait_ge(sv.h, f3[("tmp", tw, e, hp)])
                            if x13_free_st[0]:
                                g.wait_ge(st.h, x13_free_st[0])
                            if e < NE:
                                g.wait_ge(sv.h, f3[("wrep_v", tw, e)])
                                mulsrc = wrep
                            else:
                                g.wait_ge(sv.h, f3[("sgrep", tw)])
                                mulsrc = sgrep
                            sg.inc(g.tensor_tensor(x13w[:, (ei * 8 + hp) * 512:(ei * 8 + hp + 1) * 512],
                                                   tmpb[jp][:], mulsrc[:], ALU.mult))
                            f3[("x13", tw, e, hp)] = sg.val
                        tmp_free[jp] = f3[("x13", tw, e, hp)]
                        if hp == 7 and e < NE:
                            wrep_user[0] = f3[("x13", tw, e, 7)]

            def emit_wd_load(ei2, e2):
                half_b2 = (ei2 % 2) * 4

                @blk.sync
                def _(sync, tw=tw, e2=e2, half_b2=half_b2):
                    if tw == 0 and stage == 3:
                        if e2 < NE:
                            sd_pcs.wait(sync, pc_e_mark[e2])
                        else:
                            sync.wait_ge(sd_pcd.h, 64)
                    for sl in range(half_b2, half_b2 + 4):
                        pu = slot_user.get(sl)
                        if pu:
                            sync.wait_ge(st.h, pu)
                    for half in range(2):
                        sd_fw.issue(sync.dma_start(
                            out=warena[:, half_b2 * 4096 + half * 8192: half_b2 * 4096 + (half + 1) * 8192]
                                .rearrange("p (hc f) -> p hc f", f=C),
                            in_=down_src(e2, half).rearrange("(hc p) f -> p hc f", p=128)), key=half_b2)
                f3[("wdld", tw, e2)] = sd_fw.n - 1

            for ei in range(min(2, len(grp))):
                emit_wd_load(ei, grp[ei])

            for ei, e in enumerate(grp):
                half_base = (ei % 2) * 4
                for tt in range(4):
                    jj = tt % 2

                    @blk.tensor
                    def _(t, tw=tw, e=e, ei=ei, gi=gi, tt=tt, jj=jj, half_base=half_base):
                        sd_fw.wait(t, f3[("wdld", tw, e)])
                        t.wait_ge(sg.h, f3[("x13", tw, e, 7)])
                        if pso_free.get(jj):
                            t.wait_ge(sv.h, pso_free[jj])
                        if ei == 0 and tt == 0:
                            for jpx in (0, 1):
                                if px1_free.get(jpx):
                                    t.wait_ge(ss.h, px1_free[jpx])
                                if px3_free.get(jpx):
                                    t.wait_ge(sv.h, px3_free[jpx])
                        for hc in range(8):
                            for n4 in range(4):
                                MM(t, pso2[jj][:, n4 * 512:(n4 + 1) * 512],
                                         x13w[:, (ei * 8 + hc) * 512 + tt * 128:(ei * 8 + hc) * 512 + (tt + 1) * 128],
                                         warena[:, half_base * 4096 + hc * C + n4 * 512:
                                                half_base * 4096 + hc * C + (n4 + 1) * 512],
                                         start=(hc == 0), stop=(hc == 7))
                        f3[("dmm", tw, gi, ei, tt)] = st.inc(_lm[0])
                        if tt == 3:
                            for sl in range(half_base, half_base + 4):
                                slot_user[sl] = f3[("dmm", tw, gi, ei, tt)]

                    @blk.vector
                    def _(v, tw=tw, gi=gi, ei=ei, tt=tt, jj=jj):
                        v.wait_ge(st.h, f3[("dmm", tw, gi, ei, tt)])
                        if gi == 0 and ei == 0:
                            sv.inc(v.tensor_copy(accb[:, tt * C:(tt + 1) * C], pso2[jj][:]))
                        else:
                            sv.inc(v.tensor_tensor(accb[:, tt * C:(tt + 1) * C],
                                                   accb[:, tt * C:(tt + 1) * C], pso2[jj][:], ALU.add))
                        f3[("accadd", tw, gi, ei, tt)] = sv.val
                    pso_free[jj] = f3[("accadd", tw, gi, ei, tt)]

                if ei + 2 < len(grp):
                    emit_wd_load(ei + 2, grp[ei + 2])

            x13_free_st[0] = f3[("dmm", tw, gi, len(grp) - 1, 3)]
            px1_free[0] = None
            px1_free[1] = None
            if ("accadd", tw, gi, len(grp) - 1, 2) in f3:
                pa_guard = f3[("accadd", tw, gi, len(grp) - 1, 2)]
            else:
                pa_guard = f3[("accadd", tw, gi, len(grp) - 1, 3)]
            px3_free[0] = pa_guard
            px3_free[1] = pa_guard
            px1_guard = pa_guard

            f3[("pAfree", tw, gi)] = pa_guard


        for tt in range(4):
            @blk.sync
            def _(sync, tw=tw, tt=tt, tw0=tw0):
                if ("outdma", tw, tt - 1) in f3:
                    sync.wait_ge(sd_out.h, f3[("outdma", tw, tt - 1)])
                elif ("outdma", tw - 1, 3) in f3:
                    sync.wait_ge(sd_out.h, f3[("outdma", tw - 1, 3)])
                f3[("x3ld", tw, tt)] = sd_fx.issue(
                    sync.dma_start(out=xt3[:], in_=x2_scr[tw0 + tt * 128:tw0 + (tt + 1) * 128, :]))

            @blk.vector
            def _(v, tw=tw, tt=tt):
                last_gi = len(groups) - 1
                last_ei = len(groups[last_gi]) - 1
                v.wait_ge(sv.h, f3[("accadd", tw, last_gi, last_ei, tt)])
                sd_fx.wait(v, f3[("x3ld", tw, tt)])
                sv.inc(v.tensor_tensor(xt3[:], xt3[:], accb[:, tt * C:(tt + 1) * C], ALU.add))
                f3[("final", tw, tt)] = sv.val

            @blk.sync
            def _(sync, tw=tw, tt=tt, tw0=tw0):
                sync.wait_ge(sv.h, f3[("final", tw, tt)])
                sd_out.inc(sync.dma_start(out=out_own[tw0 + tt * 128:tw0 + (tt + 1) * 128, :], in_=xt3[:]), 16)
                f3[("outdma", tw, tt)] = sd_out.val

        f3[("accdone", tw)] = f3[("accadd", tw, len(groups) - 1, len(groups[-1]) - 1, 3)]
        f3[("lnlast", tw)] = f3[("x3", tw, 9, 7)]

    @blk.sync
    def _(sync):
        sync.wait_ge(sd_out.h, sd_out.val)
        if debug:
            sync.wait_ge(sd_dbg.h, sd_dbg.val)

    blk.flush()
    return nc



def _host_arrays(inputs, c):
    b, s = c // 4, c % 4
    x = inputs["L0_attn_hidden_states"]
    rows = ((np.arange(NT) * 4 + s)[:, None] * 128 + np.arange(128)[None, :]).ravel()
    d = {}
    d["x_own"] = np.ascontiguousarray(x[b][rows]).astype(np.float32, copy=False)

    inv = (1.0 / (10000.0 ** (np.arange(0, ROT, 2, dtype=np.float64) / ROT))).astype(np.float64)
    emb = rows[:, None].astype(np.float64) * inv[None, :]
    embf = np.concatenate([emb, emb], axis=1)
    d["cos_own"] = np.ascontiguousarray(np.cos(embf).T).astype(ml_dtypes.bfloat16)
    d["sin_own"] = np.ascontiguousarray(np.sin(embf).T).astype(ml_dtypes.bfloat16)

    tri = np.triu(np.ones((128, 128), dtype=np.float32))
    mk = np.zeros((8, 128, 1024), dtype=np.float32)
    for pp in range(8):
        for half in range(2):
            dd = 2 * pp + half
            for j in range(4):
                grel = 4 * j + s
                blkm = mk[pp][:, half * 512 + j * 128: half * 512 + (j + 1) * 128]
                if dd < grel:
                    blkm[:] = 1.0
                elif dd == grel:
                    blkm[:] = tri
    d["maskp"] = mk.reshape(8, 128 * 1024).astype(ml_dtypes.bfloat16)
    return d


def _shared_arrays(inputs):
    f32 = np.float32
    d = {}
    d["wqkv"] = np.ascontiguousarray(inputs["L0_attn_W_qkv"], dtype=f32)
    d["bqkv_col"] = np.ascontiguousarray(np.asarray(inputs["L0_attn_b_qkv"], dtype=f32).reshape(48, 128).T)
    d["wdense"] = np.ascontiguousarray(inputs["L0_attn_W_dense"], dtype=f32)
    d["bd_row"] = np.ascontiguousarray(np.asarray(inputs["L0_attn_b_dense"], dtype=f32)[None, :])
    d["gwT"] = np.ascontiguousarray(np.asarray(inputs["L0_ffn_gate_w"], dtype=f32).T)
    d["wgT"] = np.ascontiguousarray(np.asarray(inputs["L0_ffn_w_gate"], dtype=f32).transpose(0, 2, 1)).reshape(NE * C, H_FF)
    d["wuT"] = np.ascontiguousarray(np.asarray(inputs["L0_ffn_w_up"], dtype=f32).transpose(0, 2, 1)).reshape(NE * C, H_FF)
    d["wdT"] = np.ascontiguousarray(np.asarray(inputs["L0_ffn_w_down"], dtype=f32).transpose(0, 2, 1)).reshape(NE * H_FF, C)
    d["shgT"] = np.ascontiguousarray(np.asarray(inputs["L0_ffn_sh_gate"], dtype=f32).T)
    d["shuT"] = np.ascontiguousarray(np.asarray(inputs["L0_ffn_sh_up"], dtype=f32).T)
    d["shdT"] = np.ascontiguousarray(np.asarray(inputs["L0_ffn_sh_down"], dtype=f32).T)
    d["seT_col"] = np.ascontiguousarray(np.asarray(inputs["L0_ffn_se_gate_w"], dtype=f32).reshape(16, 128).T)
    d["ident128"] = np.eye(128, dtype=ml_dtypes.bfloat16)
    d["identf32"] = np.eye(128, dtype=f32)
    d["ones1"] = np.ones((1, 128), dtype=ml_dtypes.bfloat16)
    d["ones128"] = np.ones((128, 128), dtype=ml_dtypes.bfloat16)
    oh = np.zeros((8, 8 * 128), dtype=np.float32)
    for e in range(8):
        oh[e, e * 128:(e + 1) * 128] = 1.0
    d["onehot8"] = oh.astype(ml_dtypes.bfloat16)
    R = np.zeros((ROT, ROT), dtype=np.float32)
    for m in range(16):
        R[m, m + 16] = -1.0
        R[m + 16, m] = 1.0
    d["rpermT"] = np.ascontiguousarray(R.T).astype(ml_dtypes.bfloat16)
    d["cm8"] = np.full((128, 1), -8.0, dtype=f32)
    return d


_NC_CACHE = {}


def _get_nc(debug=False):
    key = bool(debug)
    if key not in _NC_CACHE:
        _NC_CACHE[key] = build_nc(debug=key)
    return _NC_CACHE[key]


def kernel(**inputs):
    nc = _get_nc(debug=False)
    shared = _shared_arrays(inputs)
    in_maps = []
    for c in range(NCORE):
        m = dict(shared)
        m.update(_host_arrays(inputs, c))
        in_maps.append(m)
    res = run_bass_kernel_spmd(nc, in_maps, core_ids=list(range(NCORE))).results

    x = inputs["L0_attn_hidden_states"]
    out = np.empty((2, S, C), dtype=np.float32)
    for c in range(NCORE):
        b, s = c // 4, c % 4
        rows = ((np.arange(NT) * 4 + s)[:, None] * 128 + np.arange(128)[None, :]).ravel()
        out[b][rows] = res[c]["out_own"]
    return out
